# version 17
# baseline (speedup 1.0000x reference)
import numpy as np
import ml_dtypes
import concourse.bass as bass
import concourse.mybir as mybir
from concourse.bass_utils import run_bass_kernel_spmd
from contextlib import ExitStack

F32 = mybir.dt.float32
BF16 = mybir.dt.bfloat16
AF = mybir.ActivationFunctionType
ALU = mybir.AluOpType
AX = mybir.AxisListType

T = 2048
D = 1024
NCH = D // 128
NTC = T // 512
BF = ml_dtypes.bfloat16

DBG = []
MOE_STAGES = {"router", "spill", "experts", "final"}
MOE_LIMIT = None
L2F_DEPTH = 6
L2F_PARTS = {"attn", "ffn"}
SUBS = ["l0f", "l0a", "l0f", "l1f", "l1a", "l1f",
        "l2f", "l2a", "l2f", "l3f", "l3a", "l3f"]

LAUNCH_GROUPS = [list(range(12))]


_WEIGHT_SPECS = {}


def _prep(inputs):
    g = inputs
    out = {}
    f32 = np.float32

    def cc(name, a):
        a = np.ascontiguousarray(a, dtype=f32)
        out[name] = a
        _WEIGHT_SPECS[name] = a.shape

    wq = np.transpose(np.asarray(g["L0_attn_Wq"], f32), (1, 0, 2)).reshape(D, D)
    wk = np.transpose(np.asarray(g["L0_attn_Wk"], f32), (1, 0, 2)).reshape(D, D)
    wv = np.transpose(np.asarray(g["L0_attn_Wv"], f32), (1, 0, 2)).reshape(D, D)
    cc("l0a_wq", wq); cc("l0a_wk", wk); cc("l0a_wv", wv)
    cc("l0a_wo", g["L0_attn_Wo"])
    cc("l0a_bq", np.asarray(g["L0_attn_bq"], f32).reshape(-1).reshape(NCH, 128).T)
    cc("l0a_bk", np.asarray(g["L0_attn_bk"], f32).reshape(-1).reshape(NCH, 128).T)
    cc("l0a_bv", np.asarray(g["L0_attn_bv"], f32).reshape(-1).reshape(NCH, 128).T)
    cc("l0a_bo", np.asarray(g["L0_attn_bo"], f32).reshape(NCH, 128).T)
    ident = np.eye(128, dtype=f32)
    cc("c_ident", ident)
    negb = np.triu(np.full((128, 128), -1e9, dtype=f32), k=1)
    cc("c_causal_add", np.triu(np.full((128, 128), -1e9, f32), 1))

    w1 = np.asarray(g["L0_ffn_W1"], f32)
    img = w1.reshape(8, 8, 512, 8, 128).transpose(0, 1, 4, 3, 2)
    cc("l0f_w1", img.reshape(8, 8, 128, 8 * 512))
    w2 = np.asarray(g["L0_ffn_W2"], f32)
    img = w2.reshape(8, 8, 128, 32, 128).transpose(0, 1, 3, 4, 2)
    img = w2.reshape(8, 8, 128, 32, 128)
    img = img.transpose(0, 1, 4, 3, 2)
    cc("l0f_w2", np.ascontiguousarray(img).reshape(8, 8, 128, 32 * 128))
    cc("l0f_wrt", np.asarray(g["L0_ffn_Wr"], f32).T.reshape(8, 128, 8).transpose(1, 0, 2).reshape(128, 64))
    cc("l0f_br", np.ascontiguousarray(np.broadcast_to(np.asarray(g["L0_ffn_br"], f32)[None, :], (128, 8))))
    b1 = np.asarray(g["L0_ffn_b1"], f32)
    cc("l0f_b1", b1.reshape(8, 32, 128).transpose(2, 0, 1).reshape(128, 256))
    cc("l0f_b2", np.asarray(g["L0_ffn_b2"], f32))
    cc("l0f_g", np.asarray(g["L0_ffn_gamma"], f32).reshape(NCH, 128).T)
    cc("l0f_bt", np.asarray(g["L0_ffn_beta"], f32).reshape(NCH, 128).T)
    esel = np.zeros((8, 8, 128), f32)
    for e in range(8):
        esel[e, e, :] = 1.0
    cc("c_esel", esel.transpose(1, 0, 2).reshape(8, 8 * 128))

    w0 = np.asarray(g["L3_ffn_w0_kernel"], f32)
    img = w0.reshape(8, 8, 128, 4, 512).transpose(0, 3, 2, 1, 4)
    cc("l3f_w0", np.ascontiguousarray(img).reshape(8, 4, 128, 8 * 512))
    w1k = np.asarray(g["L3_ffn_w1_kernel"], f32)
    img = w1k.reshape(8, 8, 128, 4, 512).transpose(0, 3, 2, 1, 4)
    cc("l3f_w1", np.ascontiguousarray(img).reshape(8, 4, 128, 8 * 512))
    wo = np.asarray(g["L3_ffn_wo_kernel"], f32)
    img = wo.reshape(8, 16, 128, 8, 128).transpose(0, 3, 2, 1, 4)
    cc("l3f_wo", np.ascontiguousarray(img).reshape(8, 8, 128, 16 * 128))
    cc("l3f_wrt", np.asarray(g["L3_ffn_gate_kernel"], f32).reshape(8, 128, 8).transpose(1, 0, 2).reshape(128, 64))

    def wimg(W):
        return np.ascontiguousarray(np.asarray(W, f32).reshape(8, 128, 8, 128).transpose(0, 3, 2, 1))
    cc("l1a_wk", wimg(g["L1_attn_Wk"]).reshape(8, 128, 1024))
    cc("l1a_wr", wimg(g["L1_attn_Wr"]).reshape(8, 128, 1024))
    cc("l1a_wg", wimg(g["L1_attn_Wg"]).reshape(8, 128, 1024))
    cc("l1a_wo", wimg(g["L1_attn_Wo"]).reshape(8, 128, 1024))
    wv_ = np.asarray(g["L1_attn_Wv"], f32)
    cc("l1a_wv", np.ascontiguousarray(wv_.reshape(2, 512, 8, 128).transpose(0, 3, 2, 1)).reshape(2, 128, 4096))
    cc("l1a_alora", np.ascontiguousarray(np.asarray(g["L1_attn_A_lora"], f32).reshape(8, 128, 160)))
    cc("l1a_blora", np.ascontiguousarray(np.asarray(g["L1_attn_B_lora"], f32).transpose(1, 0, 2)).reshape(32, 5120))
    cc("l1a_tda", np.ascontiguousarray(np.asarray(g["L1_attn_td_A"], f32).reshape(8, 128, 64)))
    cc("l1a_tdb", np.asarray(g["L1_attn_td_B"], f32))
    def pp(a):
        return np.ascontiguousarray(np.asarray(a, f32).reshape(-1).reshape(NCH, 128).T)
    cc("l1a_miux", pp(g["L1_attn_miu_x"]))
    cc("l1a_lam", np.concatenate([pp(np.asarray(g["L1_attn_lambda_"], f32)[f]) for f in range(5)], axis=1))
    cc("l1a_tdmiu", pp(g["L1_attn_td_miu"]))
    cc("l1a_u", pp(g["L1_attn_u"]))
    cc("l1a_gng", pp(g["L1_attn_gn_gamma"]))
    cc("l1a_gnb", pp(g["L1_attn_gn_beta"]))
    ms = np.zeros((128, 128), f32)
    for s_ in range(128):
        ms[s_, s_ + 1:] = 1.0
    cc("c_mstrict", ms)
    idf = np.eye(128, dtype=f32)
    cc("c_identf", idf)

    wq3 = np.asarray(g["L3_attn_Wqkv"], f32)
    img = wq3.reshape(8, 128, 3, 8, 128).transpose(2, 3, 1, 0, 4)
    cc("l3a_wqk", np.ascontiguousarray(img[0:2]).reshape(2, 8, 128, 1024))
    imgv = wq3[:, 2048:3072].reshape(8, 128, 1024).transpose(1, 0, 2)
    cc("l3a_wv", np.ascontiguousarray(imgv).reshape(128, 8192))
    wo3 = np.asarray(g["L3_attn_Wout"], f32)
    img = wo3.reshape(8, 128, 8, 128).transpose(2, 1, 0, 3)
    cc("l3a_wo", np.ascontiguousarray(img).reshape(8, 128, 1024))
    cc("l3a_bo", np.asarray(g["L3_attn_bout"], f32).reshape(NCH, 128).T)
    ii = np.arange(32, dtype=np.float64)
    theta3 = np.power(10000.0, -2.0 * (ii - 1.0) / 64.0)
    ang = np.arange(T, dtype=np.float64)[:, None] * theta3[None, :]
    cosr = np.cos(ang); sinr = np.sin(ang)
    cosF = np.zeros((128, T)); sinF = np.zeros((128, T))
    for j in range(128):
        i_ = (j % 64) // 2
        cosF[j] = cosr[:, i_]; sinF[j] = sinr[:, i_]
    out["c_l3cos"] = np.ascontiguousarray(cosF.astype(BF)); _WEIGHT_SPECS["c_l3cos"] = (128, T)
    out["c_l3sin"] = np.ascontiguousarray(sinF.astype(BF)); _WEIGHT_SPECS["c_l3sin"] = (128, T)
    P64 = np.zeros((64, 64), f32)
    for i_ in range(32):
        P64[2 * i_ + 1, 2 * i_] = -1.0
        P64[2 * i_, 2 * i_ + 1] = 1.0
    Pm = np.zeros((128, 128), f32)
    Pm[0:64, 0:64] = P64; Pm[64:128, 64:128] = P64
    out["c_l3perm"] = np.ascontiguousarray(Pm.astype(BF)); _WEIGHT_SPECS["c_l3perm"] = (128, 128)
    l3m = np.zeros((128, 2048), f32)
    for dv in range(4):
        for kp in range(128):
            for qcol in range(512):
                if kp <= qcol - dv * 128:
                    l3m[kp, dv * 512 + qcol] = 1.0
    out["c_l3mask"] = np.ascontiguousarray(l3m.astype(BF)); _WEIGHT_SPECS["c_l3mask"] = (128, 2048)

    cc("l2a_wqkv", g["L2_attn_w_qkv"])
    cc("l2a_wo", g["L2_attn_w_out"])
    bqkv = np.asarray(g["L2_attn_b_qkv"], f32)
    cc("l2a_bq", bqkv[0:1024].reshape(NCH, 128).T)
    cc("l2a_bk", bqkv[1024:2048].reshape(NCH, 128).T)
    cc("l2a_bv", bqkv[2048:3072].reshape(NCH, 128).T)
    cc("l2a_bo", np.asarray(g["L2_attn_b_out"], f32).reshape(NCH, 128).T)
    cc("l2a_g", np.asarray(g["L2_attn_ln_gamma"], f32).reshape(NCH, 128).T)
    cc("l2a_b", np.asarray(g["L2_attn_ln_beta"], f32).reshape(NCH, 128).T)

    wqkv = np.asarray(g["L2_ffn_Wqkv"], f32)
    img = wqkv.reshape(6, 8, 128, 3, 1024).transpose(0, 3, 2, 1, 4)
    cc("l2f_wqkv", np.ascontiguousarray(img).reshape(6, 3, 128, 8192))
    wo2 = np.asarray(g["L2_ffn_Wo"], f32)
    img = wo2.reshape(6, 8, 128, 8, 128).transpose(0, 3, 2, 1, 4)
    cc("l2f_wo", np.ascontiguousarray(img).reshape(6, 8, 128, 1024))
    w1p = np.zeros((6, 1024, 5632), f32)
    w1in = np.asarray(g["L2_ffn_W1"], f32)
    w1p[:, :, 0:2730] = w1in[:, :, 0:2730]
    w1p[:, :, 2816:2816 + 2730] = w1in[:, :, 2730:5460]
    img = w1p.reshape(6, 8, 128, 44, 128).transpose(0, 3, 2, 1, 4)
    cc("l2f_w1", np.ascontiguousarray(img).reshape(6, 44, 128, 1024))
    w2p = np.zeros((6, 2816, 1024), f32)
    w2p[:, 0:2730, :] = np.asarray(g["L2_ffn_W2"], f32)
    img = w2p.reshape(6, 22, 128, 8, 128).transpose(0, 3, 2, 1, 4)
    cc("l2f_w2", np.ascontiguousarray(img).reshape(6, 8, 128, 2816))
    b1p = np.zeros((6, 5632), f32)
    b1in = np.asarray(g["L2_ffn_b1"], f32)
    b1p[:, 0:2730] = b1in[:, 0:2730]
    b1p[:, 2816:2816 + 2730] = b1in[:, 2730:5460]
    cc("l2f_b1", b1p.reshape(6, 44, 128).transpose(0, 2, 1))
    cc("l2f_b2", np.asarray(g["L2_ffn_b2"], f32).reshape(6, 8, 128).transpose(0, 2, 1))
    cc("l2f_lng", np.asarray(g["L2_ffn_ln_g"], f32).reshape(6, 8, 128).transpose(0, 2, 1))
    cc("l2f_lnb", np.asarray(g["L2_ffn_ln_b"], f32).reshape(6, 8, 128).transpose(0, 2, 1))
    cc("l2f_qs", np.asarray(g["L2_ffn_q_scale"], f32))
    cc("l2f_ks", np.asarray(g["L2_ffn_k_scale"], f32))
    qs_ = np.asarray(g["L2_ffn_q_scale"], f32)
    ks_ = np.asarray(g["L2_ffn_k_scale"], f32)
    qsp_ = np.concatenate([qs_[:, 32:], qs_[:, :32]], axis=1)
    ksp_ = np.concatenate([ks_[:, 32:], ks_[:, :32]], axis=1)
    sc4 = np.stack([np.tile(qs_, (1, 2)), np.tile(qsp_, (1, 2)),
                    np.tile(ks_, (1, 2)), np.tile(ksp_, (1, 2))], axis=1)
    cc("l2f_scales", sc4.reshape(6, 4, 128, 1))
    DH, WIN = 64, 256
    half = np.arange(0, DH, 2, dtype=np.float64)
    inv_freq = 1.0 / 10000.0 ** (half / DH)
    tarr = np.arange(T, dtype=np.float64)
    freqs = np.concatenate([tarr[:, None] * inv_freq[None, :]] * 2, -1)
    base_scale = (half + 0.4 * DH) / (1.4 * DH)
    power = (tarr - T // 2) / (WIN // 2)
    scale = np.concatenate([base_scale[None, :] ** power[:, None]] * 2, -1)
    cosf, sinf = np.cos(freqs), np.sin(freqs)
    def fmimg(a):
        b = np.concatenate([a.T, a.T], axis=0)
        return np.ascontiguousarray(b.astype(BF))
    out["c_cs8fm"] = fmimg(cosf * scale * 8.0); _WEIGHT_SPECS["c_cs8fm"] = (128, T)
    out["c_ss8fm"] = fmimg(sinf * scale * 8.0); _WEIGHT_SPECS["c_ss8fm"] = (128, T)
    out["c_cifm"] = fmimg(cosf / scale); _WEIGHT_SPECS["c_cifm"] = (128, T)
    out["c_sifm"] = fmimg(sinf / scale); _WEIGHT_SPECS["c_sifm"] = (128, T)
    P64h = np.zeros((64, 64), f32)
    for j_ in range(32):
        P64h[j_ + 32, j_] = -1.0
        P64h[j_, j_ + 32] = 1.0
    Pm2 = np.zeros((128, 128), f32)
    Pm2[0:64, 0:64] = P64h; Pm2[64:128, 64:128] = P64h
    out["c_l2perm"] = np.ascontiguousarray(Pm2.astype(BF)); _WEIGHT_SPECS["c_l2perm"] = (128, 128)
    cc("c_causal_low", np.tril(np.full((128, 128), -1e9, f32), -1))

    cc("l1f_theta", g["L1_ffn_theta"])
    cc("l1f_phi", g["L1_ffn_phi"])
    cc("l1f_thn", g["L1_ffn_theta_noise"])
    cc("l1f_phn", g["L1_ffn_phi_noise"])
    return out


_SUB_WEIGHTS = {
    "l1f": ["l1f_theta", "l1f_phi", "l1f_thn", "l1f_phn"],
    "l0a": ["l0a_wq", "l0a_wk", "l0a_wv", "l0a_wo", "l0a_bq", "l0a_bk", "l0a_bv",
            "l0a_bo", "c_ident", "c_causal_add"],
    "l2a": ["l2a_wqkv", "l2a_wo", "l2a_bq", "l2a_bk", "l2a_bv", "l2a_bo",
            "l2a_g", "l2a_b", "c_ident"],
    "l0f": ["l0f_w1", "l0f_w2", "l0f_wrt", "l0f_br", "l0f_b1", "l0f_b2",
            "l0f_g", "l0f_bt", "c_esel", "c_ident"],
    "l3f": ["l3f_w0", "l3f_w1", "l3f_wo", "l3f_wrt", "c_esel", "c_ident"],
    "l3a": ["l3a_wqk", "l3a_wv", "l3a_wo", "l3a_bo", "c_l3cos", "c_l3sin",
            "c_l3perm", "c_l3mask"],
    "l2f": ["l2f_wqkv", "l2f_wo", "l2f_w1", "l2f_w2", "l2f_b1", "l2f_b2",
            "l2f_lng", "l2f_lnb", "l2f_qs", "l2f_ks", "l2f_scales",
            "c_cs8fm", "c_ss8fm", "c_cifm", "c_sifm",
            "c_l2perm", "c_causal_low", "c_causal_add", "c_ident"],
    "l1a": ["l1a_wk", "l1a_wr", "l1a_wg", "l1a_wo", "l1a_wv", "l1a_alora",
            "l1a_blora", "l1a_tda", "l1a_tdb", "l1a_miux", "l1a_lam", "l1a_tdmiu",
            "l1a_u", "l1a_gng", "l1a_gnb", "c_mstrict", "c_identf", "c_ident"],
}


SEMS = ["LDS", "LDG", "LDH", "MM", "EV", "VV", "GG", "ST"]


class Ctx:
    def __init__(self, nc, es, block):
        self.nc, self.es, self.block = nc, es, block
        self.cnt = {s: 0 for s in SEMS}
        self.sem = {}
        for s in SEMS:
            self.sem[s] = es.enter_context(nc.semaphore(s))
        self.dram = {}
        self.x_in_hbm = False

    def inc(self, instr, name, n=1):
        instr.then_inc(self.sem[name], n)
        self.cnt[name] += n
        return self.cnt[name]

    def dbg(self, tag, ap):
        if tag not in DBG:
            return
        b = self.block
        nc = self.nc
        dt = ap.dtype
        dout = nc.dram_tensor("dbg_" + tag, [ap.shape[0], ap.shape[1]], dt,
                              kind="ExternalOutput").ap()
        @b.sync
        def _(s):
            self.w(s, "MM", "EV", "VV", "GG")
            ins = s.dma_start(dout[:, :], ap)
            self.inc(ins, "ST", 16)
        tgt = self.cnt["ST"]
        for eng in ("tensor", "scalar", "vector", "gpsimd"):
            @getattr(b, eng)
            def _(e):
                e.wait_ge(self.sem["ST"], tgt)

    def w(self, e, *names):
        for n in names:
            if self.cnt[n] > 0:
                e.wait_ge(self.sem[n], self.cnt[n])

    def wall(self, e):
        self.w(e, *SEMS)

    def fence(self, e, ins, name):
        self.inc(ins, name)
        e.wait_ge(self.sem[name], self.cnt[name])



def emit_ln(C, H, xa, xn, eps_col=0, src_f32=True, gamma=None, beta=None, router=False):
    b = C.block
    ps_stat, ps_m, ps_r = H["PS"][5], H["PS"][6], H["PS"][7]
    t_s, tsq_s = H["t_s"], H["tsq_s"]
    row_m, row_r = H["row_m"], H["row_r"]
    ones_mean = H["ones_cf"] if src_f32 else H["ones_bf"]

    for tc in range(NTC):
        sl = slice(tc * 512, (tc + 1) * 512)

        @b.tensor
        def _(t):
            C.wall(t)
            for d_ in range(NCH):
                ins = t.matmul(ps_stat[0:1, 0:512], ones_mean[:, 0:1], xa[:, d_, sl],
                               start=(d_ == 0), stop=(d_ == NCH - 1))
            C.inc(ins, "MM")

        @b.scalar
        def _(s):
            C.w(s, "MM")
            ins = s.activation(row_m[0:1, 0:512], ps_stat[0:1, 0:512], AF.Copy,
                               bias=0.0, scale=1.0 / D)
            C.inc(ins, "EV")

        @b.tensor
        def _(t):
            C.w(t, "EV", "VV")
            ins = t.matmul(ps_m[:, 0:512], H["ones_f"][0:1, :], row_m[0:1, 0:512],
                           start=True, stop=True)
            C.inc(ins, "MM")

        @b.vector
        def _(v):
            C.w(v, "MM")
            for d_ in range(NCH):
                ins = v.tensor_tensor(t_s[:, d_ * 512:(d_ + 1) * 512], xa[:, d_, sl],
                                      ps_m[:, 0:512], op=ALU.subtract)
            C.inc(ins, "VV")

        @b.gpsimd
        def _(gp):
            C.w(gp, "VV")
            for d_ in range(NCH):
                ins = gp.tensor_tensor(tsq_s[:, d_ * 512:(d_ + 1) * 512],
                                       t_s[:, d_ * 512:(d_ + 1) * 512],
                                       t_s[:, d_ * 512:(d_ + 1) * 512], op=ALU.mult)
            C.inc(ins, "GG")

        @b.tensor
        def _(t):
            C.w(t, "GG", "EV")
            for d_ in range(NCH):
                ins = t.matmul(ps_stat[0:1, 0:512], H["ones_bf"][:, 0:1],
                               tsq_s[:, d_ * 512:(d_ + 1) * 512],
                               start=(d_ == 0), stop=(d_ == NCH - 1))
            C.inc(ins, "MM")

        @b.scalar
        def _(s):
            C.w(s, "MM")
            ins = s.activation(row_r[0:1, 0:512], ps_stat[0:1, 0:512], AF.Ln,
                               bias=H["cst"][0:1, eps_col:eps_col + 1], scale=1.0 / D)
            C.fence(s, ins, "EV")
            ins = s.activation(row_r[0:1, 0:512], row_r[0:1, 0:512], AF.Exp,
                               bias=0.0, scale=-0.5)
            C.inc(ins, "EV")

        @b.tensor
        def _(t):
            C.w(t, "EV", "VV")
            ins = t.matmul(ps_r[:, 0:512], H["ones_f"][0:1, :], row_r[0:1, 0:512],
                           start=True, stop=True)
            C.inc(ins, "MM")

        @b.vector
        def _(v):
            C.w(v, "MM")
            for d_ in range(NCH):
                if gamma is None:
                    ins = v.tensor_tensor(xn[:, d_, sl], t_s[:, d_ * 512:(d_ + 1) * 512],
                                          ps_r[:, 0:512], op=ALU.mult)
                else:
                    v.tensor_tensor(t_s[:, d_ * 512:(d_ + 1) * 512],
                                    t_s[:, d_ * 512:(d_ + 1) * 512],
                                    ps_r[:, 0:512], op=ALU.mult)
                    ins = v.tensor_scalar(xn[:, d_, sl], t_s[:, d_ * 512:(d_ + 1) * 512],
                                          gamma[:, d_:d_ + 1], beta[:, d_:d_ + 1],
                                          op0=ALU.mult, op1=ALU.add)
            C.inc(ins, "VV")

        if router:
            for d_ in range(NCH):
                @b.vector
                def _(v):
                    C.w(v, "MM")
                    ins = v.tensor_tensor(H["xn32"][:, 0:512], t_s[:, d_ * 512:(d_ + 1) * 512],
                                          ps_r[:, 0:512], op=ALU.mult)
                    C.inc(ins, "VV")
                @b.tensor
                def _(t):
                    C.w(t, "VV", "EV")
                    for tt in range(4):
                        ins = t.matmul(H["PS"][tt][:, 0:8],
                                       H["xn32"][:, tt * 128:(tt + 1) * 128],
                                       H["wrtf"][:, d_ * 8:(d_ + 1) * 8],
                                       start=(d_ == 0), stop=(d_ == NCH - 1))
                    C.inc(ins, "MM")
            @b.scalar
            def _(s):
                C.w(s, "MM")
                for tt in range(4):
                    ins = s.copy(H["ltm"][:, tc * 32 + tt * 8: tc * 32 + tt * 8 + 8],
                                 H["PS"][tt][:, 0:8])
                C.inc(ins, "EV")


def emit_l1f(C, H, li):
    b = C.block
    xa, xn = H["xa"], H["xn"]
    th, phi_, thn, phn = (C.dram["l1f_theta"], C.dram["l1f_phi"],
                          C.dram["l1f_thn"], C.dram["l1f_phn"])
    stage = H["wstage_f"]
    ct = H["AR2"][:, 0:8192].rearrange("p (d o) -> p d o", d=NCH)
    st = H["AR2"][:, 8192:16384].rearrange("p (d o) -> p d o", d=NCH)
    PS = H["PS"]
    ps_a, ps_b = PS[0], PS[1]
    f1, f2 = H["f512_a"], H["f512_b"]

    for d_ in range(NCH):
        for (mat, noise, dst, bias) in ((th, thn, ct, H["cst"][:, 1:2]),
                                        (phi_, phn, st, 0.0)):
            @b.sync
            def _(s):
                s.wait_ge(C.sem["LDS"], C.cnt["LDS"])
                C.w(s, "EV", "MM", "VV")
                ins = s.dma_start(stage[:, 0:1024], mat[d_ * 128:(d_ + 1) * 128, :])
                C.inc(ins, "LDS", 16)
                ins = s.dma_start(stage[:, 1024:2048], noise[d_ * 128:(d_ + 1) * 128, :])
                C.inc(ins, "LDS", 16)

            @b.vector
            def _(v):
                C.w(v, "LDS")
                ins = v.tensor_tensor(stage[:, 0:1024], stage[:, 0:1024],
                                      stage[:, 1024:2048], op=ALU.add)
                C.inc(ins, "VV")

            @b.scalar
            def _(s):
                C.w(s, "VV")
                ins = s.activation(dst[:, d_, :], stage[:, 0:1024], AF.Sin,
                                   bias=bias, scale=1.0)
                C.inc(ins, "EV")

    for oc in range(NCH):
        for tc in range(NTC):
            sl = slice(tc * 512, (tc + 1) * 512)

            @b.tensor
            def _(t):
                C.w(t, "EV", "VV")
                for d_ in range(NCH):
                    t.matmul(ps_a[:, 0:512], ct[:, d_, oc * 128:(oc + 1) * 128],
                             xn[:, d_, sl], start=(d_ == 0), stop=(d_ == NCH - 1))
                for d_ in range(NCH):
                    ins = t.matmul(ps_b[:, 0:512], st[:, d_, oc * 128:(oc + 1) * 128],
                                   xn[:, d_, sl], start=(d_ == 0), stop=(d_ == NCH - 1))
                C.inc(ins, "MM")

            @b.scalar
            def _(s):
                C.w(s, "MM")
                s.activation(f1[:, 0:512], ps_a[:, 0:512], AF.Square)
                ins = s.activation(f2[:, 0:512], ps_b[:, 0:512], AF.Square)
                C.inc(ins, "EV")

            @b.vector
            def _(v):
                C.w(v, "EV")
                ins = v.tensor_tensor(f1[:, 0:512], f1[:, 0:512], f2[:, 0:512],
                                      op=ALU.add)
                C.inc(ins, "VV")

            @b.scalar
            def _(s):
                C.w(s, "VV")
                ins = s.activation(f2[:, 0:512], f1[:, 0:512], AF.Sqrt,
                                   bias=0.0, scale=0.25)
                C.inc(ins, "EV")

            @b.vector
            def _(v):
                C.w(v, "EV")
                ins = v.tensor_tensor(xa[:, oc, sl], xa[:, oc, sl], f2[:, 0:512],
                                      op=ALU.add)
                C.inc(ins, "VV")


def _load_w_full(C, dst, dram_ap, col0=0, rows=D):
    b = C.block
    nch = rows // 128
    @b.gpsimd
    def _(g):
        g.wait_ge(C.sem["LDG"], C.cnt["LDG"])
        C.w(g, "MM", "EV", "VV")
        for c_ in range(nch):
            ins = g.dma_start(dst[:, c_ * 1024:(c_ + 1) * 1024],
                              dram_ap[c_ * 128:(c_ + 1) * 128, col0:col0 + 1024])
            C.inc(ins, "LDG", 16)


def emit_mha(C, H, wq, wk, wv, wo, causal, src_xn=None):
    b = C.block
    xa, xn = H["xa"], (src_xn if src_xn is not None else H["xn"])
    PS = H["PS"]
    scr = H["scr16"]
    scrv = scr.rearrange("p (c k) -> p c k", c=NCH)
    QT = H["AR2"][:, 0:16384].rearrange("p (c t) -> p c t", c=NCH)
    KT = H["AR2"][:, 16384:32768].rearrange("p (c t) -> p c t", c=NCH)
    xa_bf = H["xa_t"][:].bitcast(BF16)
    V = xa_bf[:, 0:16384].rearrange("p (tt c) -> p tt c", tt=16)
    yT = xa_bf[:, 16384:32768].rearrange("p (c t) -> p c t", c=NCH)
    xsp = C.dram["xspill"]
    t_s = H["t_s"]
    NH = 16

    @b.sync
    def _(s):
        C.wall(s)
        for d_ in range(NCH):
            ins = s.dma_start(xsp[d_ * 128:(d_ + 1) * 128, :], xa[:, d_, :])
            C.inc(ins, "ST", 16)

    @b.gpsimd
    def _(g):
        g.wait_ge(C.sem["LDG"], C.cnt["LDG"])
        C.w(g, "MM", "EV", "VV")
        ins = g.dma_start(H["ident_bf"][:], C.dram["c_ident"][:, :])
        C.inc(ins, "LDG", 16)
        if causal:
            ins = g.dma_start(H["mask_f"][:], C.dram["c_causal_add"][:, :])
            C.inc(ins, "LDG", 16)
    @b.vector
    def _(v):
        C.w(v, "LDS")
        ins = v.tensor_scalar(H["m1_s"][:, 0:8], H["b8"][:, 0:8], 0.125, None,
                              op0=ALU.mult)
        C.inc(ins, "VV")

    for (wspec, kind) in ((wq, "q"), (wk, "k"), (wv, "v")):
        _load_w_full(C, scr, C.dram[wspec[0]], col0=wspec[1])
        if kind in ("q", "k"):
            dst = QT if kind == "q" else KT
            for oc in range(NCH):
                for tc in range(NTC):
                    sl = slice(tc * 512, (tc + 1) * 512)
                    @b.tensor
                    def _(t):
                        C.w(t, "LDG", "EV", "VV")
                        for c_ in range(NCH):
                            ins = t.matmul(PS[tc % 2][:, 0:512],
                                           scr[:, c_ * 1024 + oc * 128: c_ * 1024 + oc * 128 + 128],
                                           xn[:, c_, sl],
                                           start=(c_ == 0), stop=(c_ == NCH - 1))
                        C.inc(ins, "MM")
                    @b.scalar
                    def _(s):
                        C.w(s, "MM")
                        if kind == "q":
                            ins = s.activation(dst[:, oc, sl], PS[tc % 2][:, 0:512],
                                               AF.Identity, bias=H["m1_s"][:, oc:oc + 1],
                                               scale=0.125)
                        else:
                            ins = s.activation(dst[:, oc, sl], PS[tc % 2][:, 0:512],
                                               AF.Identity, bias=H["b8"][:, 8 + oc:9 + oc],
                                               scale=1.0)
                        C.inc(ins, "EV")
        else:
            for tt in range(16):
                for cb in range(2):
                    @b.tensor
                    def _(t):
                        C.w(t, "LDG", "VV", "EV", "ST")
                        for c_ in range(NCH):
                            ins = t.matmul(PS[cb % 2][:, 0:512],
                                           xn[:, c_, tt * 128:(tt + 1) * 128],
                                           scrv[:, c_, cb * 512:(cb + 1) * 512],
                                           start=(c_ == 0), stop=(c_ == NCH - 1))
                        C.inc(ins, "MM")
                    @b.scalar
                    def _(s):
                        C.w(s, "MM")
                        ins = s.copy(V[:, tt, cb * 512:(cb + 1) * 512], PS[cb % 2][:, 0:512])
                        C.inc(ins, "EV")

    C.dbg("qt", QT.tensor.ap() if False else H["AR2"][:, 0:16384])
    C.dbg("kt", H["AR2"][:, 16384:32768])
    C.dbg("v", xa_bf[:, 0:16384])
    attn = scr[:, 0:2048]
    slab = scr[:, 2048:4096]
    ps4b = PS[4].bitcast(BF16)
    prev_exp = [0]
    for h in range(NH):
        po = (h % 2) * 64
        ch = h // 2
        st = {}
        ev0h = [0]

        def exp_count(qt):
            c = ev0h[0]
            for q in range(qt):
                nkq = (q + 1) if causal else 16
                c += 2 + (nkq + 7) // 8
            return c + 1

        def stage_scores(qt):
            nk = (qt + 1) if causal else 16
            nkb = (nk * 128 + 511) // 512
            ev_guard = C.cnt["EV"]
            @b.tensor
            def _(t):
                if qt >= 1:
                    t.wait_ge(C.sem["EV"], exp_count(qt - 1))
                elif h == 0:
                    t.wait_ge(C.sem["EV"], ev_guard)
                    C.w(t, "VV")
                else:
                    t.wait_ge(C.sem["EV"], prev_exp[0])
                for kb in range(nkb):
                    N = min(512, nk * 128 - kb * 512)
                    ins = t.matmul(PS[kb][:, 0:N],
                                   QT[po:po + 64, ch, qt * 128:(qt + 1) * 128],
                                   KT[po:po + 64, ch, kb * 512:kb * 512 + N],
                                   start=True, stop=True)
                C.inc(ins, "MM")
            st[("sc", qt)] = C.cnt["MM"]

        def stage_rest(qt):
            nk = (qt + 1) if causal else 16
            nkb = (nk * 128 + 511) // 512
            kbd, off = (qt * 128) // 512, (qt * 128) % 512
            @b.vector
            def _(v):
                v.wait_ge(C.sem["MM"], st[("sc", qt)])
                if causal:
                    ins = v.tensor_tensor(PS[kbd][:, off:off + 128],
                                          PS[kbd][:, off:off + 128],
                                          H["mask_f"][:], op=ALU.add)
                    C.fence(v, ins, "VV")
                for kb in range(nkb):
                    N = min(512, nk * 128 - kb * 512)
                    ins = v.reduce_max(H["m1_s"][:, kb:kb + 1], PS[kb][:, 0:N],
                                       axis=AX.X, negate=True)
                C.fence(v, ins, "VV")
                ins = v.tensor_reduce(H["m1_s"][:, 4:5], H["m1_s"][:, 0:nkb],
                                      axis=AX.X, op=ALU.min)
                C.inc(ins, "VV")
            st[("mx", qt)] = C.cnt["VV"]
            @b.scalar
            def _(s):
                s.wait_ge(C.sem["VV"], st[("mx", qt)])
                for kb in range(nkb):
                    N = min(512, nk * 128 - kb * 512)
                    ins = s.activation(attn[:, kb * 512:kb * 512 + N], PS[kb][:, 0:N],
                                       AF.Exp, bias=H["m1_s"][:, 4:5],
                                       accum_out=H["f512_b"][:, kb:kb + 1])
                C.inc(ins, "EV")
            st[("exp", qt)] = C.cnt["EV"]
            @b.vector
            def _(v):
                v.wait_ge(C.sem["EV"], st[("exp", qt)])
                ins = v.tensor_reduce(H["f512_b"][:, 4:5], H["f512_b"][:, 0:nkb],
                                      axis=AX.X, op=ALU.add)
                C.fence(v, ins, "VV")
                ins = v.reciprocal(H["f512_b"][:, 5:6], H["f512_b"][:, 4:5])
                C.fence(v, ins, "VV")
                ins = v.tensor_scalar(attn[:, 0:nk * 128], attn[:, 0:nk * 128],
                                      H["f512_b"][:, 5:6], None, op0=ALU.mult)
                C.inc(ins, "VV")
            st[("nm", qt)] = C.cnt["VV"]
            done = 0
            while done < nk:
                grp = min(8, nk - done)
                @b.tensor
                def _(t):
                    t.wait_ge(C.sem["VV"], st[("nm", qt)])
                    C.w(t, "EV")
                    for j in range(grp):
                        kt = done + j
                        ins = t.transpose(ps4b[:, j * 128:(j + 1) * 128],
                                          attn[:, kt * 128:(kt + 1) * 128],
                                          H["ident_bf"][:])
                    C.inc(ins, "MM")
                mmtp = C.cnt["MM"]
                @b.scalar
                def _(s):
                    s.wait_ge(C.sem["MM"], mmtp)
                    ins = s.copy(slab[:, done * 128:(done + grp) * 128],
                                 ps4b[:, 0:grp * 128])
                    C.inc(ins, "EV")
                done += grp
            st[("slab", qt)] = C.cnt["EV"]
            @b.tensor
            def _(t):
                t.wait_ge(C.sem["EV"], st[("slab", qt)])
                for kt in range(nk):
                    ins = t.matmul(PS[5][0:64, 0:128], V[:, kt, h * 64:(h + 1) * 64],
                                   slab[:, kt * 128:(kt + 1) * 128],
                                   start=(kt == 0), stop=(kt == nk - 1))
                C.inc(ins, "MM")
            st[("yt", qt)] = C.cnt["MM"]
            @b.scalar
            def _(s):
                s.wait_ge(C.sem["MM"], st[("yt", qt)])
                ins = s.activation(yT[po:po + 64, ch, qt * 128:(qt + 1) * 128],
                                   PS[5][0:64, 0:128], AF.Identity,
                                   bias=H["b8"][po:po + 64, 24 + ch:25 + ch],
                                   scale=1.0)
                C.inc(ins, "EV")
            st[("yev", qt)] = C.cnt["EV"]

        ev0h[0] = C.cnt["EV"]
        for qt in range(17):
            if qt < 16:
                stage_scores(qt)
            if qt >= 1:
                stage_rest(qt - 1)
        assert st[("exp", 15)] == exp_count(15), (st[("exp", 15)], exp_count(15))
        prev_exp[0] = st[("exp", 15)]

    _load_w_full(C, scr, C.dram[wo[0]], col0=wo[1])
    for oc in range(NCH):
        @b.sync
        def _(s):
            s.wait_ge(C.sem["LDS"], C.cnt["LDS"])
            C.w(s, "VV", "ST")
            ins = s.dma_start(t_s[:, 0:2048], xsp[oc * 128:(oc + 1) * 128, :])
            C.inc(ins, "LDS", 16)
        for tc in range(NTC):
            sl = slice(tc * 512, (tc + 1) * 512)
            @b.tensor
            def _(t):
                C.w(t, "LDG", "VV", "EV")
                for c_ in range(NCH):
                    ins = t.matmul(PS[tc % 2][:, 0:512],
                                   scr[:, c_ * 1024 + oc * 128: c_ * 1024 + oc * 128 + 128],
                                   yT[:, c_, sl], start=(c_ == 0), stop=(c_ == NCH - 1))
                C.inc(ins, "MM")
            @b.vector
            def _(v):
                C.w(v, "MM", "LDS", "ST")
                ins = v.scalar_tensor_tensor(H["f512_b"][:, 0:512], PS[tc % 2][:, 0:512],
                                             H["b8"][:, 16 + oc:17 + oc],
                                             t_s[:, sl], op0=ALU.add, op1=ALU.add)
                C.inc(ins, "VV")
            @b.sync
            def _(s):
                C.w(s, "VV")
                ins = s.dma_start(xsp[oc * 128:(oc + 1) * 128, sl], H["f512_b"][:, 0:512])
                C.inc(ins, "ST", 16)
    C.x_in_hbm = True



def _load_b8(C, H, names):
    b = C.block
    @b.sync
    def _(s):
        s.wait_ge(C.sem["LDS"], C.cnt["LDS"])
        C.w(s, "VV", "EV")
        for (nm, c0) in names:
            ins = s.dma_start(H["b8"][:, c0:c0 + 8], C.dram[nm][:, :])
            C.inc(ins, "LDS", 16)


def emit_l0a(C, H, li):
    _load_b8(C, H, [("l0a_bq", 0), ("l0a_bk", 8), ("l0a_bo", 16), ("l0a_bv", 24)])
    emit_mha(C, H, ("l0a_wq", 0), ("l0a_wk", 0), ("l0a_wv", 0), ("l0a_wo", 0),
             causal=True)


def emit_l2a(C, H, li):
    b = C.block
    xa, xn = H["xa"], H["xn"]
    _load_b8(C, H, [("l2a_bq", 0), ("l2a_bk", 8), ("l2a_bo", 16), ("l2a_bv", 24),
                    ("l2a_g", 32), ("l2a_b", 40)])
    @b.vector
    def _(v):
        C.w(v, "LDS", "EV", "MM", "GG")
        for d_ in range(NCH):
            ins = v.tensor_tensor(xa[:, d_, :], xa[:, d_, :], xn[:, d_, :], op=ALU.add)
        C.inc(ins, "VV")
    emit_ln(C, H, xn, xn, src_f32=False,
            gamma=H["b8"][:, 32:40], beta=H["b8"][:, 40:48])
    emit_mha(C, H, ("l2a_wqkv", 0), ("l2a_wqkv", 1024), ("l2a_wqkv", 2048),
             ("l2a_wo", 0), causal=False)




def emit_moe(C, H, kind, li):
    b = C.block
    xa, xn = H["xa"], H["xn"]
    PS = H["PS"]
    scr = H["scr16"]
    xsp = C.dram["xspill"]
    t_s = H["t_s"]
    is0 = kind == "l0f"
    fa, fb = H["f512_a"], H["f512_b"]
    E = 8
    NFC = 8 if is0 else 4
    NFCC = 32 if is0 else 16
    halves = 2 if is0 else 1
    TK = T // halves
    AR2h = H["AR2"].rearrange("p (f t) -> p f t", f=NFCC)
    wrt_d = C.dram["l0f_wrt" if is0 else "l3f_wrt"]
    w1_d = C.dram["l0f_w1" if is0 else "l3f_w0"]
    w1b_d = None if is0 else C.dram["l3f_w1"]
    w2_d = C.dram["l0f_w2" if is0 else "l3f_wo"]
    ps4b = PS[4].bitcast(BF16)

    @b.gpsimd
    def _(g):
        g.wait_ge(C.sem["LDG"], C.cnt["LDG"])
        C.w(g, "MM", "EV", "VV")
        ins = g.dma_start(H["esel"][:], C.dram["c_esel"][:, :])
        C.inc(ins, "LDG", 16)
        ins = g.dma_start(H["ident_bf"][:], C.dram["c_ident"][:, :])
        C.inc(ins, "LDG", 16)
        if is0:
            ins = g.dma_start(H["b2sb"][:], C.dram["l0f_b2"][:, :])
            C.inc(ins, "LDG", 16)
    @b.sync
    def _(s):
        s.wait_ge(C.sem["LDS"], C.cnt["LDS"])
        C.w(s, "VV", "EV")
        if is0:
            ins = s.dma_start(H["b8"][:, 0:8], C.dram["l0f_br"][:, :])
            C.inc(ins, "LDS", 16)
            ins = s.dma_start(H["b256"][:], C.dram["l0f_b1"][:, :])
            C.inc(ins, "LDS", 16)
            ins = s.dma_start(H["b8"][:, 32:40], C.dram["l0f_g"][:, :])
            C.inc(ins, "LDS", 16)
            ins = s.dma_start(H["b8"][:, 40:48], C.dram["l0f_bt"][:, :])
            C.inc(ins, "LDS", 16)
    if is0:
        @b.vector
        def _(v):
            C.w(v, "LDS")
            v.tensor_scalar(H["b8"][:, 16:24], H["b8"][:, 32:40], 0.5, None, op0=ALU.mult)
            ins = v.tensor_scalar(H["b8"][:, 24:32], H["b8"][:, 40:48], 0.5, None, op0=ALU.mult)
            C.inc(ins, "VV")

    for tt in range(16 if "router" in MOE_STAGES else 0):
        @b.vector
        def _(v):
            C.w(v, "LDS", "LDG", "EV")
            l8 = H["f512_b"][:, 0:8]
            if is0:
                ins = v.tensor_tensor(l8, H["ltm"][:, tt * 8:(tt + 1) * 8],
                                      H["b8"][:, 0:8], op=ALU.add)
            else:
                ins = v.tensor_copy(l8, H["ltm"][:, tt * 8:(tt + 1) * 8])
            C.fence(v, ins, "VV")
            m1 = H["f512_b"][:, 8:9]
            ins = v.tensor_reduce(m1, l8, axis=AX.X, op=ALU.max)
            C.fence(v, ins, "VV")
            eq1 = H["f512_b"][:, 16:24]
            ins = v.tensor_scalar(eq1, l8, m1, None, op0=ALU.is_equal)
            C.fence(v, ins, "VV")
            msk = H["f512_b"][:, 24:32]
            ins = v.scalar_tensor_tensor(msk, eq1, -1e30, l8, op0=ALU.mult, op1=ALU.add)
            C.fence(v, ins, "VV")
            m2 = H["f512_b"][:, 9:10]
            ins = v.tensor_reduce(m2, msk, axis=AX.X, op=ALU.max)
            C.fence(v, ins, "VV")
            d12 = H["f512_b"][:, 10:11]
            ins = v.tensor_scalar(d12, m1, m2, None, op0=ALU.subtract)
            C.inc(ins, "VV")
        @b.scalar
        def _(s):
            C.w(s, "VV")
            s.activation(H["f512_b"][:, 11:12], H["f512_b"][:, 10:11], AF.Sigmoid)
            ins = s.activation(H["f512_b"][:, 12:13], H["f512_b"][:, 10:11], AF.Sigmoid,
                               bias=0.0, scale=-1.0)
            C.inc(ins, "EV")
        @b.vector
        def _(v):
            C.w(v, "EV")
            eq1 = H["f512_b"][:, 16:24]
            msk = H["f512_b"][:, 24:32]
            eq2 = H["f512_b"][:, 32:40]
            ins = v.tensor_scalar(eq2, msk, H["f512_b"][:, 9:10], None, op0=ALU.is_equal)
            C.fence(v, ins, "VV")
            t8 = H["f512_b"][:, 40:48]
            ins = v.tensor_scalar(t8, eq1, H["f512_b"][:, 11:12], None, op0=ALU.mult)
            C.fence(v, ins, "VV")
            ins = v.scalar_tensor_tensor(H["w8b"][:], eq2, H["f512_b"][:, 12:13], t8,
                                         op0=ALU.mult, op1=ALU.add)
            C.inc(ins, "VV")
        @b.tensor
        def _(t):
            C.w(t, "VV", "EV")
            ins = t.transpose(ps4b[0:8, 0:128], H["w8b"][:], H["ident_bf"][:])
            C.inc(ins, "MM")
        @b.scalar
        def _(s):
            C.w(s, "MM")
            ins = s.copy(H["w8T"][0:8, tt * 128:(tt + 1) * 128], ps4b[0:8, 0:128])
            C.inc(ins, "EV")

    @b.sync
    def _(s):
        C.wall(s)
        for d_ in range(NCH):
            ins = s.dma_start(xsp[d_ * 128:(d_ + 1) * 128, :], xa[:, d_, :])
            C.inc(ins, "ST", 16)
    C.dbg("w8t", H["w8T"][0:8, :])
    if is0 and "spill" in MOE_STAGES:
        for dc in range(NCH):
            for tc in range(NTC):
                sl = slice(tc * 512, (tc + 1) * 512)
                @b.tensor
                def _(t):
                    C.w(t, "EV", "VV", "ST")
                    ins = t.matmul(PS[2][:, 0:512], H["b2sb"][0:8, dc * 128:(dc + 1) * 128],
                                   H["w8T"][0:8, sl], start=True, stop=True)
                    C.inc(ins, "MM")
                @b.vector
                def _(v):
                    C.w(v, "MM", "ST")
                    ins = v.scalar_tensor_tensor(xa[:, dc, sl], PS[2][:, 0:512], 1.0,
                                                 xn[:, dc, sl], op0=ALU.mult, op1=ALU.add)
                    C.inc(ins, "VV")

    if "experts" not in MOE_STAGES:
        halves = 0
    for half in range(halves):
        tok0 = half * TK
        for e in range(E if MOE_LIMIT is None else MOE_LIMIT[0]):
            for nb in range(TK // 512):
                sl = slice(tok0 + nb * 512, tok0 + (nb + 1) * 512)
                @b.tensor
                def _(t):
                    C.w(t, "EV", "VV")
                    ins = t.matmul(PS[3][:, 0:512], H["esel"][0:8, e * 128:(e + 1) * 128],
                                   H["w8T"][0:8, sl], start=True, stop=True)
                    C.inc(ins, "MM")
                @b.scalar
                def _(s):
                    C.w(s, "MM")
                    ins = s.copy(H["wb"][:, sl], PS[3][:, 0:512])
                    C.inc(ins, "EV")
            wbcur = H["wb"]
            NG = 4 * (TK // 512)
            for fc in range(NFC if MOE_LIMIT is None else min(NFC, MOE_LIMIT[1])):
                slot = fc % 2 if is0 else 0
                lsem = "LDG" if slot == 0 else "LDH"
                @b.gpsimd
                def _(g):
                    if is0:
                        need = C.cnt["MM"] - NG if fc >= 1 else C.cnt["MM"]
                        g.wait_ge(C.sem["MM"], max(0, need))
                    else:
                        C.w(g, "MM")
                    g.wait_ge(C.sem[lsem], C.cnt[lsem])
                    ins = g.dma_start(scr[:, slot * 4096:(slot + 1) * 4096],
                                      w1_d[e, fc, :, :])
                    C.inc(ins, lsem, 16)
                    if not is0:
                        ins = g.dma_start(scr[:, 4096:8192], w1b_d[e, fc, :, :])
                        C.inc(ins, lsem, 16)
                ldg_t = C.cnt[lsem]
                mm0 = C.cnt["MM"]
                ev0 = C.cnt["EV"]
                vv0 = C.cnt["VV"]
                @b.tensor
                def _(t):
                    t.wait_ge(C.sem[lsem], ldg_t)
                    gi = 0
                    for fsub in range(4):
                        for tc2 in range(TK // 512):
                            sl = slice(tok0 + tc2 * 512, tok0 + (tc2 + 1) * 512)
                            if is0:
                                t.wait_ge(C.sem["EV"], max(ev0, ev0 + gi - 1))
                            else:
                                t.wait_ge(C.sem["VV"], max(vv0, vv0 + 2 * (gi - 1)))
                            pa = PS[gi % 2] if is0 else PS[(gi % 2) * 2]
                            for ccx in range(NCH):
                                ins = t.matmul(pa[:, 0:512],
                                               scr[:, slot * 4096 + ccx * 512 + fsub * 128: slot * 4096 + ccx * 512 + fsub * 128 + 128],
                                               xn[:, ccx, sl], start=(ccx == 0), stop=(ccx == NCH - 1))
                            if not is0:
                                pb = PS[(gi % 2) * 2 + 1]
                                for ccx in range(NCH):
                                    ins = t.matmul(pb[:, 0:512],
                                                   scr[:, 4096 + ccx * 512 + fsub * 128: 4096 + ccx * 512 + fsub * 128 + 128],
                                                   xn[:, ccx, sl], start=(ccx == 0), stop=(ccx == NCH - 1))
                            C.inc(ins, "MM")
                            gi += 1
                if is0:
                    @b.scalar
                    def _(s):
                        gi = 0
                        for fsub in range(4):
                            fcidx = fc * 4 + fsub
                            for tc2 in range(TK // 512):
                                hsl = slice(tc2 * 512, (tc2 + 1) * 512)
                                s.wait_ge(C.sem["MM"], mm0 + gi + 1)
                                ins = s.activation(AR2h[:, fcidx, hsl], PS[gi % 2][:, 0:512],
                                                   AF.Relu,
                                                   bias=H["b256"][:, e * 32 + fcidx:e * 32 + fcidx + 1])
                                C.inc(ins, "EV")
                                gi += 1
                    @b.vector
                    def _(v):
                        gi = 0
                        for fsub in range(4):
                            fcidx = fc * 4 + fsub
                            for tc2 in range(TK // 512):
                                sl = slice(tok0 + tc2 * 512, tok0 + (tc2 + 1) * 512)
                                hsl = slice(tc2 * 512, (tc2 + 1) * 512)
                                v.wait_ge(C.sem["EV"], ev0 + gi + 1)
                                ins = v.tensor_tensor(AR2h[:, fcidx, hsl], AR2h[:, fcidx, hsl],
                                                      wbcur[:, sl], op=ALU.mult)
                                C.inc(ins, "VV")
                                gi += 1
                else:
                    @b.scalar
                    def _(s):
                        gi = 0
                        for fsub in range(4):
                            for tc2 in range(TK // 512):
                                s.wait_ge(C.sem["MM"], mm0 + gi + 1)
                                if gi >= 2:
                                    s.wait_ge(C.sem["VV"], vv0 + 2 * (gi - 1))
                                ins = s.activation(fa[:, 0:512] if gi % 2 == 0 else fb[:, 0:512],
                                                   PS[(gi % 2) * 2][:, 0:512], AF.Silu)
                                C.inc(ins, "EV")
                                gi += 1
                    @b.vector
                    def _(v):
                        gi = 0
                        for fsub in range(4):
                            fcidx = fc * 4 + fsub
                            for tc2 in range(TK // 512):
                                sl = slice(tok0 + tc2 * 512, tok0 + (tc2 + 1) * 512)
                                hsl = slice(tc2 * 512, (tc2 + 1) * 512)
                                v.wait_ge(C.sem["EV"], ev0 + gi + 1)
                                ff = fa if gi % 2 == 0 else fb
                                ins = v.tensor_tensor(ff[:, 0:512], ff[:, 0:512],
                                                      PS[(gi % 2) * 2 + 1][:, 0:512], op=ALU.mult)
                                C.fence(v, ins, "VV")
                                ins = v.tensor_tensor(AR2h[:, fcidx, hsl], ff[:, 0:512],
                                                      wbcur[:, sl], op=ALU.mult)
                                C.inc(ins, "VV")
                                gi += 1
            NG2 = TK // 512
            for dc in range(NCH if MOE_LIMIT is None else min(NCH, MOE_LIMIT[2])):
                slot = dc % 2
                @b.gpsimd
                def _(g):
                    g.wait_ge(C.sem["LDG"], C.cnt["LDG"])
                    need = C.cnt["MM"] - NG2 if dc >= 1 else C.cnt["MM"]
                    g.wait_ge(C.sem["MM"], max(0, need))
                    C.w(g, "EV", "VV") if dc == 0 else None
                    g.wait_ge(C.sem[lsem], C.cnt[lsem])
                    ins = g.dma_start(scr[:, slot * 4096:slot * 4096 + NFCC * 128],
                                      w2_d[e, dc, :, :])
                    C.inc(ins, "LDG", 16)
                ldg_t = C.cnt["LDG"]
                mm0 = C.cnt["MM"]
                vv0 = C.cnt["VV"]
                @b.tensor
                def _(t):
                    t.wait_ge(C.sem["LDG"], ldg_t)
                    for tc2 in range(TK // 512):
                        t.wait_ge(C.sem["VV"], max(vv0, vv0 + tc2 - 1))
                        for fcc in range(NFCC):
                            ins = t.matmul(PS[4 + tc2 % 2][:, 0:512],
                                           scr[:, slot * 4096 + fcc * 128:slot * 4096 + (fcc + 1) * 128],
                                           AR2h[:, fcc, slice(tc2 * 512, (tc2 + 1) * 512)],
                                           start=(fcc == 0), stop=(fcc == NFCC - 1))
                        C.inc(ins, "MM")
                @b.vector
                def _(v):
                    for tc2 in range(TK // 512):
                        sl = slice(tok0 + tc2 * 512, tok0 + (tc2 + 1) * 512)
                        v.wait_ge(C.sem["MM"], mm0 + tc2 + 1)
                        if (not is0) and e == 0:
                            ins = v.tensor_copy(xa[:, dc, sl], PS[4 + tc2 % 2][:, 0:512])
                        else:
                            ins = v.tensor_tensor(xa[:, dc, sl], xa[:, dc, sl],
                                                  PS[4 + tc2 % 2][:, 0:512], op=ALU.add)
                        C.inc(ins, "VV")

    if "final" not in MOE_STAGES:
        C.x_in_hbm = True
        return
    if is0:
        emit_ln(C, H, xa, xn, gamma=H["b8"][:, 16:24], beta=H["b8"][:, 24:32])
    for d_ in range(NCH):
        @b.sync
        def _(s):
            s.wait_ge(C.sem["LDS"], C.cnt["LDS"])
            C.w(s, "VV", "EV", "ST")
            ins = s.dma_start(t_s[:, 0:2048], xsp[d_ * 128:(d_ + 1) * 128, :])
            C.inc(ins, "LDS", 16)
        for tc in range(NTC):
            sl = slice(tc * 512, (tc + 1) * 512)
            @b.vector
            def _(v):
                C.w(v, "LDS", "ST")
                if is0:
                    ins = v.tensor_tensor(H["f512_b"][:, 0:512], t_s[:, sl],
                                          xn[:, d_, sl], op=ALU.add)
                else:
                    ins = v.scalar_tensor_tensor(H["f512_b"][:, 0:512], xa[:, d_, sl],
                                                 0.5, t_s[:, sl], op0=ALU.mult, op1=ALU.add)
                C.inc(ins, "VV")
            @b.sync
            def _(s):
                C.w(s, "VV")
                ins = s.dma_start(xsp[d_ * 128:(d_ + 1) * 128, sl], H["f512_b"][:, 0:512])
                C.inc(ins, "ST", 16)
    C.x_in_hbm = True


def emit_l0f(C, H, li):
    emit_moe(C, H, "l0f", li)


def emit_l3f(C, H, li):
    emit_moe(C, H, "l3f", li)



def emit_l1a(C, H, li):
    b = C.block
    xa, xn = H["xa"], H["xn"]
    PS = H["PS"]
    scr = H["scr16"]
    xsp = C.dram["xspill"]
    t_s = H["t_s"]
    xa_bf = H["xa_t"][:].bitcast(BF16)
    kT = xa_bf[:, 0:16384].rearrange("p (c t) -> p c t", c=NCH)
    rT = xa_bf[:, 16384:32768].rearrange("p (c t) -> p c t", c=NCH)
    Vtm = H["xn_t"][:].rearrange("p (tt c) -> p tt c", tt=16)
    ynT = H["xn_t"][:].rearrange("p (c t) -> p c t", c=NCH)
    AR = H["AR2"]
    dxa = AR[:, 0:16384].rearrange("p (c t) -> p c t", c=NCH)
    g_fm = dxa
    hb = AR[:, 16384:32768].rearrange("p (c t) -> p c t", c=NCH)
    y_tm = AR[:, 16384:32768].rearrange("p (tt c) -> p tt c", tt=16)
    fa, fb = H["f512_a"], H["f512_b"]
    sw = scr[:, 0:4096].bitcast(F32)
    ps4b = PS[4].bitcast(BF16)

    @b.gpsimd
    def _(g):
        g.wait_ge(C.sem["LDG"], C.cnt["LDG"])
        C.w(g, "MM", "EV", "VV")
        for (dst, nm) in ((H["pp"][:, 0:8], "l1a_miux"), (H["pp"][:, 8:48], "l1a_lam"),
                          (H["tdmiu2"][:], "l1a_tdmiu"), (H["u2"][:], "l1a_u"),
                          (H["msk2"][:, 0:128], "c_mstrict"), (H["msk2"][:, 128:256], "c_identf"),
                          (H["ident_bf"][:], "c_ident"), (H["tdb"][:], "l1a_tdb")):
            ins = g.dma_start(dst, C.dram[nm][:, :])
            C.inc(ins, "LDG", 16)
    @b.sync
    def _(s):
        s.wait_ge(C.sem["LDS"], C.cnt["LDS"])
        C.w(s, "VV", "EV")
        ins = s.dma_start(H["b8"][:, 32:40], C.dram["l1a_gng"][:, :])
        C.inc(ins, "LDS", 16)
        ins = s.dma_start(H["b8"][:, 40:48], C.dram["l1a_gnb"][:, :])
        C.inc(ins, "LDS", 16)

    @b.sync
    def _(s):
        C.wall(s)
        for d_ in range(NCH):
            ins = s.dma_start(xsp[d_ * 128:(d_ + 1) * 128, :], xa[:, d_, :])
            C.inc(ins, "ST", 16)

    @b.vector
    def _(v):
        C.w(v, "LDG", "LDS", "EV", "MM")
        for d_ in range(NCH):
            v.tensor_scalar(dxa[:, d_, 0:1], xn[:, d_, 0:1], -1.0, None, op0=ALU.mult)
            ins = v.tensor_tensor(dxa[:, d_, 1:2048], xn[:, d_, 0:2047],
                                  xn[:, d_, 1:2048], op=ALU.subtract)
        C.fence(v, ins, "VV")
        for d_ in range(NCH):
            ins = v.scalar_tensor_tensor(hb[:, d_, :], dxa[:, d_, :],
                                         H["pp"][:, d_:d_ + 1], xn[:, d_, :],
                                         op0=ALU.mult, op1=ALU.add)
        C.inc(ins, "VV")

    @b.gpsimd
    def _(g):
        g.wait_ge(C.sem["LDG"], C.cnt["LDG"])
        C.w(g, "VV", "MM", "EV")
        for ccx in range(NCH):
            ins = g.dma_start(scr[:, 4096 + ccx * 160: 4096 + (ccx + 1) * 160],
                              C.dram["l1a_alora"][ccx, :, :])
            C.inc(ins, "LDG", 16)
    for f in range(5):
        dst = (H["loraA"], H["loraB"], H["loraC"])[f // 2]
        r0 = (f % 2) * 32 if f < 4 else 0
        for tc in range(NTC):
            sl = slice(tc * 512, (tc + 1) * 512)
            @b.tensor
            def _(t):
                C.w(t, "LDG", "VV", "EV")
                for ccx in range(NCH):
                    ins = t.matmul(PS[0][r0:r0 + 32, 0:512],
                                   scr[:, 4096 + ccx * 160 + f * 32: 4096 + ccx * 160 + (f + 1) * 32],
                                   hb[:, ccx, sl], start=(ccx == 0), stop=(ccx == NCH - 1),
                                   tile_position=(0, r0))
                C.inc(ins, "MM")
            @b.scalar
            def _(s):
                C.w(s, "MM")
                ins = s.activation(dst[r0:r0 + 32, sl], PS[0][r0:r0 + 32, 0:512], AF.Tanh)
                C.inc(ins, "EV")

    for f, tag in ((0, "w"), (1, "k"), (3, "r"), (4, "g"), (2, "v")):
        lsrc = (H["loraA"], H["loraB"], H["loraC"])[f // 2]
        r0 = (f % 2) * 32 if f < 4 else 0
        @b.gpsimd
        def _(g):
            g.wait_ge(C.sem["LDG"], C.cnt["LDG"])
            C.w(g, "MM", "EV", "VV")
            ins = g.dma_start(H["blf"][r0:r0 + 32, :],
                              C.dram["l1a_blora"][:, f * 1024:(f + 1) * 1024])
            C.inc(ins, "LDG", 16)
        for d_ in range(NCH):
            for tc in range(NTC):
                sl = slice(tc * 512, (tc + 1) * 512)
                @b.tensor
                def _(t):
                    C.w(t, "LDG", "VV", "EV")
                    ins = t.matmul(PS[0][:, 0:512], H["blf"][r0:r0 + 32, d_ * 128:(d_ + 1) * 128],
                                   lsrc[r0:r0 + 32, sl], start=True, stop=True)
                    C.inc(ins, "MM")
                @b.scalar
                def _(s):
                    C.w(s, "MM")
                    ins = s.activation(fa[:, 0:512], PS[0][:, 0:512], AF.Identity,
                                       bias=H["pp"][:, 8 + f * 8 + d_:9 + f * 8 + d_])
                    C.inc(ins, "EV")
                @b.vector
                def _(v):
                    C.w(v, "EV")
                    if tag == "v":
                        if tc == 0:
                            v.tensor_scalar(fb[:, 0:1], xn[:, d_, 0:1], -1.0, None, op0=ALU.mult)
                            ins = v.tensor_tensor(fb[:, 1:512], xn[:, d_, 0:511],
                                                  xn[:, d_, 1:512], op=ALU.subtract)
                        else:
                            ins = v.tensor_tensor(fb[:, 0:512], xn[:, d_, tc * 512 - 1:tc * 512 + 511],
                                                  xn[:, d_, sl], op=ALU.subtract)
                        C.fence(v, ins, "VV")
                        ins = v.tensor_tensor(fb[:, 0:512], fb[:, 0:512], fa[:, 0:512], op=ALU.mult)
                        C.fence(v, ins, "VV")
                    else:
                        ins = v.tensor_tensor(fb[:, 0:512], dxa[:, d_, sl], fa[:, 0:512], op=ALU.mult)
                        C.fence(v, ins, "VV")
                    ins = v.tensor_tensor(hb[:, d_, sl], fb[:, 0:512], xn[:, d_, sl], op=ALU.add)
                    C.inc(ins, "VV")
        if tag == "w":
            @b.gpsimd
            def _(g):
                g.wait_ge(C.sem["LDG"], C.cnt["LDG"])
                C.w(g, "MM", "EV", "VV")
                for ccx in range(NCH):
                    ins = g.dma_start(scr[:, 5440 + ccx * 64: 5440 + (ccx + 1) * 64],
                                      C.dram["l1a_tda"][ccx, :, :])
                    C.inc(ins, "LDG", 16)
            for tc in range(NTC):
                sl = slice(tc * 512, (tc + 1) * 512)
                @b.tensor
                def _(t):
                    C.w(t, "LDG", "VV", "EV")
                    for ccx in range(NCH):
                        ins = t.matmul(PS[1][0:64, 0:512], scr[:, 5440 + ccx * 64: 5440 + (ccx + 1) * 64],
                                       hb[:, ccx, sl], start=(ccx == 0), stop=(ccx == NCH - 1))
                    C.inc(ins, "MM")
                @b.scalar
                def _(s):
                    C.w(s, "MM")
                    ins = s.activation(H["twT"][:, sl], PS[1][0:64, 0:512], AF.Tanh)
                    C.inc(ins, "EV")
        elif tag == "v":
            for cb in range(2):
                @b.gpsimd
                def _(g):
                    g.wait_ge(C.sem["LDG"], C.cnt["LDG"])
                    C.w(g, "MM", "EV", "VV")
                    ins = g.dma_start(scr[:, 4096:8192], C.dram["l1a_wv"][cb, :, :])
                    C.inc(ins, "LDG", 16)
                scrv = scr[:, 4096:8192].rearrange("p (c j) -> p c j", c=NCH)
                for tt in range(16):
                    @b.tensor
                    def _(t):
                        C.w(t, "LDG", "VV", "EV")
                        for ccx in range(NCH):
                            ins = t.matmul(PS[1][:, 0:512], hb[:, ccx, tt * 128:(tt + 1) * 128],
                                           scrv[:, ccx, :], start=(ccx == 0), stop=(ccx == NCH - 1))
                        C.inc(ins, "MM")
                    @b.scalar
                    def _(s):
                        C.w(s, "MM")
                        ins = s.copy(Vtm[:, tt, cb * 512:(cb + 1) * 512], PS[1][:, 0:512])
                        C.inc(ins, "EV")
        else:
            wnm = {"k": "l1a_wk", "r": "l1a_wr", "g": "l1a_wg"}[tag]
            dstT = {"k": kT, "r": rT, "g": g_fm}[tag]
            for oc in range(NCH):
                @b.gpsimd
                def _(g):
                    g.wait_ge(C.sem["LDG"], C.cnt["LDG"])
                    C.w(g, "MM", "EV", "VV")
                    ins = g.dma_start(scr[:, 4096:5120], C.dram[wnm][oc, :, :])
                    C.inc(ins, "LDG", 16)
                scw = scr[:, 4096:5120].rearrange("p (c j) -> p c j", c=NCH)
                for tc in range(NTC):
                    sl = slice(tc * 512, (tc + 1) * 512)
                    @b.tensor
                    def _(t):
                        C.w(t, "LDG", "VV", "EV")
                        for ccx in range(NCH):
                            ins = t.matmul(PS[1][:, 0:512], scw[:, ccx, :],
                                           hb[:, ccx, sl], start=(ccx == 0), stop=(ccx == NCH - 1))
                        C.inc(ins, "MM")
                    @b.scalar
                    def _(s):
                        C.w(s, "MM")
                        ins = s.activation(dstT[:, oc, sl], PS[1][:, 0:512],
                                           AF.Silu if tag == "g" else AF.Identity)
                        C.inc(ins, "EV")

    zzf = sw[:, 896:1024]
    @b.vector
    def _(v):
        C.w(v, "EV", "MM")
        v.memset(zzf, 0.0)
        ins = v.memset(H["Ssb"][:], 0.0)
        C.inc(ins, "VV")
    for c_ in range(16):
        for h in range(16):
            B, chh = (h % 2) * 64, h // 2
            ew, CSs, tmp = sw[B:B + 64, 0:128], sw[B:B + 64, 128:256], sw[B:B + 64, 256:384]
            aex, ainv = sw[B:B + 64, 384:512], sw[B:B + 64, 512:640]
            ek, kinv = sw[B:B + 64, 640:768], sw[B:B + 64, 768:896]
            tmp2 = sw[B:B + 64, 1152:1280]
            plc = sw[B:B + 64, 1024:1025]
            zz = zzf[B:B + 64, :]
            rt_b = scr[B:B + 64, 4096:4224]
            kt_b = scr[B:B + 64, 4224:4352]
            kh_b = scr[B:B + 64, 4352:4480]
            kc_b = scr[B:B + 64, 4480:4608]
            A1, A2, Am = scr[:, 4608:4736], scr[:, 4736:4864], scr[:, 4864:4992]
            khtm = scr[:, 4992:5056]
            Ss = H["Ssb"][B:B + 64, chh * 64:(chh + 1) * 64]
            Sbf = H["Sbf"][B:B + 64, :]
            @b.tensor
            def _(t):
                C.w(t, "VV", "EV")
                ins = t.matmul(PS[0][B:B + 64, 0:128], H["tdb"][0:64, h * 64:(h + 1) * 64],
                               H["twT"][0:64, c_ * 128:(c_ + 1) * 128], start=True, stop=True,
                               tile_position=(0, B))
                C.inc(ins, "MM")
            @b.scalar
            def _(s):
                C.w(s, "MM")
                ins = s.activation(ew, PS[0][B:B + 64, 0:128], AF.Exp,
                                   bias=H["tdmiu2"][B:B + 64, chh:chh + 1])
                C.inc(ins, "EV")
            @b.vector
            def _(v):
                C.w(v, "EV")
                ins = v.tensor_tensor_scan(CSs, ew, zz, 0.0, op0=ALU.add, op1=ALU.add)
                C.fence(v, ins, "VV")
                ins = v.tensor_tensor(tmp, ew, CSs, op=ALU.subtract)
                C.fence(v, ins, "VV")
                ins = v.tensor_scalar(tmp2, CSs, CSs[:, 127:128], None, op0=ALU.subtract)
                C.inc(ins, "VV")
            @b.scalar
            def _(s):
                C.w(s, "VV")
                s.activation(ainv, CSs, AF.Exp)
                s.activation(ek, tmp2, AF.Exp)
                s.activation(kinv, tmp, AF.Exp, bias=0.0, scale=-1.0)
                s.activation(aex, tmp, AF.Exp)
                ins = s.activation(plc, CSs[:, 127:128], AF.Exp, bias=0.0, scale=-1.0)
                C.inc(ins, "EV")
            @b.vector
            def _(v):
                C.w(v, "EV")
                v.tensor_tensor(rt_b, rT[B:B + 64, chh, c_ * 128:(c_ + 1) * 128], aex, op=ALU.mult)
                v.tensor_tensor(kt_b, kT[B:B + 64, chh, c_ * 128:(c_ + 1) * 128], ainv, op=ALU.mult)
                v.tensor_tensor(kh_b, kT[B:B + 64, chh, c_ * 128:(c_ + 1) * 128], ek, op=ALU.mult)
                ins = v.tensor_tensor(kc_b, kT[B:B + 64, chh, c_ * 128:(c_ + 1) * 128], kinv, op=ALU.mult)
                C.fence(v, ins, "VV")
                ins = v.tensor_scalar(kc_b, kc_b, H["u2"][B:B + 64, chh:chh + 1], None, op0=ALU.mult)
                C.fence(v, ins, "VV")
                ins = v.tensor_copy(Sbf, Ss)
                C.inc(ins, "VV")
            @b.tensor
            def _(t):
                C.w(t, "VV", "EV")
                t.matmul(PS[1][:, 0:128], kt_b, rt_b, start=True, stop=True)
                t.matmul(PS[2][:, 0:128], kc_b, rt_b, start=True, stop=True)
                ins = t.transpose(ps4b[:, 0:64], kh_b, H["ident_bf"][B:B + 64, B:B + 64])
                C.inc(ins, "MM")
            @b.scalar
            def _(s):
                C.w(s, "MM")
                ins = s.copy(khtm, ps4b[:, 0:64])
                C.inc(ins, "EV")
            @b.vector
            def _(v):
                C.w(v, "MM", "EV")
                v.tensor_tensor(A1, PS[1][:, 0:128], H["msk2"][:, 0:128], op=ALU.mult)
                ins = v.tensor_tensor(A2, PS[2][:, 0:128], H["msk2"][:, 128:256], op=ALU.mult)
                C.fence(v, ins, "VV")
                ins = v.tensor_tensor(Am, A1, A2, op=ALU.add)
                C.inc(ins, "VV")
            @b.tensor
            def _(t):
                C.w(t, "VV", "EV")
                t.matmul(PS[3][:, 0:64], Am, Vtm[:, c_, h * 64:(h + 1) * 64],
                         start=True, stop=False)
                t.matmul(PS[3][:, 0:64], rt_b, Sbf, start=False, stop=True)
                ins = t.matmul(PS[5][B:B + 64, 0:64], khtm, Vtm[:, c_, h * 64:(h + 1) * 64],
                               start=True, stop=True, tile_position=(0, B))
                C.inc(ins, "MM")
            @b.scalar
            def _(s):
                C.w(s, "MM")
                ins = s.copy(y_tm[:, c_, h * 64:(h + 1) * 64], PS[3][:, 0:64])
                C.inc(ins, "EV")
            @b.vector
            def _(v):
                C.w(v, "MM")
                ins = v.tensor_scalar(Ss, Ss, plc, None, op0=ALU.mult)
                C.fence(v, ins, "VV")
                ins = v.tensor_tensor(Ss, Ss, PS[5][B:B + 64, 0:64], op=ALU.add)
                C.inc(ins, "VV")

    ynp = scr[:, 5056:5120]
    for tt in range(16):
        for h in range(16):
            po, chh = (h % 2) * 64, h // 2
            ysl = y_tm[:, tt, h * 64:(h + 1) * 64]
            @b.vector
            def _(v):
                C.w(v, "EV", "MM")
                ins = v.bn_stats(fa[:, 0:6], ysl)
                C.fence(v, ins, "VV")
                ins = v.bn_aggr(fa[:, 8:10], fa[:, 0:6])
                C.inc(ins, "VV")
            @b.scalar
            def _(s):
                C.w(s, "VV")
                ins = s.activation(fa[:, 12:13], fa[:, 9:10], AF.Ln, bias=H["cst"][:, 2:3])
                C.fence(s, ins, "EV")
                ins = s.activation(fa[:, 12:13], fa[:, 12:13], AF.Exp, bias=0.0, scale=-0.5)
                C.inc(ins, "EV")
            @b.vector
            def _(v):
                C.w(v, "EV")
                ins = v.tensor_scalar(ynp, ysl, fa[:, 8:9], fa[:, 12:13],
                                      op0=ALU.subtract, op1=ALU.mult)
                C.inc(ins, "VV")
            @b.tensor
            def _(t):
                C.w(t, "VV", "EV")
                if po == 0:
                    ins = t.transpose(ps4b[0:64, 0:128], ynp, H["ident_bf"][:])
                else:
                    ins = t.transpose(ps4b[64:128, 0:128], ynp, H["ident_bf"][:],
                                      tile_position=(0, 64))
                C.inc(ins, "MM")
            @b.scalar
            def _(s):
                C.w(s, "MM")
                ins = s.activation(ynT[po:po + 64, chh, tt * 128:(tt + 1) * 128],
                                   ps4b[po:po + 64, 0:128], AF.Identity,
                                   bias=H["b8"][po:po + 64, 40 + chh:41 + chh],
                                   scale=H["b8"][po:po + 64, 32 + chh:33 + chh])
                C.inc(ins, "EV")
    @b.vector
    def _(v):
        C.w(v, "EV", "MM")
        for d_ in range(NCH):
            ins = v.tensor_tensor(g_fm[:, d_, :], g_fm[:, d_, :], ynT[:, d_, :], op=ALU.mult)
        C.inc(ins, "VV")
    for oc in range(NCH):
        @b.gpsimd
        def _(g):
            g.wait_ge(C.sem["LDG"], C.cnt["LDG"])
            C.w(g, "MM", "EV", "VV")
            ins = g.dma_start(scr[:, 4096:5120], C.dram["l1a_wo"][oc, :, :])
            C.inc(ins, "LDG", 16)
        @b.sync
        def _(s):
            s.wait_ge(C.sem["LDS"], C.cnt["LDS"])
            C.w(s, "VV", "ST")
            ins = s.dma_start(t_s[:, 0:2048], xsp[oc * 128:(oc + 1) * 128, :])
            C.inc(ins, "LDS", 16)
        scw = scr[:, 4096:5120].rearrange("p (c j) -> p c j", c=NCH)
        for tc in range(NTC):
            sl = slice(tc * 512, (tc + 1) * 512)
            @b.tensor
            def _(t):
                C.w(t, "LDG", "VV", "EV")
                for ccx in range(NCH):
                    ins = t.matmul(PS[0][:, 0:512], scw[:, ccx, :], g_fm[:, ccx, sl],
                                   start=(ccx == 0), stop=(ccx == NCH - 1))
                C.inc(ins, "MM")
            @b.vector
            def _(v):
                C.w(v, "MM", "LDS", "ST")
                ins = v.scalar_tensor_tensor(fb[:, 0:512], PS[0][:, 0:512], 1.0,
                                             t_s[:, sl], op0=ALU.mult, op1=ALU.add)
                C.inc(ins, "VV")
            @b.sync
            def _(s):
                C.w(s, "VV")
                ins = s.dma_start(xsp[oc * 128:(oc + 1) * 128, sl], fb[:, 0:512])
                C.inc(ins, "ST", 16)
    C.x_in_hbm = True



def emit_l2f(C, H, li):
    b = C.block
    xa, xn = H["xa"], H["xn"]
    PS = H["PS"]
    scr = H["scr16"]
    scrq = scr.rearrange("p (c j) -> p c j", c=NCH)
    xsp = C.dram["xspill"]
    t_s = H["t_s"]
    fa, fb = H["f512_a"], H["f512_b"]
    ps4b = PS[4].bitcast(BF16)
    QTf = H["AR2"][:, 0:16384].rearrange("p (c t) -> p c t", c=NCH)
    KTf = H["AR2"][:, 16384:32768].rearrange("p (c t) -> p c t", c=NCH)
    xa_bf = H["xa_t"][:].bitcast(BF16)
    Vtm = xa_bf[:, 0:16384].rearrange("p (tt c) -> p tt c", tt=16)
    xsb = xn
    ix = xa
    xsp2 = C.dram["xspill2"]
    hAR = H["AR2"][:, 0:22528].rearrange("p (f t) -> p f t", f=22)
    qt_s, qsq, rr = H["qt_s"], H["qsq_s"], H["rr"]
    probs = H["qt_s"]

    @b.gpsimd
    def _(g):
        g.wait_ge(C.sem["LDG"], C.cnt["LDG"])
        C.w(g, "MM", "EV", "VV")
        ins = g.dma_start(H["mask_f"][:], C.dram["c_causal_add"][:, :])
        C.inc(ins, "LDG", 16)
        ins = g.dma_start(H["mlow"][:], C.dram["c_causal_low"][:, :])
        C.inc(ins, "LDG", 16)
        ins = g.dma_start(H["ident_bf"][:], C.dram["c_ident"][:, :])
        C.inc(ins, "LDG", 16)
        ins = g.dma_start(H["l2perm_sb"][:], C.dram["c_l2perm"][:, :])
        C.inc(ins, "LDG", 16)

    @b.sync
    def _(s):
        C.wall(s)
        for d_ in range(NCH):
            ins = s.dma_start(xsp[d_ * 128:(d_ + 1) * 128, :], xa[:, d_, :])
            C.inc(ins, "ST", 16)
    @b.vector
    def _(v):
        C.w(v, "ST", "MM", "EV", "GG", "LDG")
        for d_ in range(NCH):
            ins = v.tensor_copy(ix[:, d_, :], xn[:, d_, :])
        C.inc(ins, "VV")

    for il in range(L2F_DEPTH):
        @b.vector
        def _(v):
            C.w(v, "MM", "EV")
            for d_ in range(NCH):
                if d_ < 4:
                    ins = v.tensor_copy(xsb[:, d_, :], ix[:, d_, :])
                else:
                    v.memset(xsb[:, d_, 0:1], 0.0)
                    ins = v.tensor_copy(xsb[:, d_, 1:2048], ix[:, d_, 0:2047])
            C.inc(ins, "VV")
        @b.sync
        def _(s):
            s.wait_ge(C.sem["LDS"], C.cnt["LDS"])
            C.w(s, "VV", "EV", "MM")
            for ti in range(4):
                ins = s.dma_start(H["rr"][:, 44 + ti:45 + ti],
                                  C.dram["l2f_scales"][il, ti, :, :])
                C.inc(ins, "LDS", 16)
        @b.gpsimd
        def _(g):
            g.wait_ge(C.sem["LDG"], C.cnt["LDG"])
            C.w(g, "EV", "VV", "MM")
            for ti, nm in enumerate(("c_cs8fm", "c_ss8fm", "c_cifm", "c_sifm")):
                ins = g.dma_start(H["tabF"][:, ti * 2048:(ti + 1) * 2048], C.dram[nm][:, :])
                C.inc(ins, "LDG", 16)
        @b.vector
        def _(v):
            C.w(v, "LDS", "LDG")
            for ti in range(4):
                ins = v.tensor_scalar(H["tabF"][:, ti * 2048:(ti + 1) * 2048],
                                      H["tabF"][:, ti * 2048:(ti + 1) * 2048],
                                      H["rr"][:, 44 + ti:45 + ti], None, op0=ALU.mult)
            C.inc(ins, "VV")
        @b.sync
        def _(s):
            C.w(s, "VV", "EV", "MM", "ST")
            for d_ in range(NCH):
                ins = s.dma_start(xsp2[d_ * 128:(d_ + 1) * 128, :], ix[:, d_, :])
                C.inc(ins, "ST", 16)
        for sec in range(3):
            @b.gpsimd
            def _(g):
                g.wait_ge(C.sem["LDG"], C.cnt["LDG"])
                C.w(g, "MM", "EV", "VV")
                for ccx in range(NCH):
                    ins = g.dma_start(scrq[:, ccx, :],
                                      C.dram["l2f_wqkv"][il, sec, :, ccx * 1024:(ccx + 1) * 1024])
                    C.inc(ins, "LDG", 16)
            ldg_t = C.cnt["LDG"]
            stq = {}

            def qk_mm(tt):
                p = tt % 2
                @b.tensor
                def _(t):
                    t.wait_ge(C.sem["LDG"], ldg_t)
                    if tt >= 2:
                        t.wait_ge(C.sem["EV"], stq[("ev", tt - 2)])
                    elif tt == 0:
                        C.w(t, "EV", "VV")
                    for cb in range(2):
                        for ccx in range(NCH):
                            ins = t.matmul(PS[2 * p + cb][:, 0:512],
                                           xsb[:, ccx, tt * 128:(tt + 1) * 128],
                                           scrq[:, ccx, cb * 512:(cb + 1) * 512],
                                           start=(ccx == 0), stop=(ccx == NCH - 1))
                    C.inc(ins, "MM")
                stq[("mm", tt)] = C.cnt["MM"]

            def v_rest(tt):
                p = tt % 2
                @b.scalar
                def _(s):
                    s.wait_ge(C.sem["MM"], stq[("mm", tt)])
                    s.copy(Vtm[:, tt, 0:512], PS[2 * p][:, 0:512])
                    ins = s.copy(Vtm[:, tt, 512:1024], PS[2 * p + 1][:, 0:512])
                    C.inc(ins, "EV")
                stq[("ev", tt)] = C.cnt["EV"]

            def qk_rest(tt):
                p = tt % 2
                @b.scalar
                def _(s):
                    s.wait_ge(C.sem["MM"], stq[("mm", tt)])
                    if tt >= 1:
                        s.wait_ge(C.sem["MM"], stq[("tp", tt - 1)])
                    s.copy(qt_s[:, 0:512], PS[2 * p][:, 0:512])
                    ins = s.copy(qt_s[:, 512:1024], PS[2 * p + 1][:, 0:512])
                    C.inc(ins, "EV")
                stq[("ev", tt)] = C.cnt["EV"]
                @b.gpsimd
                def _(g):
                    g.wait_ge(C.sem["EV"], stq[("ev", tt)])
                    ins = g.tensor_tensor(qsq[:], qt_s[:], qt_s[:], op=ALU.mult)
                    C.inc(ins, "GG")
                gg_t = C.cnt["GG"]
                @b.vector
                def _(v):
                    v.wait_ge(C.sem["GG"], gg_t)
                    ins = v.tensor_reduce(rr[:, 0:16], qsq[:].rearrange("p (h k) -> p h k", k=64),
                                          axis=AX.X, op=ALU.add)
                    C.inc(ins, "VV")
                vv_t = C.cnt["VV"]
                @b.scalar
                def _(s):
                    s.wait_ge(C.sem["VV"], vv_t)
                    ins = s.activation(rr[:, 16:32], rr[:, 0:16], AF.Sqrt)
                    C.inc(ins, "EV")
                ev2_t = C.cnt["EV"]
                @b.vector
                def _(v):
                    v.wait_ge(C.sem["EV"], ev2_t)
                    ins = v.tensor_scalar(rr[:, 16:32], rr[:, 16:32], 1e-12, None, op0=ALU.max)
                    C.fence(v, ins, "VV")
                    ins = v.reciprocal(rr[:, 32:48], rr[:, 16:32])
                    C.fence(v, ins, "VV")
                    if sec == 1:
                        for h in range(16):
                            ins = v.tensor_scalar(qt_s[:, h * 64:(h + 1) * 64],
                                                  qt_s[:, h * 64:(h + 1) * 64],
                                                  rr[:, 32 + h:33 + h], None, op0=ALU.mult)
                    else:
                        ins = v.tensor_copy(H["rqall"][:, tt * 16:(tt + 1) * 16], rr[:, 32:48])
                    C.inc(ins, "VV")
                stq[("nm", tt)] = C.cnt["VV"]
                dstT = QTf if sec == 0 else KTf
                for hg in range(2):
                    @b.tensor
                    def _(t):
                        t.wait_ge(C.sem["VV"], stq[("nm", tt)])
                        if tt >= 1 or hg == 1:
                            t.wait_ge(C.sem["EV"], C.cnt["EV"])
                        for j in range(8):
                            h = hg * 8 + j
                            ins = t.transpose(ps4b[0:64, j * 128:(j + 1) * 128],
                                              qt_s[:, h * 64:(h + 1) * 64],
                                              H["ident_bf"][:])
                        C.inc(ins, "MM")
                    mmtp = C.cnt["MM"]
                    @b.scalar
                    def _(s):
                        s.wait_ge(C.sem["MM"], mmtp)
                        for j in range(8):
                            h = hg * 8 + j
                            po, chh = (h % 2) * 64, h // 2
                            ins = s.copy(dstT[po:po + 64, chh, tt * 128:(tt + 1) * 128],
                                         ps4b[0:64, j * 128:(j + 1) * 128])
                        C.inc(ins, "EV")
                stq[("tp", tt)] = C.cnt["MM"]

            for tt in range(17):
                if tt < 16:
                    qk_mm(tt)
                if tt >= 1:
                    (v_rest if sec == 2 else qk_rest)(tt - 1)
        for si, srcT in ((0, QTf), (1, KTf)):
            toff = 0 if si == 0 else 4096
            rst = {}
            for u in range(33):
                d_, tc = u // 4, u % 4
                if u < 32:
                    mm_pre = C.cnt["MM"]
                    @b.tensor
                    def _(t):
                        if u >= 2:
                            t.wait_ge(C.sem["VV"], rst[("tt", u - 2)])
                        else:
                            C.w(t, "EV", "VV")
                        sl = slice(tc * 512, (tc + 1) * 512)
                        ins = t.matmul(PS[6 + u % 2][:, 0:512], H["l2perm_sb"][:],
                                       srcT[:, d_, sl], start=True, stop=True)
                        C.inc(ins, "MM")
                    rst[("mm", u)] = C.cnt["MM"]
                if u >= 1:
                    up = u - 1
                    dp, tcp = up // 4, up % 4
                    slp = slice(tcp * 512, (tcp + 1) * 512)
                    @b.vector
                    def _(v):
                        v.wait_ge(C.sem["MM"], rst[("mm", up)])
                        v.tensor_tensor(fa[:, 0:512], srcT[:, dp, slp],
                                        H["tabF"][:, toff + tcp * 512:toff + (tcp + 1) * 512],
                                        op=ALU.mult)
                        ins = v.tensor_tensor(fb[:, 0:512], PS[6 + up % 2][:, 0:512],
                                              H["tabF"][:, 2048 + toff + tcp * 512:2048 + toff + (tcp + 1) * 512],
                                              op=ALU.mult)
                        C.fence(v, ins, "VV")
                        ins = v.tensor_tensor(srcT[:, dp, slp], fa[:, 0:512], fb[:, 0:512],
                                              op=ALU.add)
                        C.inc(ins, "VV")
                    rst[("tt", up)] = C.cnt["VV"]
        for hp in range(8 if "attn" in L2F_PARTS else 0):
            chh = hp
            st = {}

            def _sc(qt):
                kts = [kt for kt in (qt - 2, qt - 1, qt) if kt >= 0]
                return kts, kts[0] * 128, len(kts) * 128

            def stage_scores(qt):
                p = qt % 2
                kts, k0, NK = _sc(qt)
                ev_guard = C.cnt["EV"]
                vv_guard = C.cnt["VV"]
                @b.tensor
                def _(t):
                    if qt >= 2:
                        t.wait_ge(C.sem["EV"], st[("exp", qt - 2)])
                    elif hp == 0:
                        t.wait_ge(C.sem["EV"], ev_guard)
                        t.wait_ge(C.sem["VV"], vv_guard)
                    for pо in (0, 64):
                        ins = t.matmul(PS[2 * p + pо // 64][:, 0:NK],
                                       QTf[pо:pо + 64, chh, qt * 128:(qt + 1) * 128],
                                       KTf[pо:pо + 64, chh, k0:k0 + NK], start=True, stop=True)
                    C.inc(ins, "MM")
                st[("sc", qt)] = C.cnt["MM"]

            def stage_rest(qt):
                p = qt % 2
                kts, k0, NK = _sc(qt)
                @b.vector
                def _(v):
                    v.wait_ge(C.sem["MM"], st[("sc", qt)])
                    for j2 in range(2):
                        psb = PS[2 * p + j2]
                        v.reduce_max(fa[:, j2 * 4:j2 * 4 + 1], psb[:, 0:NK],
                                     axis=AX.X, negate=True)
                        v.tensor_tensor(psb[:, NK - 128:NK], psb[:, NK - 128:NK],
                                        H["mask_f"][:], op=ALU.add)
                        if len(kts) == 3:
                            v.tensor_tensor(psb[:, 0:128], psb[:, 0:128],
                                            H["mlow"][:], op=ALU.add)
                    ins = v.tensor_copy(fa[:, 8:9], fa[:, 8:9])
                    C.fence(v, ins, "VV")
                    for j2 in range(2):
                        h = chh * 2 + j2
                        ins = v.tensor_tensor(fa[:, j2 * 4 + 1:j2 * 4 + 2], fa[:, j2 * 4:j2 * 4 + 1],
                                              H["rqall"][:, qt * 16 + h:qt * 16 + h + 1], op=ALU.mult)
                    C.inc(ins, "VV")
                st[("mx", qt)] = C.cnt["VV"]
                @b.scalar
                def _(s):
                    s.wait_ge(C.sem["VV"], st[("mx", qt)])
                    for j2 in range(2):
                        h = chh * 2 + j2
                        buf = probs if j2 == 0 else H["qsq_s"]
                        ins = s.activation(buf[:, p * 512:p * 512 + NK],
                                           PS[2 * p + j2][:, 0:NK], AF.Exp,
                                           bias=fa[:, j2 * 4 + 1:j2 * 4 + 2],
                                           scale=H["rqall"][:, qt * 16 + h:qt * 16 + h + 1],
                                           accum_out=fa[:, j2 * 4 + 2:j2 * 4 + 3])
                    C.inc(ins, "EV")
                st[("exp", qt)] = C.cnt["EV"]
                @b.vector
                def _(v):
                    v.wait_ge(C.sem["EV"], st[("exp", qt)])
                    v.reciprocal(fa[:, 3:4], fa[:, 2:3])
                    ins = v.reciprocal(fa[:, 7:8], fa[:, 6:7])
                    C.fence(v, ins, "VV")
                    for j2 in range(2):
                        buf = probs if j2 == 0 else H["qsq_s"]
                        ins = v.tensor_scalar(buf[:, p * 512:p * 512 + NK],
                                              buf[:, p * 512:p * 512 + NK],
                                              fa[:, j2 * 4 + 3:j2 * 4 + 4], None, op0=ALU.mult)
                    C.inc(ins, "VV")
                st[("nm", qt)] = C.cnt["VV"]
                @b.tensor
                def _(t):
                    t.wait_ge(C.sem["VV"], st[("nm", qt)])
                    if qt >= 2:
                        t.wait_ge(C.sem["EV"], st[("slab", qt - 2)])
                    for j2 in range(2):
                        buf = probs if j2 == 0 else H["qsq_s"]
                        for j in range(len(kts)):
                            ins = t.transpose(ps4b[:, (j2 * 3 + j) * 128:(j2 * 3 + j + 1) * 128],
                                              buf[:, p * 512 + j * 128:p * 512 + (j + 1) * 128],
                                              H["ident_bf"][:])
                    C.inc(ins, "MM")
                st[("tp", qt)] = C.cnt["MM"]
                @b.scalar
                def _(s):
                    s.wait_ge(C.sem["MM"], st[("tp", qt)])
                    ins = s.copy(H["slab2"][:, 0:768], ps4b[:, 0:768])
                    C.inc(ins, "EV")
                st[("slab", qt)] = C.cnt["EV"]
                @b.tensor
                def _(t):
                    t.wait_ge(C.sem["EV"], st[("slab", qt)])
                    for j2 in range(2):
                        h = chh * 2 + j2
                        for j, kt in enumerate(kts):
                            ins = t.matmul(PS[5 + j2][0:64, 0:128],
                                           Vtm[:, kt, h * 64:(h + 1) * 64],
                                           H["slab2"][:, (j2 * 3 + j) * 128:(j2 * 3 + j + 1) * 128],
                                           start=(j == 0), stop=(j == len(kts) - 1))
                    C.inc(ins, "MM")
                st[("yt", qt)] = C.cnt["MM"]
                @b.scalar
                def _(s):
                    s.wait_ge(C.sem["MM"], st[("yt", qt)])
                    for j2 in range(2):
                        pо = j2 * 64
                        ins = s.copy(QTf[pо:pо + 64, chh, qt * 128:(qt + 1) * 128],
                                     PS[5 + j2][0:64, 0:128])
                    C.inc(ins, "EV")
                st[("yev", qt)] = C.cnt["EV"]

            for qt in range(17):
                if qt < 16:
                    stage_scores(qt)
                if qt >= 1:
                    stage_rest(qt - 1)
        for oc in range(NCH):
            @b.gpsimd
            def _(g):
                g.wait_ge(C.sem["LDG"], C.cnt["LDG"])
                C.w(g, "MM", "EV", "VV")
                ins = g.dma_start(scr[:, 4096:5120], C.dram["l2f_wo"][il, oc, :, :])
                C.inc(ins, "LDG", 16)
            @b.sync
            def _(s):
                s.wait_ge(C.sem["LDS"], C.cnt["LDS"])
                C.w(s, "VV", "EV")
                ins = s.dma_start(t_s[:, 0:2048], xsp2[oc * 128:(oc + 1) * 128, :])
                C.inc(ins, "LDS", 16)
            scw = scr[:, 4096:5120].rearrange("p (c j) -> p c j", c=NCH)
            for tc in range(NTC):
                sl = slice(tc * 512, (tc + 1) * 512)
                @b.tensor
                def _(t):
                    C.w(t, "LDG", "VV", "EV")
                    for ccx in range(NCH):
                        ins = t.matmul(PS[0][:, 0:512], scw[:, ccx, :], QTf[:, ccx, sl],
                                       start=(ccx == 0), stop=(ccx == NCH - 1))
                    C.inc(ins, "MM")
                @b.vector
                def _(v):
                    C.w(v, "MM", "LDS")
                    ins = v.scalar_tensor_tensor(ix[:, oc, sl], PS[0][:, 0:512], 1.0,
                                                 t_s[:, sl], op0=ALU.mult, op1=ALU.add)
                    C.inc(ins, "VV")
        C.dbg("ycat", H["AR2"][:, 0:16384])
        C.dbg("ixa", H["xa_t"][:])
        if "ffn" not in L2F_PARTS:
            continue
        @b.sync
        def _(s):
            s.wait_ge(C.sem["LDS"], C.cnt["LDS"])
            C.w(s, "VV", "EV", "MM")
            ins = s.dma_start(H["b8"][:, 32:40], C.dram["l2f_lng"][il, :, :])
            C.inc(ins, "LDS", 16)
            ins = s.dma_start(H["b8"][:, 40:48], C.dram["l2f_lnb"][il, :, :])
            C.inc(ins, "LDS", 16)
            ins = s.dma_start(H["rr"][:, 0:44], C.dram["l2f_b1"][il, :, :])
            C.inc(ins, "LDS", 16)
            ins = s.dma_start(H["b8"][:, 0:8], C.dram["l2f_b2"][il, :, :])
            C.inc(ins, "LDS", 16)
        @b.vector
        def _(v):
            C.w(v, "MM", "EV", "LDS")
            for d_ in range(NCH):
                if d_ < 4:
                    ins = v.tensor_copy(xsb[:, d_, :], ix[:, d_, :])
                else:
                    v.memset(xsb[:, d_, 0:1], 0.0)
                    ins = v.tensor_copy(xsb[:, d_, 1:2048], ix[:, d_, 0:2047])
            C.inc(ins, "VV")
        emit_ln(C, H, xsb, xsb, src_f32=False,
                gamma=H["b8"][:, 32:40], beta=H["b8"][:, 40:48])
        for half in range(2):
            tok0 = half * 1024
            for fc in range(22):
                slot = 4096 + (fc % 2) * 2048
                lsem = "LDG" if fc % 2 == 0 else "LDH"
                @b.gpsimd
                def _(g):
                    need = C.cnt["MM"] - 2 if fc >= 1 else C.cnt["MM"]
                    g.wait_ge(C.sem["MM"], max(0, need))
                    if fc == 0:
                        C.w(g, "EV", "VV")
                    g.wait_ge(C.sem[lsem], C.cnt[lsem])
                    ins = g.dma_start(scr[:, slot:slot + 1024], C.dram["l2f_w1"][il, fc, :, :])
                    C.inc(ins, lsem, 16)
                    ins = g.dma_start(scr[:, slot + 1024:slot + 2048],
                                      C.dram["l2f_w1"][il, 22 + fc, :, :])
                    C.inc(ins, lsem, 16)
                ldg_t = C.cnt[lsem]
                mm0, ev0, vv0 = C.cnt["MM"], C.cnt["EV"], C.cnt["VV"]
                scw1 = scr[:, slot:slot + 1024].rearrange("p (c j) -> p c j", c=NCH)
                scw2 = scr[:, slot + 1024:slot + 2048].rearrange("p (c j) -> p c j", c=NCH)
                @b.tensor
                def _(t):
                    t.wait_ge(C.sem[lsem], ldg_t)
                    for tc2 in range(2):
                        gi = fc * 2 + tc2
                        p = gi % 2
                        sl = slice(tok0 + tc2 * 512, tok0 + (tc2 + 1) * 512)
                        t.wait_ge(C.sem["VV"], max(vv0 - 1, vv0 + tc2 - 1))
                        for ccx in range(NCH):
                            t.matmul(PS[p][:, 0:512], scw1[:, ccx, :], xsb[:, ccx, sl],
                                     start=(ccx == 0), stop=(ccx == NCH - 1))
                        for ccx in range(NCH):
                            ins = t.matmul(PS[2 + p][:, 0:512], scw2[:, ccx, :],
                                           xsb[:, ccx, sl], start=(ccx == 0), stop=(ccx == NCH - 1))
                        C.inc(ins, "MM")
                @b.scalar
                def _(s):
                    for tc2 in range(2):
                        gi = fc * 2 + tc2
                        p = gi % 2
                        s.wait_ge(C.sem["MM"], mm0 + tc2 + 1)
                        s.wait_ge(C.sem["VV"], max(vv0 - 1, vv0 + tc2 - 1))
                        ins = s.activation(fa[:, 0:512] if p == 0 else fb[:, 0:512],
                                           PS[2 + p][:, 0:512], AF.Gelu,
                                           bias=H["rr"][:, 22 + fc:23 + fc])
                        C.inc(ins, "EV")
                @b.vector
                def _(v):
                    for tc2 in range(2):
                        gi = fc * 2 + tc2
                        p = gi % 2
                        hsl = slice(tc2 * 512, (tc2 + 1) * 512)
                        v.wait_ge(C.sem["EV"], ev0 + tc2 + 1)
                        ins = v.scalar_tensor_tensor(hAR[:, fc, hsl], PS[p][:, 0:512],
                                                     H["rr"][:, fc:fc + 1],
                                                     fa[:, 0:512] if p == 0 else fb[:, 0:512],
                                                     op0=ALU.add, op1=ALU.mult)
                        C.inc(ins, "VV")
            for dc in range(NCH):
                slot = (dc % 2) * 4096
                lsem = "LDG" if dc % 2 == 0 else "LDH"
                @b.gpsimd
                def _(g):
                    need = C.cnt["MM"] - 2 if dc >= 1 else C.cnt["MM"]
                    g.wait_ge(C.sem["MM"], max(0, need))
                    if dc == 0:
                        C.w(g, "EV", "VV")
                    g.wait_ge(C.sem[lsem], C.cnt[lsem])
                    ins = g.dma_start(scr[:, slot:slot + 2816], C.dram["l2f_w2"][il, dc, :, :])
                    C.inc(ins, lsem, 16)
                ldg_t = C.cnt[lsem]
                mm0, vv0 = C.cnt["MM"], C.cnt["VV"]
                @b.tensor
                def _(t):
                    t.wait_ge(C.sem[lsem], ldg_t)
                    for tc2 in range(2):
                        t.wait_ge(C.sem["VV"], max(vv0, vv0 + tc2 - 1))
                        for fcc in range(22):
                            ins = t.matmul(PS[4 + tc2 % 2][:, 0:512],
                                           scr[:, slot + fcc * 128:slot + (fcc + 1) * 128],
                                           hAR[:, fcc, slice(tc2 * 512, (tc2 + 1) * 512)],
                                           start=(fcc == 0), stop=(fcc == 21))
                        C.inc(ins, "MM")
                @b.vector
                def _(v):
                    for tc2 in range(2):
                        sl = slice(tok0 + tc2 * 512, tok0 + (tc2 + 1) * 512)
                        v.wait_ge(C.sem["MM"], mm0 + tc2 + 1)
                        ins = v.scalar_tensor_tensor(ix[:, dc, sl], PS[4 + tc2 % 2][:, 0:512],
                                                     H["b8"][:, dc:dc + 1], ix[:, dc, sl],
                                                     op0=ALU.add, op1=ALU.add)
                        C.inc(ins, "VV")

    C.dbg("ixf", H["xa_t"][:])
    for d_ in range(NCH):
        @b.sync
        def _(s):
            s.wait_ge(C.sem["LDS"], C.cnt["LDS"])
            C.w(s, "VV", "EV", "ST")
            ins = s.dma_start(t_s[:, 0:2048], xsp[d_ * 128:(d_ + 1) * 128, :])
            C.inc(ins, "LDS", 16)
        for tc in range(NTC):
            sl = slice(tc * 512, (tc + 1) * 512)
            @b.vector
            def _(v):
                C.w(v, "LDS", "ST")
                ins = v.scalar_tensor_tensor(fb[:, 0:512], ix[:, d_, sl], 0.5,
                                             t_s[:, sl], op0=ALU.mult, op1=ALU.add)
                C.inc(ins, "VV")
            @b.sync
            def _(s):
                C.w(s, "VV")
                ins = s.dma_start(xsp[d_ * 128:(d_ + 1) * 128, sl], fb[:, 0:512])
                C.inc(ins, "ST", 16)
    C.x_in_hbm = True



def emit_l3a(C, H, li):
    b = C.block
    xa, xn = H["xa"], H["xn"]
    PS = H["PS"]
    scr = H["scr16"]
    scrv = scr.rearrange("p (c k) -> p c k", c=NCH)
    xsp = C.dram["xspill"]
    t_s = H["t_s"]
    fa, fb = H["f512_a"], H["f512_b"]
    xa_bf = H["xa_t"][:].bitcast(BF16)
    QT = H["AR2"][:, 0:16384].rearrange("p (c t) -> p c t", c=NCH)
    KT = H["AR2"][:, 16384:32768].rearrange("p (c t) -> p c t", c=NCH)
    V = xa_bf[:, 0:16384].rearrange("p (tt c) -> p tt c", tt=16)
    KRT = xa_bf[:, 16384:32768].rearrange("p (c t) -> p c t", c=NCH)
    QRT = H["xn_t"][:].rearrange("p (c t) -> p c t", c=NCH)
    ycat = QRT

    @b.gpsimd
    def _(g):
        g.wait_ge(C.sem["LDG"], C.cnt["LDG"])
        C.w(g, "MM", "EV", "VV")
        for (dst, nm) in ((H["l3cos"][:], "c_l3cos"), (H["l3sin"][:], "c_l3sin"),
                          (H["l3perm"][:], "c_l3perm"), (H["l3mask"][:], "c_l3mask")):
            ins = g.dma_start(dst, C.dram[nm][:, :])
            C.inc(ins, "LDG", 16)
    @b.sync
    def _(s):
        s.wait_ge(C.sem["LDS"], C.cnt["LDS"])
        C.w(s, "VV", "EV")
        ins = s.dma_start(H["b8"][:, 16:24], C.dram["l3a_bo"][:, :])
        C.inc(ins, "LDS", 16)

    @b.sync
    def _(s):
        C.wall(s)
        for d_ in range(NCH):
            ins = s.dma_start(xsp[d_ * 128:(d_ + 1) * 128, :], xa[:, d_, :])
            C.inc(ins, "ST", 16)

    for sec in range(2):
        @b.gpsimd
        def _(g):
            g.wait_ge(C.sem["LDG"], C.cnt["LDG"])
            C.w(g, "MM", "EV", "VV")
            for oc in range(NCH):
                ins = g.dma_start(scrv[:, oc, :], C.dram["l3a_wqk"][sec, oc, :, :])
                C.inc(ins, "LDG", 16)
        dst = QT if sec == 0 else KT
        for oc in range(NCH):
            for tc in range(NTC):
                sl = slice(tc * 512, (tc + 1) * 512)
                @b.tensor
                def _(t):
                    C.w(t, "LDG", "EV", "VV")
                    for c_ in range(NCH):
                        ins = t.matmul(PS[tc % 2][:, 0:512],
                                       scr[:, c_ * 1024 + oc * 128:c_ * 1024 + oc * 128 + 128]
                                       if False else scrv[:, oc, c_ * 128:(c_ + 1) * 128],
                                       xn[:, c_, sl], start=(c_ == 0), stop=(c_ == NCH - 1))
                    C.inc(ins, "MM")
                @b.scalar
                def _(s):
                    C.w(s, "MM")
                    ins = s.activation(dst[:, oc, sl], PS[tc % 2][:, 0:512],
                                       AF.Identity, scale=0.125)
                    C.inc(ins, "EV")
    @b.gpsimd
    def _(g):
        g.wait_ge(C.sem["LDG"], C.cnt["LDG"])
        C.w(g, "MM", "EV", "VV")
        ins = g.dma_start(scr[:, 0:8192], C.dram["l3a_wv"][:, :])
        C.inc(ins, "LDG", 16)
    for tt in range(16):
        for cb in range(2):
            @b.tensor
            def _(t):
                C.w(t, "LDG", "VV", "EV", "ST")
                for c_ in range(NCH):
                    ins = t.matmul(PS[cb % 2][:, 0:512],
                                   xn[:, c_, tt * 128:(tt + 1) * 128],
                                   scrv[:, c_, cb * 512:(cb + 1) * 512],
                                   start=(c_ == 0), stop=(c_ == NCH - 1))
                C.inc(ins, "MM")
            @b.scalar
            def _(s):
                C.w(s, "MM", "ST")
                ins = s.copy(V[:, tt, cb * 512:(cb + 1) * 512], PS[cb % 2][:, 0:512])
                C.inc(ins, "EV")
    for (srcT, dstT) in ((QT, QRT), (KT, KRT)):
        for d_ in range(NCH):
            for tc in range(NTC):
                sl = slice(tc * 512, (tc + 1) * 512)
                @b.tensor
                def _(t):
                    C.w(t, "EV", "VV")
                    ins = t.matmul(PS[2][:, 0:512], H["l3perm"][:], srcT[:, d_, sl],
                                   start=True, stop=True)
                    C.inc(ins, "MM")
                @b.vector
                def _(v):
                    C.w(v, "MM")
                    v.tensor_tensor(fa[:, 0:512], srcT[:, d_, sl],
                                    H["l3cos"][:, sl], op=ALU.mult)
                    ins = v.tensor_tensor(fb[:, 0:512], PS[2][:, 0:512],
                                          H["l3sin"][:, sl], op=ALU.mult)
                    C.fence(v, ins, "VV")
                    ins = v.tensor_tensor(dstT[:, d_, sl], fa[:, 0:512], fb[:, 0:512],
                                          op=ALU.add)
                    C.inc(ins, "VV")
    for h in range(16):
        po, chh = (h % 2) * 64, h // 2
        for qb in range(NTC):
            qsl = slice(qb * 512, (qb + 1) * 512)
            last_kt = min(15, (qb * 512 + 511) // 128)
            st = {}
            ev_tail = C.cnt["EV"]
            vv_tail = C.cnt["VV"]

            def sc_stage(kt):
                p = kt % 2
                @b.tensor
                def _(t):
                    if kt >= 2:
                        t.wait_ge(C.sem["EV"], st[("exp", kt - 2)])
                    else:
                        t.wait_ge(C.sem["EV"], ev_tail)
                        t.wait_ge(C.sem["VV"], vv_tail)
                    ins = t.matmul(PS[2 * p][:, 0:512],
                                   KRT[po:po + 64, chh, kt * 128:(kt + 1) * 128],
                                   QRT[po:po + 64, chh, qsl], start=True, stop=True)
                    ins = t.matmul(PS[2 * p + 1][:, 0:512],
                                   KT[po:po + 64, chh, kt * 128:(kt + 1) * 128],
                                   QT[po:po + 64, chh, qsl], start=True, stop=True)
                    C.inc(ins, "MM")
                st[("sc", kt)] = C.cnt["MM"]

            def exp_stage(kt):
                p = kt % 2
                @b.scalar
                def _(s):
                    s.wait_ge(C.sem["MM"], st[("sc", kt)])
                    if kt >= 2:
                        s.wait_ge(C.sem["MM"], st[("acc", kt - 2)])
                    s.activation(scr[:, p * 512:(p + 1) * 512], PS[2 * p][:, 0:512], AF.Exp)
                    ins = s.activation(scr[:, 1024 + p * 512:1024 + (p + 1) * 512],
                                       PS[2 * p + 1][:, 0:512], AF.Exp)
                    C.inc(ins, "EV")
                st[("exp", kt)] = C.cnt["EV"]
                dv = kt * 128 - qb * 512
                @b.vector
                def _(v):
                    v.wait_ge(C.sem["EV"], st[("exp", kt)])
                    if dv >= 0:
                        ins = v.tensor_tensor(scr[:, p * 512:(p + 1) * 512],
                                              scr[:, p * 512:(p + 1) * 512],
                                              H["l3mask"][:, (dv // 128) * 512:(dv // 128 + 1) * 512],
                                              op=ALU.mult)
                    else:
                        ins = v.tensor_copy(fa[:, 8:9], fa[:, 8:9])
                    C.inc(ins, "VV")
                st[("mk", kt)] = C.cnt["VV"]

            def acc_stage(kt):
                p = kt % 2
                @b.tensor
                def _(t):
                    t.wait_ge(C.sem["VV"], st[("mk", kt)])
                    first = (kt == 0)
                    t.matmul(PS[5][0:64, 0:512], V[:, kt, h * 64:(h + 1) * 64],
                             scr[:, p * 512:(p + 1) * 512],
                             start=first, stop=(kt == last_kt))
                    ins = t.matmul(PS[6][0:1, 0:512], H["ones_bf"][:, 0:1],
                                   scr[:, 1024 + p * 512:1024 + (p + 1) * 512],
                                   start=first, stop=(kt == last_kt))
                    C.inc(ins, "MM")
                st[("acc", kt)] = C.cnt["MM"]

            for kt in range(last_kt + 2):
                if kt <= last_kt:
                    sc_stage(kt)
                if kt >= 1:
                    exp_stage(kt - 1)
                    acc_stage(kt - 1)
            ysb = scr[po:po + 64, 512:1024] if False else scr[po:po + 64, 2048:2560]
            @b.scalar
            def _(s):
                s.wait_ge(C.sem["MM"], st[("acc", last_kt)])
                s.copy(ysb, PS[5][0:64, 0:512])
                ins = s.activation(fa[0:1, 0:512], PS[6][0:1, 0:512], AF.Identity)
                C.inc(ins, "EV")
            @b.vector
            def _(v):
                C.w(v, "EV")
                ins = v.reciprocal(fb[0:1, 0:512], fa[0:1, 0:512])
                C.inc(ins, "VV")
            @b.tensor
            def _(t):
                C.w(t, "VV")
                ins = t.matmul(PS[7][:, 0:512], H["ones_f"][0:1, :], fb[0:1, 0:512],
                               start=True, stop=True)
                C.inc(ins, "MM")
            @b.vector
            def _(v):
                C.w(v, "MM")
                ins = v.tensor_tensor(ycat[po:po + 64, chh, qsl], ysb,
                                      PS[7][po:po + 64, 0:512], op=ALU.mult)
                C.inc(ins, "VV")
    for oc in range(NCH):
        @b.gpsimd
        def _(g):
            g.wait_ge(C.sem["LDG"], C.cnt["LDG"])
            C.w(g, "MM", "EV", "VV")
            ins = g.dma_start(scr[:, 4096:5120], C.dram["l3a_wo"][oc, :, :])
            C.inc(ins, "LDG", 16)
        @b.sync
        def _(s):
            s.wait_ge(C.sem["LDS"], C.cnt["LDS"])
            C.w(s, "VV", "ST")
            ins = s.dma_start(t_s[:, 0:2048], xsp[oc * 128:(oc + 1) * 128, :])
            C.inc(ins, "LDS", 16)
        scw = scr[:, 4096:5120].rearrange("p (c j) -> p c j", c=NCH)
        for tc in range(NTC):
            sl = slice(tc * 512, (tc + 1) * 512)
            @b.tensor
            def _(t):
                C.w(t, "LDG", "VV", "EV")
                for c_ in range(NCH):
                    ins = t.matmul(PS[0][:, 0:512], scw[:, c_, :], ycat[:, c_, sl],
                                   start=(c_ == 0), stop=(c_ == NCH - 1))
                C.inc(ins, "MM")
            @b.vector
            def _(v):
                C.w(v, "MM", "LDS", "ST")
                ins = v.scalar_tensor_tensor(fb[:, 0:512], PS[0][:, 0:512],
                                             H["b8"][:, 16 + oc:17 + oc],
                                             t_s[:, sl], op0=ALU.add, op1=ALU.add)
                C.inc(ins, "VV")
            @b.sync
            def _(s):
                C.w(s, "VV")
                ins = s.dma_start(xsp[oc * 128:(oc + 1) * 128, sl], fb[:, 0:512])
                C.inc(ins, "ST", 16)
    C.x_in_hbm = True

_EMITTERS = {"l1f": emit_l1f, "l0a": emit_l0a, "l2a": emit_l2a,
             "l0f": emit_l0f, "l3f": emit_l3f, "l1a": emit_l1a, "l2f": emit_l2f,
             "l3a": emit_l3a}



def build_program(sub_idxs):
    nc = bass.Bass("TRN2", target_bir_lowering=False)
    es = ExitStack()
    x_in = nc.dram_tensor("x", [D, T], F32, kind="ExternalInput").ap()
    y_out = nc.dram_tensor("y", [D, T], F32, kind="ExternalOutput").ap()
    xspill = nc.dram_tensor("xspill", [D, T], F32, kind="Internal").ap()
    xspill2 = nc.dram_tensor("xspill2", [D, T], F32, kind="Internal").ap()

    block = es.enter_context(nc.Block())
    C = Ctx(nc, es, block)

    need = []
    for i in sub_idxs:
        for wname in _SUB_WEIGHTS.get(SUBS[i], []):
            if wname not in need:
                need.append(wname)
    for wname in need:
        shape = list(_WEIGHT_SPECS[wname])
        dt = BF16 if wname in ("c_cs8fm", "c_ss8fm", "c_cifm", "c_sifm", "c_l2perm", "c_l3cos",
                               "c_l3sin", "c_l3perm", "c_l3mask") else F32
        C.dram[wname] = nc.dram_tensor(wname, shape, dt, kind="ExternalInput").ap()


    H = {}
    kinds = {SUBS[i] for i in sub_idxs}
    has_l1a = "l1a" in kinds
    has_moe = bool(kinds & {"l0f", "l3f"})
    sb = lambda name, shape, dt: es.enter_context(nc.sbuf_tensor(name, shape, dt))
    H["xa_t"] = sb("xa", [128, NCH * T], F32)
    H["xa"] = H["xa_t"][:].rearrange("p (d t) -> p d t", d=NCH)
    H["xn_t"] = sb("xn", [128, NCH * T], BF16)
    H["xn"] = H["xn_t"][:].rearrange("p (d t) -> p d t", d=NCH)
    H["AR2"] = sb("AR2", [128, 32768], BF16)[:]
    H["t_s"] = H["AR2"][:, 16384:24576].bitcast(F32)
    H["tsq_s"] = H["AR2"][:, 24576:28672]
    H["f512_a"] = sb("f512_a", [128, 512], F32)[:]
    H["f512_b"] = sb("f512_b", [128, 512], F32)[:]
    H["row_m"] = H["f512_a"][0:1, :]
    H["row_r"] = H["f512_b"][0:1, :]
    H["scr16"] = sb("scr16", [128, 8192], BF16)[:]
    H["wstage_f"] = H["scr16"][:, 0:4096].bitcast(F32)
    H["ident_bf"] = sb("ident_bf", [128, 128], BF16)[:]
    H["mask_f"] = sb("mask_f", [128, 128], F32)[:]
    H["m1_s"] = sb("m1_s", [128, 8], F32)[:]
    H["b8"] = sb("b8", [128, 48], F32)[:]
    H["ones_bf"] = sb("ones_bf", [128, 1], BF16)[:]
    H["cst"] = sb("cst", [128, 8], F32)[:]
    H["ones_f"] = sb("ones_f", [1, 128], F32)[:]
    H["ones_cf"] = sb("ones_cf", [128, 1], F32)[:]
    H["SM"] = sb("SM", [128, 12288], BF16)[:]

    def carve(off_b, part, n, dt):
        e0 = off_b // 2
        if dt == BF16:
            return H["SM"][0:part, e0:e0 + n]
        return H["SM"][0:part, e0:e0 + 2 * n].bitcast(F32)

    if has_moe:
        H["wb"] = carve(0, 128, 2048, BF16)
        H["w8b"] = carve(4096, 128, 8, BF16)
        H["w8T"] = carve(4160, 8, 2048, BF16)
        H["wrtf"] = carve(8256, 128, 64, F32)
        H["ltm"] = carve(8512, 128, 128, F32)
        H["xn32"] = carve(9024, 128, 512, F32)
        H["esel"] = carve(11072, 8, 1024, BF16)
        H["b2sb"] = carve(13120, 8, 1024, BF16)
        H["b256"] = carve(15168, 128, 256, F32)
    if has_l1a:
        H["loraA"] = carve(0, 64, 2048, BF16)
        H["loraB"] = carve(4096, 64, 2048, BF16)
        H["loraC"] = carve(8192, 32, 2048, BF16)
        H["twT"] = carve(12288, 64, 2048, BF16)
        H["tdb"] = carve(16384, 64, 1024, BF16)
        H["blf"] = carve(18432, 64, 1024, BF16)
        H["Ssb"] = carve(20480, 128, 512, F32)
        H["Sbf"] = carve(22528, 128, 64, BF16)
        H["pp"] = carve(22656, 128, 64, F32)
        H["u2"] = carve(22912, 128, 8, F32)
        H["tdmiu2"] = carve(22944, 128, 8, F32)
        H["msk2"] = carve(22976, 128, 256, F32)
    if "l3a" in kinds:
        H["l3cos"] = carve(0, 128, 2048, BF16)
        H["l3sin"] = carve(4096, 128, 2048, BF16)
        H["l3perm"] = carve(8192, 128, 128, BF16)
        H["l3mask"] = carve(8448, 128, 2048, BF16)
    if "l2f" in kinds:
        H["qt_s"] = carve(0, 128, 1024, BF16)
        H["qsq_s"] = carve(2048, 128, 1024, BF16)
        H["rr"] = carve(4096, 128, 48, F32)
        H["tabF"] = carve(4288, 128, 8192, BF16)
        H["mlow"] = carve(20672, 128, 128, F32)
        H["rqall"] = carve(21184, 128, 256, F32)
        H["slab2"] = carve(22208, 128, 768, BF16)
        H["l2perm_sb"] = carve(23744, 128, 128, BF16)
    H["PS"] = [es.enter_context(nc.psum_tensor(f"ps{i}", [128, 512], F32))[:]
               for i in range(8)]

    with es:
        @block.gpsimd
        def _(g):
            g.memset(H["ones_bf"], 1.0)
            g.memset(H["ones_f"], 1.0)
            g.memset(H["ones_cf"], 1.0)
            g.memset(H["cst"][:, 0:1], 1e-5)
            g.memset(H["cst"][:, 1:2], float(np.pi / 2))
            ins = g.memset(H["cst"][:, 2:3], 1.6e-4)
            C.inc(ins, "GG")

        @block.sync
        def _(s):
            s.wait_ge(C.sem["LDS"], C.cnt["LDS"])
            for d_ in range(NCH):
                ins = s.dma_start(H["xa"][:, d_, :], x_in[d_ * 128:(d_ + 1) * 128, :])
                C.inc(ins, "LDS", 16)

        C.dram["xspill"] = xspill
        C.dram["xspill2"] = xspill2
        for i in sub_idxs:
            kind = SUBS[i]
            ismoe = kind in ("l0f", "l3f")
            if ismoe:
                wrt_nm = "l0f_wrt" if kind == "l0f" else "l3f_wrt"
                @block.sync
                def _(s):
                    s.wait_ge(C.sem["LDS"], C.cnt["LDS"])
                    C.w(s, "MM", "EV", "VV")
                    ins = s.dma_start(H["wrtf"][:], C.dram[wrt_nm][:, :])
                    C.inc(ins, "LDS", 16)
            emit_ln(C, H, H["xa"], H["xn"], router=ismoe)
            _EMITTERS[kind](C, H, i)
            if C.x_in_hbm:
                @block.sync
                def _(s):
                    s.wait_ge(C.sem["LDS"], C.cnt["LDS"])
                    C.wall(s)
                    for d_ in range(NCH):
                        ins = s.dma_start(H["xa"][:, d_, :],
                                          xspill[d_ * 128:(d_ + 1) * 128, :])
                        C.inc(ins, "LDS", 16)
                C.x_in_hbm = False

        @block.sync
        def _(s):
            C.w(s, "VV", "EV", "MM", "GG")
            for d_ in range(NCH):
                ins = s.dma_start(y_out[d_ * 128:(d_ + 1) * 128, :], H["xa"][:, d_, :])
                C.inc(ins, "ST", 16)
            s.wait_ge(C.sem["ST"], C.cnt["ST"])
    return nc



_PROG_CACHE = {}


def _get_program(key):
    if key not in _PROG_CACHE:
        _PROG_CACHE[key] = build_program(list(key))
    return _PROG_CACHE[key]


def run_groups(consts, xs, groups):
    for grp in groups:
        need = []
        for i in grp:
            for wname in _SUB_WEIGHTS.get(SUBS[i], []):
                if wname not in need:
                    need.append(wname)
        nc = _get_program(tuple(grp))
        in_maps = []
        for c in range(8):
            m = {"x": xs[c]}
            for wname in need:
                m[wname] = consts[wname]
            in_maps.append(m)
        res = run_bass_kernel_spmd(nc, in_maps, core_ids=list(range(8)))
        xs = [np.asarray(res.results[c]["y"]) for c in range(8)]
    return xs


def kernel(**inputs):
    consts = _prep(inputs)
    for k, v in consts.items():
        _WEIGHT_SPECS[k] = v.shape
    x = inputs["L0_attn_embeddings"]
    xs = [np.ascontiguousarray(np.asarray(x[c], np.float32).T) for c in range(8)]
    xs = run_groups(consts, xs, LAUNCH_GROUPS)
    out = np.stack([xs[c].T for c in range(8)], axis=0)
    return np.ascontiguousarray(out, dtype=np.float32)


# version 18 (final)
# speedup vs baseline: 1.0459x; 1.0459x over previous
import numpy as np
import ml_dtypes
import concourse.bass as bass
import concourse.mybir as mybir
from concourse.bass_utils import run_bass_kernel_spmd
from contextlib import ExitStack

F32 = mybir.dt.float32
BF16 = mybir.dt.bfloat16
AF = mybir.ActivationFunctionType
ALU = mybir.AluOpType
AX = mybir.AxisListType

T = 2048
D = 1024
NCH = D // 128
NTC = T // 512
BF = ml_dtypes.bfloat16

DBG = []
MOE_STAGES = {"router", "spill", "experts", "final"}
MOE_LIMIT = None
L2F_DEPTH = 6
L2F_PARTS = {"attn", "ffn"}
SUBS = ["l0f", "l0a", "l0f", "l1f", "l1a", "l1f",
        "l2f", "l2a", "l2f", "l3f", "l3a", "l3f"]

LAUNCH_GROUPS = [list(range(12))]


_WEIGHT_SPECS = {}


def _prep(inputs):
    g = inputs
    out = {}
    f32 = np.float32

    def cc(name, a):
        a = np.ascontiguousarray(a, dtype=f32)
        out[name] = a
        _WEIGHT_SPECS[name] = a.shape

    wq = np.transpose(np.asarray(g["L0_attn_Wq"], f32), (1, 0, 2)).reshape(D, D)
    wk = np.transpose(np.asarray(g["L0_attn_Wk"], f32), (1, 0, 2)).reshape(D, D)
    wv = np.transpose(np.asarray(g["L0_attn_Wv"], f32), (1, 0, 2)).reshape(D, D)
    cc("l0a_wq", wq); cc("l0a_wk", wk); cc("l0a_wv", wv)
    cc("l0a_wo", g["L0_attn_Wo"])
    cc("l0a_bq", np.asarray(g["L0_attn_bq"], f32).reshape(-1).reshape(NCH, 128).T)
    cc("l0a_bk", np.asarray(g["L0_attn_bk"], f32).reshape(-1).reshape(NCH, 128).T)
    cc("l0a_bv", np.asarray(g["L0_attn_bv"], f32).reshape(-1).reshape(NCH, 128).T)
    cc("l0a_bo", np.asarray(g["L0_attn_bo"], f32).reshape(NCH, 128).T)
    ident = np.eye(128, dtype=f32)
    cc("c_ident", ident)
    negb = np.triu(np.full((128, 128), -1e9, dtype=f32), k=1)
    cc("c_causal_add", np.triu(np.full((128, 128), -1e9, f32), 1))

    w1 = np.asarray(g["L0_ffn_W1"], f32)
    img = w1.reshape(8, 8, 512, 8, 128).transpose(0, 1, 4, 3, 2)
    cc("l0f_w1", img.reshape(8, 8, 128, 8 * 512))
    w2 = np.asarray(g["L0_ffn_W2"], f32)
    img = w2.reshape(8, 8, 128, 32, 128).transpose(0, 1, 3, 4, 2)
    img = w2.reshape(8, 8, 128, 32, 128)
    img = img.transpose(0, 1, 4, 3, 2)
    cc("l0f_w2", np.ascontiguousarray(img).reshape(8, 8, 128, 32 * 128))
    cc("l0f_wrt", np.asarray(g["L0_ffn_Wr"], f32).T.reshape(8, 128, 8).transpose(1, 0, 2).reshape(128, 64))
    cc("l0f_br", np.ascontiguousarray(np.broadcast_to(np.asarray(g["L0_ffn_br"], f32)[None, :], (128, 8))))
    b1 = np.asarray(g["L0_ffn_b1"], f32)
    cc("l0f_b1", b1.reshape(8, 32, 128).transpose(2, 0, 1).reshape(128, 256))
    cc("l0f_b2", np.asarray(g["L0_ffn_b2"], f32))
    cc("l0f_g", np.asarray(g["L0_ffn_gamma"], f32).reshape(NCH, 128).T)
    cc("l0f_bt", np.asarray(g["L0_ffn_beta"], f32).reshape(NCH, 128).T)
    esel = np.zeros((8, 8, 128), f32)
    for e in range(8):
        esel[e, e, :] = 1.0
    cc("c_esel", esel.transpose(1, 0, 2).reshape(8, 8 * 128))

    w0 = np.asarray(g["L3_ffn_w0_kernel"], f32)
    img = w0.reshape(8, 8, 128, 4, 512).transpose(0, 3, 2, 1, 4)
    cc("l3f_w0", np.ascontiguousarray(img).reshape(8, 4, 128, 8 * 512))
    w1k = np.asarray(g["L3_ffn_w1_kernel"], f32)
    img = w1k.reshape(8, 8, 128, 4, 512).transpose(0, 3, 2, 1, 4)
    cc("l3f_w1", np.ascontiguousarray(img).reshape(8, 4, 128, 8 * 512))
    wo = np.asarray(g["L3_ffn_wo_kernel"], f32)
    img = wo.reshape(8, 16, 128, 8, 128).transpose(0, 3, 2, 1, 4)
    cc("l3f_wo", np.ascontiguousarray(img).reshape(8, 8, 128, 16 * 128))
    cc("l3f_wrt", np.asarray(g["L3_ffn_gate_kernel"], f32).reshape(8, 128, 8).transpose(1, 0, 2).reshape(128, 64))

    def wimg(W):
        return np.ascontiguousarray(np.asarray(W, f32).reshape(8, 128, 8, 128).transpose(0, 3, 2, 1))
    cc("l1a_wk", wimg(g["L1_attn_Wk"]).reshape(8, 128, 1024))
    cc("l1a_wr", wimg(g["L1_attn_Wr"]).reshape(8, 128, 1024))
    cc("l1a_wg", wimg(g["L1_attn_Wg"]).reshape(8, 128, 1024))
    cc("l1a_wo", wimg(g["L1_attn_Wo"]).reshape(8, 128, 1024))
    wv_ = np.asarray(g["L1_attn_Wv"], f32)
    cc("l1a_wv", np.ascontiguousarray(wv_.reshape(2, 512, 8, 128).transpose(0, 3, 2, 1)).reshape(2, 128, 4096))
    cc("l1a_alora", np.ascontiguousarray(np.asarray(g["L1_attn_A_lora"], f32).reshape(8, 128, 160)))
    cc("l1a_blora", np.ascontiguousarray(np.asarray(g["L1_attn_B_lora"], f32).transpose(1, 0, 2)).reshape(32, 5120))
    cc("l1a_tda", np.ascontiguousarray(np.asarray(g["L1_attn_td_A"], f32).reshape(8, 128, 64)))
    cc("l1a_tdb", np.asarray(g["L1_attn_td_B"], f32))
    def pp(a):
        return np.ascontiguousarray(np.asarray(a, f32).reshape(-1).reshape(NCH, 128).T)
    cc("l1a_miux", pp(g["L1_attn_miu_x"]))
    cc("l1a_lam", np.concatenate([pp(np.asarray(g["L1_attn_lambda_"], f32)[f]) for f in range(5)], axis=1))
    cc("l1a_tdmiu", pp(g["L1_attn_td_miu"]))
    cc("l1a_u", pp(g["L1_attn_u"]))
    cc("l1a_gng", pp(g["L1_attn_gn_gamma"]))
    cc("l1a_gnb", pp(g["L1_attn_gn_beta"]))
    ms = np.zeros((128, 128), f32)
    for s_ in range(128):
        ms[s_, s_ + 1:] = 1.0
    cc("c_mstrict", ms)
    idf = np.eye(128, dtype=f32)
    cc("c_identf", idf)

    wq3 = np.asarray(g["L3_attn_Wqkv"], f32)
    img = wq3.reshape(8, 128, 3, 8, 128).transpose(2, 3, 1, 0, 4)
    cc("l3a_wqk", np.ascontiguousarray(img[0:2]).reshape(2, 8, 128, 1024))
    imgv = wq3[:, 2048:3072].reshape(8, 128, 1024).transpose(1, 0, 2)
    cc("l3a_wv", np.ascontiguousarray(imgv).reshape(128, 8192))
    wo3 = np.asarray(g["L3_attn_Wout"], f32)
    img = wo3.reshape(8, 128, 8, 128).transpose(2, 1, 0, 3)
    cc("l3a_wo", np.ascontiguousarray(img).reshape(8, 128, 1024))
    cc("l3a_bo", np.asarray(g["L3_attn_bout"], f32).reshape(NCH, 128).T)
    ii = np.arange(32, dtype=np.float64)
    theta3 = np.power(10000.0, -2.0 * (ii - 1.0) / 64.0)
    ang = np.arange(T, dtype=np.float64)[:, None] * theta3[None, :]
    cosr = np.cos(ang); sinr = np.sin(ang)
    cosF = np.zeros((128, T)); sinF = np.zeros((128, T))
    for j in range(128):
        i_ = (j % 64) // 2
        cosF[j] = cosr[:, i_]; sinF[j] = sinr[:, i_]
    out["c_l3cos"] = np.ascontiguousarray(cosF.astype(BF)); _WEIGHT_SPECS["c_l3cos"] = (128, T)
    out["c_l3sin"] = np.ascontiguousarray(sinF.astype(BF)); _WEIGHT_SPECS["c_l3sin"] = (128, T)
    P64 = np.zeros((64, 64), f32)
    for i_ in range(32):
        P64[2 * i_ + 1, 2 * i_] = -1.0
        P64[2 * i_, 2 * i_ + 1] = 1.0
    Pm = np.zeros((128, 128), f32)
    Pm[0:64, 0:64] = P64; Pm[64:128, 64:128] = P64
    out["c_l3perm"] = np.ascontiguousarray(Pm.astype(BF)); _WEIGHT_SPECS["c_l3perm"] = (128, 128)
    l3m = np.zeros((128, 2048), f32)
    for dv in range(4):
        for kp in range(128):
            for qcol in range(512):
                if kp <= qcol - dv * 128:
                    l3m[kp, dv * 512 + qcol] = 1.0
    out["c_l3mask"] = np.ascontiguousarray(l3m.astype(BF)); _WEIGHT_SPECS["c_l3mask"] = (128, 2048)

    cc("l2a_wqkv", g["L2_attn_w_qkv"])
    cc("l2a_wo", g["L2_attn_w_out"])
    bqkv = np.asarray(g["L2_attn_b_qkv"], f32)
    cc("l2a_bq", bqkv[0:1024].reshape(NCH, 128).T)
    cc("l2a_bk", bqkv[1024:2048].reshape(NCH, 128).T)
    cc("l2a_bv", bqkv[2048:3072].reshape(NCH, 128).T)
    cc("l2a_bo", np.asarray(g["L2_attn_b_out"], f32).reshape(NCH, 128).T)
    cc("l2a_g", np.asarray(g["L2_attn_ln_gamma"], f32).reshape(NCH, 128).T)
    cc("l2a_b", np.asarray(g["L2_attn_ln_beta"], f32).reshape(NCH, 128).T)

    wqkv = np.asarray(g["L2_ffn_Wqkv"], f32)
    img = wqkv.reshape(6, 8, 128, 3, 1024).transpose(0, 3, 2, 1, 4)
    cc("l2f_wqkv", np.ascontiguousarray(img).reshape(6, 3, 128, 8192))
    wo2 = np.asarray(g["L2_ffn_Wo"], f32)
    img = wo2.reshape(6, 8, 128, 8, 128).transpose(0, 3, 2, 1, 4)
    cc("l2f_wo", np.ascontiguousarray(img).reshape(6, 8, 128, 1024))
    w1p = np.zeros((6, 1024, 5632), f32)
    w1in = np.asarray(g["L2_ffn_W1"], f32)
    w1p[:, :, 0:2730] = w1in[:, :, 0:2730]
    w1p[:, :, 2816:2816 + 2730] = w1in[:, :, 2730:5460]
    img = w1p.reshape(6, 8, 128, 44, 128).transpose(0, 3, 2, 1, 4)
    cc("l2f_w1", np.ascontiguousarray(img).reshape(6, 44, 128, 1024))
    w2p = np.zeros((6, 2816, 1024), f32)
    w2p[:, 0:2730, :] = np.asarray(g["L2_ffn_W2"], f32)
    img = w2p.reshape(6, 22, 128, 8, 128).transpose(0, 3, 2, 1, 4)
    cc("l2f_w2", np.ascontiguousarray(img).reshape(6, 8, 128, 2816))
    b1p = np.zeros((6, 5632), f32)
    b1in = np.asarray(g["L2_ffn_b1"], f32)
    b1p[:, 0:2730] = b1in[:, 0:2730]
    b1p[:, 2816:2816 + 2730] = b1in[:, 2730:5460]
    cc("l2f_b1", b1p.reshape(6, 44, 128).transpose(0, 2, 1))
    cc("l2f_b2", np.asarray(g["L2_ffn_b2"], f32).reshape(6, 8, 128).transpose(0, 2, 1))
    cc("l2f_lng", np.asarray(g["L2_ffn_ln_g"], f32).reshape(6, 8, 128).transpose(0, 2, 1))
    cc("l2f_lnb", np.asarray(g["L2_ffn_ln_b"], f32).reshape(6, 8, 128).transpose(0, 2, 1))
    cc("l2f_qs", np.asarray(g["L2_ffn_q_scale"], f32))
    cc("l2f_ks", np.asarray(g["L2_ffn_k_scale"], f32))
    qs_ = np.asarray(g["L2_ffn_q_scale"], f32)
    ks_ = np.asarray(g["L2_ffn_k_scale"], f32)
    qsp_ = np.concatenate([qs_[:, 32:], qs_[:, :32]], axis=1)
    ksp_ = np.concatenate([ks_[:, 32:], ks_[:, :32]], axis=1)
    sc4 = np.stack([np.tile(qs_, (1, 2)), np.tile(qsp_, (1, 2)),
                    np.tile(ks_, (1, 2)), np.tile(ksp_, (1, 2))], axis=1)
    cc("l2f_scales", sc4.reshape(6, 4, 128, 1))
    DH, WIN = 64, 256
    half = np.arange(0, DH, 2, dtype=np.float64)
    inv_freq = 1.0 / 10000.0 ** (half / DH)
    tarr = np.arange(T, dtype=np.float64)
    freqs = np.concatenate([tarr[:, None] * inv_freq[None, :]] * 2, -1)
    base_scale = (half + 0.4 * DH) / (1.4 * DH)
    power = (tarr - T // 2) / (WIN // 2)
    scale = np.concatenate([base_scale[None, :] ** power[:, None]] * 2, -1)
    cosf, sinf = np.cos(freqs), np.sin(freqs)
    def fmimg(a):
        b = np.concatenate([a.T, a.T], axis=0)
        return np.ascontiguousarray(b.astype(BF))
    out["c_cs8fm"] = fmimg(cosf * scale * 8.0); _WEIGHT_SPECS["c_cs8fm"] = (128, T)
    out["c_ss8fm"] = fmimg(sinf * scale * 8.0); _WEIGHT_SPECS["c_ss8fm"] = (128, T)
    out["c_cifm"] = fmimg(cosf / scale); _WEIGHT_SPECS["c_cifm"] = (128, T)
    out["c_sifm"] = fmimg(sinf / scale); _WEIGHT_SPECS["c_sifm"] = (128, T)
    P64h = np.zeros((64, 64), f32)
    for j_ in range(32):
        P64h[j_ + 32, j_] = -1.0
        P64h[j_, j_ + 32] = 1.0
    Pm2 = np.zeros((128, 128), f32)
    Pm2[0:64, 0:64] = P64h; Pm2[64:128, 64:128] = P64h
    out["c_l2perm"] = np.ascontiguousarray(Pm2.astype(BF)); _WEIGHT_SPECS["c_l2perm"] = (128, 128)
    cc("c_causal_low", np.tril(np.full((128, 128), -1e9, f32), -1))

    cc("l1f_theta", g["L1_ffn_theta"])
    cc("l1f_phi", g["L1_ffn_phi"])
    cc("l1f_thn", g["L1_ffn_theta_noise"])
    cc("l1f_phn", g["L1_ffn_phi_noise"])
    return out


_SUB_WEIGHTS = {
    "l1f": ["l1f_theta", "l1f_phi", "l1f_thn", "l1f_phn"],
    "l0a": ["l0a_wq", "l0a_wk", "l0a_wv", "l0a_wo", "l0a_bq", "l0a_bk", "l0a_bv",
            "l0a_bo", "c_ident", "c_causal_add"],
    "l2a": ["l2a_wqkv", "l2a_wo", "l2a_bq", "l2a_bk", "l2a_bv", "l2a_bo",
            "l2a_g", "l2a_b", "c_ident"],
    "l0f": ["l0f_w1", "l0f_w2", "l0f_wrt", "l0f_br", "l0f_b1", "l0f_b2",
            "l0f_g", "l0f_bt", "c_esel", "c_ident"],
    "l3f": ["l3f_w0", "l3f_w1", "l3f_wo", "l3f_wrt", "c_esel", "c_ident"],
    "l3a": ["l3a_wqk", "l3a_wv", "l3a_wo", "l3a_bo", "c_l3cos", "c_l3sin",
            "c_l3perm", "c_l3mask"],
    "l2f": ["l2f_wqkv", "l2f_wo", "l2f_w1", "l2f_w2", "l2f_b1", "l2f_b2",
            "l2f_lng", "l2f_lnb", "l2f_qs", "l2f_ks", "l2f_scales",
            "c_cs8fm", "c_ss8fm", "c_cifm", "c_sifm",
            "c_l2perm", "c_causal_low", "c_causal_add", "c_ident"],
    "l1a": ["l1a_wk", "l1a_wr", "l1a_wg", "l1a_wo", "l1a_wv", "l1a_alora",
            "l1a_blora", "l1a_tda", "l1a_tdb", "l1a_miux", "l1a_lam", "l1a_tdmiu",
            "l1a_u", "l1a_gng", "l1a_gnb", "c_mstrict", "c_identf", "c_ident"],
}


SEMS = ["LDS", "LDG", "LDH", "MM", "EV", "VV", "GG", "ST"]


class Ctx:
    def __init__(self, nc, es, block):
        self.nc, self.es, self.block = nc, es, block
        self.cnt = {s: 0 for s in SEMS}
        self.sem = {}
        for s in SEMS:
            self.sem[s] = es.enter_context(nc.semaphore(s))
        self.dram = {}
        self.x_in_hbm = False

    def inc(self, instr, name, n=1):
        instr.then_inc(self.sem[name], n)
        self.cnt[name] += n
        return self.cnt[name]

    def dbg(self, tag, ap):
        if tag not in DBG:
            return
        b = self.block
        nc = self.nc
        dt = ap.dtype
        dout = nc.dram_tensor("dbg_" + tag, [ap.shape[0], ap.shape[1]], dt,
                              kind="ExternalOutput").ap()
        @b.sync
        def _(s):
            self.w(s, "MM", "EV", "VV", "GG")
            ins = s.dma_start(dout[:, :], ap)
            self.inc(ins, "ST", 16)
        tgt = self.cnt["ST"]
        for eng in ("tensor", "scalar", "vector", "gpsimd"):
            @getattr(b, eng)
            def _(e):
                e.wait_ge(self.sem["ST"], tgt)

    def w(self, e, *names):
        for n in names:
            if self.cnt[n] > 0:
                e.wait_ge(self.sem[n], self.cnt[n])

    def wall(self, e):
        self.w(e, *SEMS)

    def fence(self, e, ins, name):
        self.inc(ins, name)
        e.wait_ge(self.sem[name], self.cnt[name])



def emit_ln(C, H, xa, xn, eps_col=0, src_f32=True, gamma=None, beta=None, router=False):
    b = C.block
    ps_stat, ps_m, ps_r = H["PS"][5], H["PS"][6], H["PS"][7]
    t_s, tsq_s = H["t_s"], H["tsq_s"]
    row_m, row_r = H["row_m"], H["row_r"]
    ones_mean = H["ones_cf"] if src_f32 else H["ones_bf"]

    for tc in range(NTC):
        sl = slice(tc * 512, (tc + 1) * 512)

        @b.tensor
        def _(t):
            C.wall(t)
            for d_ in range(NCH):
                ins = t.matmul(ps_stat[0:1, 0:512], ones_mean[:, 0:1], xa[:, d_, sl],
                               start=(d_ == 0), stop=(d_ == NCH - 1))
            C.inc(ins, "MM")

        @b.scalar
        def _(s):
            C.w(s, "MM")
            ins = s.activation(row_m[0:1, 0:512], ps_stat[0:1, 0:512], AF.Copy,
                               bias=0.0, scale=1.0 / D)
            C.inc(ins, "EV")

        @b.tensor
        def _(t):
            C.w(t, "EV", "VV")
            ins = t.matmul(ps_m[:, 0:512], H["ones_f"][0:1, :], row_m[0:1, 0:512],
                           start=True, stop=True)
            C.inc(ins, "MM")

        @b.vector
        def _(v):
            C.w(v, "MM")
            for d_ in range(NCH):
                ins = v.tensor_tensor(t_s[:, d_ * 512:(d_ + 1) * 512], xa[:, d_, sl],
                                      ps_m[:, 0:512], op=ALU.subtract)
            C.inc(ins, "VV")

        @b.gpsimd
        def _(gp):
            C.w(gp, "VV")
            for d_ in range(NCH):
                ins = gp.tensor_tensor(tsq_s[:, d_ * 512:(d_ + 1) * 512],
                                       t_s[:, d_ * 512:(d_ + 1) * 512],
                                       t_s[:, d_ * 512:(d_ + 1) * 512], op=ALU.mult)
            C.inc(ins, "GG")

        @b.tensor
        def _(t):
            C.w(t, "GG", "EV")
            for d_ in range(NCH):
                ins = t.matmul(ps_stat[0:1, 0:512], H["ones_bf"][:, 0:1],
                               tsq_s[:, d_ * 512:(d_ + 1) * 512],
                               start=(d_ == 0), stop=(d_ == NCH - 1))
            C.inc(ins, "MM")

        @b.scalar
        def _(s):
            C.w(s, "MM")
            ins = s.activation(row_r[0:1, 0:512], ps_stat[0:1, 0:512], AF.Ln,
                               bias=H["cst"][0:1, eps_col:eps_col + 1], scale=1.0 / D)
            C.fence(s, ins, "EV")
            ins = s.activation(row_r[0:1, 0:512], row_r[0:1, 0:512], AF.Exp,
                               bias=0.0, scale=-0.5)
            C.inc(ins, "EV")

        @b.tensor
        def _(t):
            C.w(t, "EV", "VV")
            ins = t.matmul(ps_r[:, 0:512], H["ones_f"][0:1, :], row_r[0:1, 0:512],
                           start=True, stop=True)
            C.inc(ins, "MM")

        @b.vector
        def _(v):
            C.w(v, "MM")
            for d_ in range(NCH):
                if gamma is None:
                    ins = v.tensor_tensor(xn[:, d_, sl], t_s[:, d_ * 512:(d_ + 1) * 512],
                                          ps_r[:, 0:512], op=ALU.mult)
                else:
                    v.tensor_tensor(t_s[:, d_ * 512:(d_ + 1) * 512],
                                    t_s[:, d_ * 512:(d_ + 1) * 512],
                                    ps_r[:, 0:512], op=ALU.mult)
                    ins = v.tensor_scalar(xn[:, d_, sl], t_s[:, d_ * 512:(d_ + 1) * 512],
                                          gamma[:, d_:d_ + 1], beta[:, d_:d_ + 1],
                                          op0=ALU.mult, op1=ALU.add)
            C.inc(ins, "VV")

        if router:
            for d_ in range(NCH):
                @b.vector
                def _(v):
                    C.w(v, "MM")
                    ins = v.tensor_tensor(H["xn32"][:, 0:512], t_s[:, d_ * 512:(d_ + 1) * 512],
                                          ps_r[:, 0:512], op=ALU.mult)
                    C.inc(ins, "VV")
                @b.tensor
                def _(t):
                    C.w(t, "VV", "EV")
                    for tt in range(4):
                        ins = t.matmul(H["PS"][tt][:, 0:8],
                                       H["xn32"][:, tt * 128:(tt + 1) * 128],
                                       H["wrtf"][:, d_ * 8:(d_ + 1) * 8],
                                       start=(d_ == 0), stop=(d_ == NCH - 1))
                    C.inc(ins, "MM")
            @b.scalar
            def _(s):
                C.w(s, "MM")
                for tt in range(4):
                    ins = s.copy(H["ltm"][:, tc * 32 + tt * 8: tc * 32 + tt * 8 + 8],
                                 H["PS"][tt][:, 0:8])
                C.inc(ins, "EV")


def emit_l1f(C, H, li):
    b = C.block
    xa, xn = H["xa"], H["xn"]
    th, phi_, thn, phn = (C.dram["l1f_theta"], C.dram["l1f_phi"],
                          C.dram["l1f_thn"], C.dram["l1f_phn"])
    stage = H["wstage_f"]
    ct = H["AR2"][:, 0:8192].rearrange("p (d o) -> p d o", d=NCH)
    st = H["AR2"][:, 8192:16384].rearrange("p (d o) -> p d o", d=NCH)
    PS = H["PS"]
    ps_a, ps_b = PS[0], PS[1]
    f1, f2 = H["f512_a"], H["f512_b"]

    for d_ in range(NCH):
        for (mat, noise, dst, bias) in ((th, thn, ct, H["cst"][:, 1:2]),
                                        (phi_, phn, st, 0.0)):
            @b.sync
            def _(s):
                s.wait_ge(C.sem["LDS"], C.cnt["LDS"])
                C.w(s, "EV", "MM", "VV")
                ins = s.dma_start(stage[:, 0:1024], mat[d_ * 128:(d_ + 1) * 128, :])
                C.inc(ins, "LDS", 16)
                ins = s.dma_start(stage[:, 1024:2048], noise[d_ * 128:(d_ + 1) * 128, :])
                C.inc(ins, "LDS", 16)

            @b.vector
            def _(v):
                C.w(v, "LDS")
                ins = v.tensor_tensor(stage[:, 0:1024], stage[:, 0:1024],
                                      stage[:, 1024:2048], op=ALU.add)
                C.inc(ins, "VV")

            @b.scalar
            def _(s):
                C.w(s, "VV")
                ins = s.activation(dst[:, d_, :], stage[:, 0:1024], AF.Sin,
                                   bias=bias, scale=1.0)
                C.inc(ins, "EV")

    for oc in range(NCH):
        for tc in range(NTC):
            sl = slice(tc * 512, (tc + 1) * 512)

            @b.tensor
            def _(t):
                C.w(t, "EV", "VV")
                for d_ in range(NCH):
                    t.matmul(ps_a[:, 0:512], ct[:, d_, oc * 128:(oc + 1) * 128],
                             xn[:, d_, sl], start=(d_ == 0), stop=(d_ == NCH - 1))
                for d_ in range(NCH):
                    ins = t.matmul(ps_b[:, 0:512], st[:, d_, oc * 128:(oc + 1) * 128],
                                   xn[:, d_, sl], start=(d_ == 0), stop=(d_ == NCH - 1))
                C.inc(ins, "MM")

            @b.scalar
            def _(s):
                C.w(s, "MM")
                s.activation(f1[:, 0:512], ps_a[:, 0:512], AF.Square)
                ins = s.activation(f2[:, 0:512], ps_b[:, 0:512], AF.Square)
                C.inc(ins, "EV")

            @b.vector
            def _(v):
                C.w(v, "EV")
                ins = v.tensor_tensor(f1[:, 0:512], f1[:, 0:512], f2[:, 0:512],
                                      op=ALU.add)
                C.inc(ins, "VV")

            @b.scalar
            def _(s):
                C.w(s, "VV")
                ins = s.activation(f2[:, 0:512], f1[:, 0:512], AF.Sqrt,
                                   bias=0.0, scale=0.25)
                C.inc(ins, "EV")

            @b.vector
            def _(v):
                C.w(v, "EV")
                ins = v.tensor_tensor(xa[:, oc, sl], xa[:, oc, sl], f2[:, 0:512],
                                      op=ALU.add)
                C.inc(ins, "VV")


def _load_w_full(C, dst, dram_ap, col0=0, rows=D):
    b = C.block
    nch = rows // 128
    @b.gpsimd
    def _(g):
        g.wait_ge(C.sem["LDG"], C.cnt["LDG"])
        C.w(g, "MM", "EV", "VV")
        for c_ in range(nch):
            ins = g.dma_start(dst[:, c_ * 1024:(c_ + 1) * 1024],
                              dram_ap[c_ * 128:(c_ + 1) * 128, col0:col0 + 1024])
            C.inc(ins, "LDG", 16)


def emit_mha(C, H, wq, wk, wv, wo, causal, src_xn=None):
    b = C.block
    xa, xn = H["xa"], (src_xn if src_xn is not None else H["xn"])
    PS = H["PS"]
    scr = H["scr16"]
    scrv = scr.rearrange("p (c k) -> p c k", c=NCH)
    QT = H["AR2"][:, 0:16384].rearrange("p (c t) -> p c t", c=NCH)
    KT = H["AR2"][:, 16384:32768].rearrange("p (c t) -> p c t", c=NCH)
    xa_bf = H["xa_t"][:].bitcast(BF16)
    V = xa_bf[:, 0:16384].rearrange("p (tt c) -> p tt c", tt=16)
    yT = xa_bf[:, 16384:32768].rearrange("p (c t) -> p c t", c=NCH)
    xsp = C.dram["xspill"]
    t_s = H["t_s"]
    NH = 16

    @b.sync
    def _(s):
        C.wall(s)
        for d_ in range(NCH):
            ins = s.dma_start(xsp[d_ * 128:(d_ + 1) * 128, :], xa[:, d_, :])
            C.inc(ins, "ST", 16)

    @b.gpsimd
    def _(g):
        g.wait_ge(C.sem["LDG"], C.cnt["LDG"])
        C.w(g, "MM", "EV", "VV")
        ins = g.dma_start(H["ident_bf"][:], C.dram["c_ident"][:, :])
        C.inc(ins, "LDG", 16)
        if causal:
            ins = g.dma_start(H["mask_f"][:], C.dram["c_causal_add"][:, :])
            C.inc(ins, "LDG", 16)
    @b.vector
    def _(v):
        C.w(v, "LDS")
        ins = v.tensor_scalar(H["m1_s"][:, 0:8], H["b8"][:, 0:8], 0.125, None,
                              op0=ALU.mult)
        C.inc(ins, "VV")

    for (wspec, kind) in ((wq, "q"), (wk, "k"), (wv, "v")):
        _load_w_full(C, scr, C.dram[wspec[0]], col0=wspec[1])
        if kind in ("q", "k"):
            dst = QT if kind == "q" else KT
            for oc in range(NCH):
                for tc in range(NTC):
                    sl = slice(tc * 512, (tc + 1) * 512)
                    @b.tensor
                    def _(t):
                        C.w(t, "LDG", "EV", "VV")
                        for c_ in range(NCH):
                            ins = t.matmul(PS[tc % 2][:, 0:512],
                                           scr[:, c_ * 1024 + oc * 128: c_ * 1024 + oc * 128 + 128],
                                           xn[:, c_, sl],
                                           start=(c_ == 0), stop=(c_ == NCH - 1))
                        C.inc(ins, "MM")
                    @b.scalar
                    def _(s):
                        C.w(s, "MM")
                        if kind == "q":
                            ins = s.activation(dst[:, oc, sl], PS[tc % 2][:, 0:512],
                                               AF.Identity, bias=H["m1_s"][:, oc:oc + 1],
                                               scale=0.125)
                        else:
                            ins = s.activation(dst[:, oc, sl], PS[tc % 2][:, 0:512],
                                               AF.Identity, bias=H["b8"][:, 8 + oc:9 + oc],
                                               scale=1.0)
                        C.inc(ins, "EV")
        else:
            for tt in range(16):
                for cb in range(2):
                    @b.tensor
                    def _(t):
                        C.w(t, "LDG", "VV", "EV", "ST")
                        for c_ in range(NCH):
                            ins = t.matmul(PS[cb % 2][:, 0:512],
                                           xn[:, c_, tt * 128:(tt + 1) * 128],
                                           scrv[:, c_, cb * 512:(cb + 1) * 512],
                                           start=(c_ == 0), stop=(c_ == NCH - 1))
                        C.inc(ins, "MM")
                    @b.scalar
                    def _(s):
                        C.w(s, "MM")
                        ins = s.copy(V[:, tt, cb * 512:(cb + 1) * 512], PS[cb % 2][:, 0:512])
                        C.inc(ins, "EV")

    C.dbg("qt", QT.tensor.ap() if False else H["AR2"][:, 0:16384])
    C.dbg("kt", H["AR2"][:, 16384:32768])
    C.dbg("v", xa_bf[:, 0:16384])
    attn = scr[:, 0:2048]
    slab = scr[:, 2048:4096]
    ps4b = PS[4].bitcast(BF16)
    prev_exp = [0]
    for h in range(NH):
        po = (h % 2) * 64
        ch = h // 2
        st = {}
        ev0h = [0]

        def exp_count(qt):
            c = ev0h[0]
            for q in range(qt):
                nkq = (q + 1) if causal else 16
                c += 2 + (nkq + 7) // 8
            return c + 1

        def stage_scores(qt):
            nk = (qt + 1) if causal else 16
            nkb = (nk * 128 + 511) // 512
            ev_guard = C.cnt["EV"]
            @b.tensor
            def _(t):
                if qt >= 1:
                    t.wait_ge(C.sem["EV"], exp_count(qt - 1))
                elif h == 0:
                    t.wait_ge(C.sem["EV"], ev_guard)
                    C.w(t, "VV")
                else:
                    t.wait_ge(C.sem["EV"], prev_exp[0])
                for kb in range(nkb):
                    N = min(512, nk * 128 - kb * 512)
                    ins = t.matmul(PS[kb][:, 0:N],
                                   QT[po:po + 64, ch, qt * 128:(qt + 1) * 128],
                                   KT[po:po + 64, ch, kb * 512:kb * 512 + N],
                                   start=True, stop=True)
                C.inc(ins, "MM")
            st[("sc", qt)] = C.cnt["MM"]

        def stage_rest(qt):
            nk = (qt + 1) if causal else 16
            nkb = (nk * 128 + 511) // 512
            kbd, off = (qt * 128) // 512, (qt * 128) % 512
            @b.vector
            def _(v):
                v.wait_ge(C.sem["MM"], st[("sc", qt)])
                if causal:
                    ins = v.tensor_tensor(PS[kbd][:, off:off + 128],
                                          PS[kbd][:, off:off + 128],
                                          H["mask_f"][:], op=ALU.add)
                    C.fence(v, ins, "VV")
                for kb in range(nkb):
                    N = min(512, nk * 128 - kb * 512)
                    ins = v.reduce_max(H["m1_s"][:, kb:kb + 1], PS[kb][:, 0:N],
                                       axis=AX.X, negate=True)
                C.fence(v, ins, "VV")
                ins = v.tensor_reduce(H["m1_s"][:, 4:5], H["m1_s"][:, 0:nkb],
                                      axis=AX.X, op=ALU.min)
                C.inc(ins, "VV")
            st[("mx", qt)] = C.cnt["VV"]
            @b.scalar
            def _(s):
                s.wait_ge(C.sem["VV"], st[("mx", qt)])
                for kb in range(nkb):
                    N = min(512, nk * 128 - kb * 512)
                    ins = s.activation(attn[:, kb * 512:kb * 512 + N], PS[kb][:, 0:N],
                                       AF.Exp, bias=H["m1_s"][:, 4:5],
                                       accum_out=H["f512_b"][:, kb:kb + 1])
                C.inc(ins, "EV")
            st[("exp", qt)] = C.cnt["EV"]
            @b.vector
            def _(v):
                v.wait_ge(C.sem["EV"], st[("exp", qt)])
                ins = v.tensor_reduce(H["f512_b"][:, 4:5], H["f512_b"][:, 0:nkb],
                                      axis=AX.X, op=ALU.add)
                C.fence(v, ins, "VV")
                ins = v.reciprocal(H["f512_b"][:, 5:6], H["f512_b"][:, 4:5])
                C.fence(v, ins, "VV")
                ins = v.tensor_scalar(attn[:, 0:nk * 128], attn[:, 0:nk * 128],
                                      H["f512_b"][:, 5:6], None, op0=ALU.mult)
                C.inc(ins, "VV")
            st[("nm", qt)] = C.cnt["VV"]
            done = 0
            while done < nk:
                grp = min(8, nk - done)
                @b.tensor
                def _(t):
                    t.wait_ge(C.sem["VV"], st[("nm", qt)])
                    C.w(t, "EV")
                    for j in range(grp):
                        kt = done + j
                        ins = t.transpose(ps4b[:, j * 128:(j + 1) * 128],
                                          attn[:, kt * 128:(kt + 1) * 128],
                                          H["ident_bf"][:])
                    C.inc(ins, "MM")
                mmtp = C.cnt["MM"]
                @b.scalar
                def _(s):
                    s.wait_ge(C.sem["MM"], mmtp)
                    ins = s.copy(slab[:, done * 128:(done + grp) * 128],
                                 ps4b[:, 0:grp * 128])
                    C.inc(ins, "EV")
                done += grp
            st[("slab", qt)] = C.cnt["EV"]
            @b.tensor
            def _(t):
                t.wait_ge(C.sem["EV"], st[("slab", qt)])
                for kt in range(nk):
                    ins = t.matmul(PS[5][0:64, 0:128], V[:, kt, h * 64:(h + 1) * 64],
                                   slab[:, kt * 128:(kt + 1) * 128],
                                   start=(kt == 0), stop=(kt == nk - 1))
                C.inc(ins, "MM")
            st[("yt", qt)] = C.cnt["MM"]
            @b.scalar
            def _(s):
                s.wait_ge(C.sem["MM"], st[("yt", qt)])
                ins = s.activation(yT[po:po + 64, ch, qt * 128:(qt + 1) * 128],
                                   PS[5][0:64, 0:128], AF.Identity,
                                   bias=H["b8"][po:po + 64, 24 + ch:25 + ch],
                                   scale=1.0)
                C.inc(ins, "EV")
            st[("yev", qt)] = C.cnt["EV"]

        ev0h[0] = C.cnt["EV"]
        for qt in range(17):
            if qt < 16:
                stage_scores(qt)
            if qt >= 1:
                stage_rest(qt - 1)
        assert st[("exp", 15)] == exp_count(15), (st[("exp", 15)], exp_count(15))
        prev_exp[0] = st[("exp", 15)]

    _load_w_full(C, scr, C.dram[wo[0]], col0=wo[1])
    for oc in range(NCH):
        @b.sync
        def _(s):
            s.wait_ge(C.sem["LDS"], C.cnt["LDS"])
            C.w(s, "VV", "ST")
            ins = s.dma_start(t_s[:, 0:2048], xsp[oc * 128:(oc + 1) * 128, :])
            C.inc(ins, "LDS", 16)
        for tc in range(NTC):
            sl = slice(tc * 512, (tc + 1) * 512)
            @b.tensor
            def _(t):
                C.w(t, "LDG", "VV", "EV")
                for c_ in range(NCH):
                    ins = t.matmul(PS[tc % 2][:, 0:512],
                                   scr[:, c_ * 1024 + oc * 128: c_ * 1024 + oc * 128 + 128],
                                   yT[:, c_, sl], start=(c_ == 0), stop=(c_ == NCH - 1))
                C.inc(ins, "MM")
            @b.vector
            def _(v):
                C.w(v, "MM", "LDS", "ST")
                ins = v.scalar_tensor_tensor(H["f512_b"][:, 0:512], PS[tc % 2][:, 0:512],
                                             H["b8"][:, 16 + oc:17 + oc],
                                             t_s[:, sl], op0=ALU.add, op1=ALU.add)
                C.inc(ins, "VV")
            @b.sync
            def _(s):
                C.w(s, "VV")
                ins = s.dma_start(xsp[oc * 128:(oc + 1) * 128, sl], H["f512_b"][:, 0:512])
                C.inc(ins, "ST", 16)
    C.x_in_hbm = True



def _load_b8(C, H, names):
    b = C.block
    @b.sync
    def _(s):
        s.wait_ge(C.sem["LDS"], C.cnt["LDS"])
        C.w(s, "VV", "EV")
        for (nm, c0) in names:
            ins = s.dma_start(H["b8"][:, c0:c0 + 8], C.dram[nm][:, :])
            C.inc(ins, "LDS", 16)


def emit_l0a(C, H, li):
    _load_b8(C, H, [("l0a_bq", 0), ("l0a_bk", 8), ("l0a_bo", 16), ("l0a_bv", 24)])
    emit_mha(C, H, ("l0a_wq", 0), ("l0a_wk", 0), ("l0a_wv", 0), ("l0a_wo", 0),
             causal=True)


def emit_l2a(C, H, li):
    b = C.block
    xa, xn = H["xa"], H["xn"]
    _load_b8(C, H, [("l2a_bq", 0), ("l2a_bk", 8), ("l2a_bo", 16), ("l2a_bv", 24),
                    ("l2a_g", 32), ("l2a_b", 40)])
    @b.vector
    def _(v):
        C.w(v, "LDS", "EV", "MM", "GG")
        for d_ in range(NCH):
            ins = v.tensor_tensor(xa[:, d_, :], xa[:, d_, :], xn[:, d_, :], op=ALU.add)
        C.inc(ins, "VV")
    emit_ln(C, H, xn, xn, src_f32=False,
            gamma=H["b8"][:, 32:40], beta=H["b8"][:, 40:48])
    emit_mha(C, H, ("l2a_wqkv", 0), ("l2a_wqkv", 1024), ("l2a_wqkv", 2048),
             ("l2a_wo", 0), causal=False)




def emit_moe(C, H, kind, li):
    b = C.block
    xa, xn = H["xa"], H["xn"]
    PS = H["PS"]
    scr = H["scr16"]
    xsp = C.dram["xspill"]
    t_s = H["t_s"]
    is0 = kind == "l0f"
    fa, fb = H["f512_a"], H["f512_b"]
    E = 8
    NFC = 8 if is0 else 4
    NFCC = 32 if is0 else 16
    halves = 2 if is0 else 1
    TK = T // halves
    AR2h = H["AR2"].rearrange("p (f t) -> p f t", f=NFCC)
    wrt_d = C.dram["l0f_wrt" if is0 else "l3f_wrt"]
    w1_d = C.dram["l0f_w1" if is0 else "l3f_w0"]
    w1b_d = None if is0 else C.dram["l3f_w1"]
    w2_d = C.dram["l0f_w2" if is0 else "l3f_wo"]
    ps4b = PS[4].bitcast(BF16)

    @b.gpsimd
    def _(g):
        g.wait_ge(C.sem["LDG"], C.cnt["LDG"])
        C.w(g, "MM", "EV", "VV")
        ins = g.dma_start(H["esel"][:], C.dram["c_esel"][:, :])
        C.inc(ins, "LDG", 16)
        ins = g.dma_start(H["ident_bf"][:], C.dram["c_ident"][:, :])
        C.inc(ins, "LDG", 16)
        if is0:
            ins = g.dma_start(H["b2sb"][:], C.dram["l0f_b2"][:, :])
            C.inc(ins, "LDG", 16)
    @b.sync
    def _(s):
        s.wait_ge(C.sem["LDS"], C.cnt["LDS"])
        C.w(s, "VV", "EV")
        if is0:
            ins = s.dma_start(H["b8"][:, 0:8], C.dram["l0f_br"][:, :])
            C.inc(ins, "LDS", 16)
            ins = s.dma_start(H["b256"][:], C.dram["l0f_b1"][:, :])
            C.inc(ins, "LDS", 16)
            ins = s.dma_start(H["b8"][:, 32:40], C.dram["l0f_g"][:, :])
            C.inc(ins, "LDS", 16)
            ins = s.dma_start(H["b8"][:, 40:48], C.dram["l0f_bt"][:, :])
            C.inc(ins, "LDS", 16)
    if is0:
        @b.vector
        def _(v):
            C.w(v, "LDS")
            v.tensor_scalar(H["b8"][:, 16:24], H["b8"][:, 32:40], 0.5, None, op0=ALU.mult)
            ins = v.tensor_scalar(H["b8"][:, 24:32], H["b8"][:, 40:48], 0.5, None, op0=ALU.mult)
            C.inc(ins, "VV")

    for tt in range(16 if "router" in MOE_STAGES else 0):
        @b.vector
        def _(v):
            C.w(v, "LDS", "LDG", "EV")
            l8 = H["f512_b"][:, 0:8]
            if is0:
                ins = v.tensor_tensor(l8, H["ltm"][:, tt * 8:(tt + 1) * 8],
                                      H["b8"][:, 0:8], op=ALU.add)
            else:
                ins = v.tensor_copy(l8, H["ltm"][:, tt * 8:(tt + 1) * 8])
            C.fence(v, ins, "VV")
            m1 = H["f512_b"][:, 8:9]
            ins = v.tensor_reduce(m1, l8, axis=AX.X, op=ALU.max)
            C.fence(v, ins, "VV")
            eq1 = H["f512_b"][:, 16:24]
            ins = v.tensor_scalar(eq1, l8, m1, None, op0=ALU.is_equal)
            C.fence(v, ins, "VV")
            msk = H["f512_b"][:, 24:32]
            ins = v.scalar_tensor_tensor(msk, eq1, -1e30, l8, op0=ALU.mult, op1=ALU.add)
            C.fence(v, ins, "VV")
            m2 = H["f512_b"][:, 9:10]
            ins = v.tensor_reduce(m2, msk, axis=AX.X, op=ALU.max)
            C.fence(v, ins, "VV")
            d12 = H["f512_b"][:, 10:11]
            ins = v.tensor_scalar(d12, m1, m2, None, op0=ALU.subtract)
            C.inc(ins, "VV")
        @b.scalar
        def _(s):
            C.w(s, "VV")
            s.activation(H["f512_b"][:, 11:12], H["f512_b"][:, 10:11], AF.Sigmoid)
            ins = s.activation(H["f512_b"][:, 12:13], H["f512_b"][:, 10:11], AF.Sigmoid,
                               bias=0.0, scale=-1.0)
            C.inc(ins, "EV")
        @b.vector
        def _(v):
            C.w(v, "EV")
            eq1 = H["f512_b"][:, 16:24]
            msk = H["f512_b"][:, 24:32]
            eq2 = H["f512_b"][:, 32:40]
            ins = v.tensor_scalar(eq2, msk, H["f512_b"][:, 9:10], None, op0=ALU.is_equal)
            C.fence(v, ins, "VV")
            t8 = H["f512_b"][:, 40:48]
            ins = v.tensor_scalar(t8, eq1, H["f512_b"][:, 11:12], None, op0=ALU.mult)
            C.fence(v, ins, "VV")
            ins = v.scalar_tensor_tensor(H["w8b"][:], eq2, H["f512_b"][:, 12:13], t8,
                                         op0=ALU.mult, op1=ALU.add)
            C.inc(ins, "VV")
        @b.tensor
        def _(t):
            C.w(t, "VV", "EV")
            ins = t.transpose(ps4b[0:8, 0:128], H["w8b"][:], H["ident_bf"][:])
            C.inc(ins, "MM")
        @b.scalar
        def _(s):
            C.w(s, "MM")
            ins = s.copy(H["w8T"][0:8, tt * 128:(tt + 1) * 128], ps4b[0:8, 0:128])
            C.inc(ins, "EV")

    @b.sync
    def _(s):
        C.wall(s)
        for d_ in range(NCH):
            ins = s.dma_start(xsp[d_ * 128:(d_ + 1) * 128, :], xa[:, d_, :])
            C.inc(ins, "ST", 16)
    C.dbg("w8t", H["w8T"][0:8, :])
    if is0 and "spill" in MOE_STAGES:
        for dc in range(NCH):
            for tc in range(NTC):
                sl = slice(tc * 512, (tc + 1) * 512)
                @b.tensor
                def _(t):
                    C.w(t, "EV", "VV", "ST")
                    ins = t.matmul(PS[2][:, 0:512], H["b2sb"][0:8, dc * 128:(dc + 1) * 128],
                                   H["w8T"][0:8, sl], start=True, stop=True)
                    C.inc(ins, "MM")
                @b.vector
                def _(v):
                    C.w(v, "MM", "ST")
                    ins = v.scalar_tensor_tensor(xa[:, dc, sl], PS[2][:, 0:512], 1.0,
                                                 xn[:, dc, sl], op0=ALU.mult, op1=ALU.add)
                    C.inc(ins, "VV")

    if "experts" not in MOE_STAGES:
        halves = 0
    for half in range(halves):
        tok0 = half * TK
        for e in range(E if MOE_LIMIT is None else MOE_LIMIT[0]):
            for nb in range(TK // 512):
                sl = slice(tok0 + nb * 512, tok0 + (nb + 1) * 512)
                @b.tensor
                def _(t):
                    C.w(t, "EV", "VV")
                    ins = t.matmul(PS[3][:, 0:512], H["esel"][0:8, e * 128:(e + 1) * 128],
                                   H["w8T"][0:8, sl], start=True, stop=True)
                    C.inc(ins, "MM")
                @b.scalar
                def _(s):
                    C.w(s, "MM")
                    ins = s.copy(H["wb"][:, sl], PS[3][:, 0:512])
                    C.inc(ins, "EV")
            wbcur = H["wb"]
            NG = 4 * (TK // 512)
            for fc in range(NFC if MOE_LIMIT is None else min(NFC, MOE_LIMIT[1])):
                slot = fc % 2 if is0 else 0
                lsem = "LDG" if slot == 0 else "LDH"
                @b.gpsimd
                def _(g):
                    if is0:
                        need = C.cnt["MM"] - NG if fc >= 1 else C.cnt["MM"]
                        g.wait_ge(C.sem["MM"], max(0, need))
                    else:
                        C.w(g, "MM")
                    g.wait_ge(C.sem[lsem], C.cnt[lsem])
                    ins = g.dma_start(scr[:, slot * 4096:(slot + 1) * 4096],
                                      w1_d[e, fc, :, :])
                    C.inc(ins, lsem, 16)
                    if not is0:
                        ins = g.dma_start(scr[:, 4096:8192], w1b_d[e, fc, :, :])
                        C.inc(ins, lsem, 16)
                ldg_t = C.cnt[lsem]
                mm0 = C.cnt["MM"]
                ev0 = C.cnt["EV"]
                vv0 = C.cnt["VV"]
                @b.tensor
                def _(t):
                    t.wait_ge(C.sem[lsem], ldg_t)
                    gi = 0
                    for fsub in range(4):
                        for tc2 in range(TK // 512):
                            sl = slice(tok0 + tc2 * 512, tok0 + (tc2 + 1) * 512)
                            if is0:
                                t.wait_ge(C.sem["EV"], max(ev0, ev0 + gi - 1))
                            else:
                                t.wait_ge(C.sem["VV"], max(vv0, vv0 + 2 * (gi - 1)))
                            pa = PS[gi % 2] if is0 else PS[(gi % 2) * 2]
                            for ccx in range(NCH):
                                ins = t.matmul(pa[:, 0:512],
                                               scr[:, slot * 4096 + ccx * 512 + fsub * 128: slot * 4096 + ccx * 512 + fsub * 128 + 128],
                                               xn[:, ccx, sl], start=(ccx == 0), stop=(ccx == NCH - 1))
                            if not is0:
                                pb = PS[(gi % 2) * 2 + 1]
                                for ccx in range(NCH):
                                    ins = t.matmul(pb[:, 0:512],
                                                   scr[:, 4096 + ccx * 512 + fsub * 128: 4096 + ccx * 512 + fsub * 128 + 128],
                                                   xn[:, ccx, sl], start=(ccx == 0), stop=(ccx == NCH - 1))
                            C.inc(ins, "MM")
                            gi += 1
                if is0:
                    @b.scalar
                    def _(s):
                        gi = 0
                        for fsub in range(4):
                            fcidx = fc * 4 + fsub
                            for tc2 in range(TK // 512):
                                hsl = slice(tc2 * 512, (tc2 + 1) * 512)
                                s.wait_ge(C.sem["MM"], mm0 + gi + 1)
                                ins = s.activation(AR2h[:, fcidx, hsl], PS[gi % 2][:, 0:512],
                                                   AF.Relu,
                                                   bias=H["b256"][:, e * 32 + fcidx:e * 32 + fcidx + 1])
                                C.inc(ins, "EV")
                                gi += 1
                    @b.vector
                    def _(v):
                        gi = 0
                        for fsub in range(4):
                            fcidx = fc * 4 + fsub
                            for tc2 in range(TK // 512):
                                sl = slice(tok0 + tc2 * 512, tok0 + (tc2 + 1) * 512)
                                hsl = slice(tc2 * 512, (tc2 + 1) * 512)
                                v.wait_ge(C.sem["EV"], ev0 + gi + 1)
                                ins = v.tensor_tensor(AR2h[:, fcidx, hsl], AR2h[:, fcidx, hsl],
                                                      wbcur[:, sl], op=ALU.mult)
                                C.inc(ins, "VV")
                                gi += 1
                else:
                    @b.scalar
                    def _(s):
                        gi = 0
                        for fsub in range(4):
                            for tc2 in range(TK // 512):
                                s.wait_ge(C.sem["MM"], mm0 + gi + 1)
                                if gi >= 2:
                                    s.wait_ge(C.sem["VV"], vv0 + 2 * (gi - 1))
                                ins = s.activation(fa[:, 0:512] if gi % 2 == 0 else fb[:, 0:512],
                                                   PS[(gi % 2) * 2][:, 0:512], AF.Silu)
                                C.inc(ins, "EV")
                                gi += 1
                    @b.vector
                    def _(v):
                        gi = 0
                        for fsub in range(4):
                            fcidx = fc * 4 + fsub
                            for tc2 in range(TK // 512):
                                sl = slice(tok0 + tc2 * 512, tok0 + (tc2 + 1) * 512)
                                hsl = slice(tc2 * 512, (tc2 + 1) * 512)
                                v.wait_ge(C.sem["EV"], ev0 + gi + 1)
                                ff = fa if gi % 2 == 0 else fb
                                ins = v.tensor_tensor(ff[:, 0:512], ff[:, 0:512],
                                                      PS[(gi % 2) * 2 + 1][:, 0:512], op=ALU.mult)
                                C.fence(v, ins, "VV")
                                ins = v.tensor_tensor(AR2h[:, fcidx, hsl], ff[:, 0:512],
                                                      wbcur[:, sl], op=ALU.mult)
                                C.inc(ins, "VV")
                                gi += 1
            NG2 = TK // 512
            for dc in range(NCH if MOE_LIMIT is None else min(NCH, MOE_LIMIT[2])):
                slot = dc % 2
                @b.gpsimd
                def _(g):
                    g.wait_ge(C.sem["LDG"], C.cnt["LDG"])
                    need = C.cnt["MM"] - NG2 if dc >= 1 else C.cnt["MM"]
                    g.wait_ge(C.sem["MM"], max(0, need))
                    C.w(g, "EV", "VV") if dc == 0 else None
                    g.wait_ge(C.sem[lsem], C.cnt[lsem])
                    ins = g.dma_start(scr[:, slot * 4096:slot * 4096 + NFCC * 128],
                                      w2_d[e, dc, :, :])
                    C.inc(ins, "LDG", 16)
                ldg_t = C.cnt["LDG"]
                mm0 = C.cnt["MM"]
                vv0 = C.cnt["VV"]
                @b.tensor
                def _(t):
                    t.wait_ge(C.sem["LDG"], ldg_t)
                    for tc2 in range(TK // 512):
                        t.wait_ge(C.sem["VV"], max(vv0, vv0 + tc2 - 1))
                        for fcc in range(NFCC):
                            ins = t.matmul(PS[4 + tc2 % 2][:, 0:512],
                                           scr[:, slot * 4096 + fcc * 128:slot * 4096 + (fcc + 1) * 128],
                                           AR2h[:, fcc, slice(tc2 * 512, (tc2 + 1) * 512)],
                                           start=(fcc == 0), stop=(fcc == NFCC - 1))
                        C.inc(ins, "MM")
                @b.vector
                def _(v):
                    for tc2 in range(TK // 512):
                        sl = slice(tok0 + tc2 * 512, tok0 + (tc2 + 1) * 512)
                        v.wait_ge(C.sem["MM"], mm0 + tc2 + 1)
                        if (not is0) and e == 0:
                            ins = v.tensor_copy(xa[:, dc, sl], PS[4 + tc2 % 2][:, 0:512])
                        else:
                            ins = v.tensor_tensor(xa[:, dc, sl], xa[:, dc, sl],
                                                  PS[4 + tc2 % 2][:, 0:512], op=ALU.add)
                        C.inc(ins, "VV")

    if "final" not in MOE_STAGES:
        C.x_in_hbm = True
        return
    if is0:
        emit_ln(C, H, xa, xn, gamma=H["b8"][:, 16:24], beta=H["b8"][:, 24:32])
    for d_ in range(NCH):
        @b.sync
        def _(s):
            s.wait_ge(C.sem["LDS"], C.cnt["LDS"])
            C.w(s, "VV", "EV", "ST")
            ins = s.dma_start(t_s[:, 0:2048], xsp[d_ * 128:(d_ + 1) * 128, :])
            C.inc(ins, "LDS", 16)
        for tc in range(NTC):
            sl = slice(tc * 512, (tc + 1) * 512)
            @b.vector
            def _(v):
                C.w(v, "LDS", "ST")
                if is0:
                    ins = v.tensor_tensor(H["f512_b"][:, 0:512], t_s[:, sl],
                                          xn[:, d_, sl], op=ALU.add)
                else:
                    ins = v.scalar_tensor_tensor(H["f512_b"][:, 0:512], xa[:, d_, sl],
                                                 0.5, t_s[:, sl], op0=ALU.mult, op1=ALU.add)
                C.inc(ins, "VV")
            @b.sync
            def _(s):
                C.w(s, "VV")
                ins = s.dma_start(xsp[d_ * 128:(d_ + 1) * 128, sl], H["f512_b"][:, 0:512])
                C.inc(ins, "ST", 16)
    C.x_in_hbm = True


def emit_l0f(C, H, li):
    emit_moe(C, H, "l0f", li)


def emit_l3f(C, H, li):
    emit_moe(C, H, "l3f", li)



def emit_l1a(C, H, li):
    b = C.block
    xa, xn = H["xa"], H["xn"]
    PS = H["PS"]
    scr = H["scr16"]
    xsp = C.dram["xspill"]
    t_s = H["t_s"]
    xa_bf = H["xa_t"][:].bitcast(BF16)
    kT = xa_bf[:, 0:16384].rearrange("p (c t) -> p c t", c=NCH)
    rT = xa_bf[:, 16384:32768].rearrange("p (c t) -> p c t", c=NCH)
    Vtm = H["xn_t"][:].rearrange("p (tt c) -> p tt c", tt=16)
    ynT = H["xn_t"][:].rearrange("p (c t) -> p c t", c=NCH)
    AR = H["AR2"]
    dxa = AR[:, 0:16384].rearrange("p (c t) -> p c t", c=NCH)
    g_fm = dxa
    hb = AR[:, 16384:32768].rearrange("p (c t) -> p c t", c=NCH)
    y_tm = AR[:, 16384:32768].rearrange("p (tt c) -> p tt c", tt=16)
    fa, fb = H["f512_a"], H["f512_b"]
    sw = scr[:, 0:4096].bitcast(F32)
    ps4b = PS[4].bitcast(BF16)

    @b.gpsimd
    def _(g):
        g.wait_ge(C.sem["LDG"], C.cnt["LDG"])
        C.w(g, "MM", "EV", "VV")
        for (dst, nm) in ((H["pp"][:, 0:8], "l1a_miux"), (H["pp"][:, 8:48], "l1a_lam"),
                          (H["tdmiu2"][:], "l1a_tdmiu"), (H["u2"][:], "l1a_u"),
                          (H["msk2"][:, 0:128], "c_mstrict"), (H["msk2"][:, 128:256], "c_identf"),
                          (H["ident_bf"][:], "c_ident"), (H["tdb"][:], "l1a_tdb")):
            ins = g.dma_start(dst, C.dram[nm][:, :])
            C.inc(ins, "LDG", 16)
    @b.sync
    def _(s):
        s.wait_ge(C.sem["LDS"], C.cnt["LDS"])
        C.w(s, "VV", "EV")
        ins = s.dma_start(H["b8"][:, 32:40], C.dram["l1a_gng"][:, :])
        C.inc(ins, "LDS", 16)
        ins = s.dma_start(H["b8"][:, 40:48], C.dram["l1a_gnb"][:, :])
        C.inc(ins, "LDS", 16)

    @b.sync
    def _(s):
        C.wall(s)
        for d_ in range(NCH):
            ins = s.dma_start(xsp[d_ * 128:(d_ + 1) * 128, :], xa[:, d_, :])
            C.inc(ins, "ST", 16)

    @b.vector
    def _(v):
        C.w(v, "LDG", "LDS", "EV", "MM")
        for d_ in range(NCH):
            v.tensor_scalar(dxa[:, d_, 0:1], xn[:, d_, 0:1], -1.0, None, op0=ALU.mult)
            ins = v.tensor_tensor(dxa[:, d_, 1:2048], xn[:, d_, 0:2047],
                                  xn[:, d_, 1:2048], op=ALU.subtract)
        C.fence(v, ins, "VV")
        for d_ in range(NCH):
            ins = v.scalar_tensor_tensor(hb[:, d_, :], dxa[:, d_, :],
                                         H["pp"][:, d_:d_ + 1], xn[:, d_, :],
                                         op0=ALU.mult, op1=ALU.add)
        C.inc(ins, "VV")

    @b.gpsimd
    def _(g):
        g.wait_ge(C.sem["LDG"], C.cnt["LDG"])
        C.w(g, "VV", "MM", "EV")
        for ccx in range(NCH):
            ins = g.dma_start(scr[:, 4096 + ccx * 160: 4096 + (ccx + 1) * 160],
                              C.dram["l1a_alora"][ccx, :, :])
            C.inc(ins, "LDG", 16)
    for f in range(5):
        dst = (H["loraA"], H["loraB"], H["loraC"])[f // 2]
        r0 = (f % 2) * 32 if f < 4 else 0
        for tc in range(NTC):
            sl = slice(tc * 512, (tc + 1) * 512)
            @b.tensor
            def _(t):
                C.w(t, "LDG", "VV", "EV")
                for ccx in range(NCH):
                    ins = t.matmul(PS[0][r0:r0 + 32, 0:512],
                                   scr[:, 4096 + ccx * 160 + f * 32: 4096 + ccx * 160 + (f + 1) * 32],
                                   hb[:, ccx, sl], start=(ccx == 0), stop=(ccx == NCH - 1),
                                   tile_position=(0, r0))
                C.inc(ins, "MM")
            @b.scalar
            def _(s):
                C.w(s, "MM")
                ins = s.activation(dst[r0:r0 + 32, sl], PS[0][r0:r0 + 32, 0:512], AF.Tanh)
                C.inc(ins, "EV")

    for f, tag in ((0, "w"), (1, "k"), (3, "r"), (4, "g"), (2, "v")):
        lsrc = (H["loraA"], H["loraB"], H["loraC"])[f // 2]
        r0 = (f % 2) * 32 if f < 4 else 0
        @b.gpsimd
        def _(g):
            g.wait_ge(C.sem["LDG"], C.cnt["LDG"])
            C.w(g, "MM", "EV", "VV")
            ins = g.dma_start(H["blf"][r0:r0 + 32, :],
                              C.dram["l1a_blora"][:, f * 1024:(f + 1) * 1024])
            C.inc(ins, "LDG", 16)
        for d_ in range(NCH):
            for tc in range(NTC):
                sl = slice(tc * 512, (tc + 1) * 512)
                @b.tensor
                def _(t):
                    C.w(t, "LDG", "VV", "EV")
                    ins = t.matmul(PS[0][:, 0:512], H["blf"][r0:r0 + 32, d_ * 128:(d_ + 1) * 128],
                                   lsrc[r0:r0 + 32, sl], start=True, stop=True)
                    C.inc(ins, "MM")
                @b.scalar
                def _(s):
                    C.w(s, "MM")
                    ins = s.activation(fa[:, 0:512], PS[0][:, 0:512], AF.Identity,
                                       bias=H["pp"][:, 8 + f * 8 + d_:9 + f * 8 + d_])
                    C.inc(ins, "EV")
                @b.vector
                def _(v):
                    C.w(v, "EV")
                    if tag == "v":
                        if tc == 0:
                            v.tensor_scalar(fb[:, 0:1], xn[:, d_, 0:1], -1.0, None, op0=ALU.mult)
                            ins = v.tensor_tensor(fb[:, 1:512], xn[:, d_, 0:511],
                                                  xn[:, d_, 1:512], op=ALU.subtract)
                        else:
                            ins = v.tensor_tensor(fb[:, 0:512], xn[:, d_, tc * 512 - 1:tc * 512 + 511],
                                                  xn[:, d_, sl], op=ALU.subtract)
                        C.fence(v, ins, "VV")
                        ins = v.tensor_tensor(fb[:, 0:512], fb[:, 0:512], fa[:, 0:512], op=ALU.mult)
                        C.fence(v, ins, "VV")
                    else:
                        ins = v.tensor_tensor(fb[:, 0:512], dxa[:, d_, sl], fa[:, 0:512], op=ALU.mult)
                        C.fence(v, ins, "VV")
                    ins = v.tensor_tensor(hb[:, d_, sl], fb[:, 0:512], xn[:, d_, sl], op=ALU.add)
                    C.inc(ins, "VV")
        if tag == "w":
            @b.gpsimd
            def _(g):
                g.wait_ge(C.sem["LDG"], C.cnt["LDG"])
                C.w(g, "MM", "EV", "VV")
                for ccx in range(NCH):
                    ins = g.dma_start(scr[:, 5440 + ccx * 64: 5440 + (ccx + 1) * 64],
                                      C.dram["l1a_tda"][ccx, :, :])
                    C.inc(ins, "LDG", 16)
            for tc in range(NTC):
                sl = slice(tc * 512, (tc + 1) * 512)
                @b.tensor
                def _(t):
                    C.w(t, "LDG", "VV", "EV")
                    for ccx in range(NCH):
                        ins = t.matmul(PS[1][0:64, 0:512], scr[:, 5440 + ccx * 64: 5440 + (ccx + 1) * 64],
                                       hb[:, ccx, sl], start=(ccx == 0), stop=(ccx == NCH - 1))
                    C.inc(ins, "MM")
                @b.scalar
                def _(s):
                    C.w(s, "MM")
                    ins = s.activation(H["twT"][:, sl], PS[1][0:64, 0:512], AF.Tanh)
                    C.inc(ins, "EV")
        elif tag == "v":
            for cb in range(2):
                @b.gpsimd
                def _(g):
                    g.wait_ge(C.sem["LDG"], C.cnt["LDG"])
                    C.w(g, "MM", "EV", "VV")
                    ins = g.dma_start(scr[:, 4096:8192], C.dram["l1a_wv"][cb, :, :])
                    C.inc(ins, "LDG", 16)
                scrv = scr[:, 4096:8192].rearrange("p (c j) -> p c j", c=NCH)
                for tt in range(16):
                    @b.tensor
                    def _(t):
                        C.w(t, "LDG", "VV", "EV")
                        for ccx in range(NCH):
                            ins = t.matmul(PS[1][:, 0:512], hb[:, ccx, tt * 128:(tt + 1) * 128],
                                           scrv[:, ccx, :], start=(ccx == 0), stop=(ccx == NCH - 1))
                        C.inc(ins, "MM")
                    @b.scalar
                    def _(s):
                        C.w(s, "MM")
                        ins = s.copy(Vtm[:, tt, cb * 512:(cb + 1) * 512], PS[1][:, 0:512])
                        C.inc(ins, "EV")
        else:
            wnm = {"k": "l1a_wk", "r": "l1a_wr", "g": "l1a_wg"}[tag]
            dstT = {"k": kT, "r": rT, "g": g_fm}[tag]
            for oc in range(NCH):
                @b.gpsimd
                def _(g):
                    g.wait_ge(C.sem["LDG"], C.cnt["LDG"])
                    C.w(g, "MM", "EV", "VV")
                    ins = g.dma_start(scr[:, 4096:5120], C.dram[wnm][oc, :, :])
                    C.inc(ins, "LDG", 16)
                scw = scr[:, 4096:5120].rearrange("p (c j) -> p c j", c=NCH)
                for tc in range(NTC):
                    sl = slice(tc * 512, (tc + 1) * 512)
                    @b.tensor
                    def _(t):
                        C.w(t, "LDG", "VV", "EV")
                        for ccx in range(NCH):
                            ins = t.matmul(PS[1][:, 0:512], scw[:, ccx, :],
                                           hb[:, ccx, sl], start=(ccx == 0), stop=(ccx == NCH - 1))
                        C.inc(ins, "MM")
                    @b.scalar
                    def _(s):
                        C.w(s, "MM")
                        ins = s.activation(dstT[:, oc, sl], PS[1][:, 0:512],
                                           AF.Silu if tag == "g" else AF.Identity)
                        C.inc(ins, "EV")

    zzf = sw[:, 896:1024]
    @b.vector
    def _(v):
        C.w(v, "EV", "MM")
        v.memset(zzf, 0.0)
        ins = v.memset(H["Ssb"][:], 0.0)
        C.inc(ins, "VV")
    for c_ in range(16):
        for h in range(16):
            B, chh = (h % 2) * 64, h // 2
            ew, CSs, tmp = sw[B:B + 64, 0:128], sw[B:B + 64, 128:256], sw[B:B + 64, 256:384]
            aex, ainv = sw[B:B + 64, 384:512], sw[B:B + 64, 512:640]
            ek, kinv = sw[B:B + 64, 640:768], sw[B:B + 64, 768:896]
            tmp2 = sw[B:B + 64, 1152:1280]
            plc = sw[B:B + 64, 1024:1025]
            zz = zzf[B:B + 64, :]
            rt_b = scr[B:B + 64, 4096:4224]
            kt_b = scr[B:B + 64, 4224:4352]
            kh_b = scr[B:B + 64, 4352:4480]
            kc_b = scr[B:B + 64, 4480:4608]
            A1, A2, Am = scr[:, 4608:4736], scr[:, 4736:4864], scr[:, 4864:4992]
            khtm = scr[:, 4992:5056]
            Ss = H["Ssb"][B:B + 64, chh * 64:(chh + 1) * 64]
            Sbf = H["Sbf"][B:B + 64, :]
            @b.tensor
            def _(t):
                C.w(t, "VV", "EV")
                ins = t.matmul(PS[0][B:B + 64, 0:128], H["tdb"][0:64, h * 64:(h + 1) * 64],
                               H["twT"][0:64, c_ * 128:(c_ + 1) * 128], start=True, stop=True,
                               tile_position=(0, B))
                C.inc(ins, "MM")
            @b.scalar
            def _(s):
                C.w(s, "MM")
                ins = s.activation(ew, PS[0][B:B + 64, 0:128], AF.Exp,
                                   bias=H["tdmiu2"][B:B + 64, chh:chh + 1])
                C.inc(ins, "EV")
            @b.vector
            def _(v):
                C.w(v, "EV")
                ins = v.tensor_tensor_scan(CSs, ew, zz, 0.0, op0=ALU.add, op1=ALU.add)
                C.fence(v, ins, "VV")
                ins = v.tensor_tensor(tmp, ew, CSs, op=ALU.subtract)
                C.fence(v, ins, "VV")
                ins = v.tensor_scalar(tmp2, CSs, CSs[:, 127:128], None, op0=ALU.subtract)
                C.inc(ins, "VV")
            @b.scalar
            def _(s):
                C.w(s, "VV")
                s.activation(ainv, CSs, AF.Exp)
                s.activation(ek, tmp2, AF.Exp)
                s.activation(kinv, tmp, AF.Exp, bias=0.0, scale=-1.0)
                s.activation(aex, tmp, AF.Exp)
                ins = s.activation(plc, CSs[:, 127:128], AF.Exp, bias=0.0, scale=-1.0)
                C.inc(ins, "EV")
            @b.vector
            def _(v):
                C.w(v, "EV")
                v.tensor_tensor(rt_b, rT[B:B + 64, chh, c_ * 128:(c_ + 1) * 128], aex, op=ALU.mult)
                v.tensor_tensor(kt_b, kT[B:B + 64, chh, c_ * 128:(c_ + 1) * 128], ainv, op=ALU.mult)
                v.tensor_tensor(kh_b, kT[B:B + 64, chh, c_ * 128:(c_ + 1) * 128], ek, op=ALU.mult)
                ins = v.tensor_tensor(kc_b, kT[B:B + 64, chh, c_ * 128:(c_ + 1) * 128], kinv, op=ALU.mult)
                C.fence(v, ins, "VV")
                ins = v.tensor_scalar(kc_b, kc_b, H["u2"][B:B + 64, chh:chh + 1], None, op0=ALU.mult)
                C.fence(v, ins, "VV")
                ins = v.tensor_copy(Sbf, Ss)
                C.inc(ins, "VV")
            @b.tensor
            def _(t):
                C.w(t, "VV", "EV")
                t.matmul(PS[1][:, 0:128], kt_b, rt_b, start=True, stop=True)
                t.matmul(PS[2][:, 0:128], kc_b, rt_b, start=True, stop=True)
                ins = t.transpose(ps4b[:, 0:64], kh_b, H["ident_bf"][B:B + 64, B:B + 64])
                C.inc(ins, "MM")
            @b.scalar
            def _(s):
                C.w(s, "MM")
                ins = s.copy(khtm, ps4b[:, 0:64])
                C.inc(ins, "EV")
            @b.vector
            def _(v):
                C.w(v, "MM", "EV")
                v.tensor_tensor(A1, PS[1][:, 0:128], H["msk2"][:, 0:128], op=ALU.mult)
                ins = v.tensor_tensor(A2, PS[2][:, 0:128], H["msk2"][:, 128:256], op=ALU.mult)
                C.fence(v, ins, "VV")
                ins = v.tensor_tensor(Am, A1, A2, op=ALU.add)
                C.inc(ins, "VV")
            @b.tensor
            def _(t):
                C.w(t, "VV", "EV")
                t.matmul(PS[3][:, 0:64], Am, Vtm[:, c_, h * 64:(h + 1) * 64],
                         start=True, stop=False)
                t.matmul(PS[3][:, 0:64], rt_b, Sbf, start=False, stop=True)
                ins = t.matmul(PS[5][B:B + 64, 0:64], khtm, Vtm[:, c_, h * 64:(h + 1) * 64],
                               start=True, stop=True, tile_position=(0, B))
                C.inc(ins, "MM")
            @b.scalar
            def _(s):
                C.w(s, "MM")
                ins = s.copy(y_tm[:, c_, h * 64:(h + 1) * 64], PS[3][:, 0:64])
                C.inc(ins, "EV")
            @b.vector
            def _(v):
                C.w(v, "MM")
                ins = v.tensor_scalar(Ss, Ss, plc, None, op0=ALU.mult)
                C.fence(v, ins, "VV")
                ins = v.tensor_tensor(Ss, Ss, PS[5][B:B + 64, 0:64], op=ALU.add)
                C.inc(ins, "VV")

    ynp = scr[:, 5056:5120]
    for tt in range(16):
        for h in range(16):
            po, chh = (h % 2) * 64, h // 2
            ysl = y_tm[:, tt, h * 64:(h + 1) * 64]
            @b.vector
            def _(v):
                C.w(v, "EV", "MM")
                ins = v.bn_stats(fa[:, 0:6], ysl)
                C.fence(v, ins, "VV")
                ins = v.bn_aggr(fa[:, 8:10], fa[:, 0:6])
                C.inc(ins, "VV")
            @b.scalar
            def _(s):
                C.w(s, "VV")
                ins = s.activation(fa[:, 12:13], fa[:, 9:10], AF.Ln, bias=H["cst"][:, 2:3])
                C.fence(s, ins, "EV")
                ins = s.activation(fa[:, 12:13], fa[:, 12:13], AF.Exp, bias=0.0, scale=-0.5)
                C.inc(ins, "EV")
            @b.vector
            def _(v):
                C.w(v, "EV")
                ins = v.tensor_scalar(ynp, ysl, fa[:, 8:9], fa[:, 12:13],
                                      op0=ALU.subtract, op1=ALU.mult)
                C.inc(ins, "VV")
            @b.tensor
            def _(t):
                C.w(t, "VV", "EV")
                if po == 0:
                    ins = t.transpose(ps4b[0:64, 0:128], ynp, H["ident_bf"][:])
                else:
                    ins = t.transpose(ps4b[64:128, 0:128], ynp, H["ident_bf"][:],
                                      tile_position=(0, 64))
                C.inc(ins, "MM")
            @b.scalar
            def _(s):
                C.w(s, "MM")
                ins = s.activation(ynT[po:po + 64, chh, tt * 128:(tt + 1) * 128],
                                   ps4b[po:po + 64, 0:128], AF.Identity,
                                   bias=H["b8"][po:po + 64, 40 + chh:41 + chh],
                                   scale=H["b8"][po:po + 64, 32 + chh:33 + chh])
                C.inc(ins, "EV")
    @b.vector
    def _(v):
        C.w(v, "EV", "MM")
        for d_ in range(NCH):
            ins = v.tensor_tensor(g_fm[:, d_, :], g_fm[:, d_, :], ynT[:, d_, :], op=ALU.mult)
        C.inc(ins, "VV")
    for oc in range(NCH):
        @b.gpsimd
        def _(g):
            g.wait_ge(C.sem["LDG"], C.cnt["LDG"])
            C.w(g, "MM", "EV", "VV")
            ins = g.dma_start(scr[:, 4096:5120], C.dram["l1a_wo"][oc, :, :])
            C.inc(ins, "LDG", 16)
        @b.sync
        def _(s):
            s.wait_ge(C.sem["LDS"], C.cnt["LDS"])
            C.w(s, "VV", "ST")
            ins = s.dma_start(t_s[:, 0:2048], xsp[oc * 128:(oc + 1) * 128, :])
            C.inc(ins, "LDS", 16)
        scw = scr[:, 4096:5120].rearrange("p (c j) -> p c j", c=NCH)
        for tc in range(NTC):
            sl = slice(tc * 512, (tc + 1) * 512)
            @b.tensor
            def _(t):
                C.w(t, "LDG", "VV", "EV")
                for ccx in range(NCH):
                    ins = t.matmul(PS[0][:, 0:512], scw[:, ccx, :], g_fm[:, ccx, sl],
                                   start=(ccx == 0), stop=(ccx == NCH - 1))
                C.inc(ins, "MM")
            @b.vector
            def _(v):
                C.w(v, "MM", "LDS", "ST")
                ins = v.scalar_tensor_tensor(fb[:, 0:512], PS[0][:, 0:512], 1.0,
                                             t_s[:, sl], op0=ALU.mult, op1=ALU.add)
                C.inc(ins, "VV")
            @b.sync
            def _(s):
                C.w(s, "VV")
                ins = s.dma_start(xsp[oc * 128:(oc + 1) * 128, sl], fb[:, 0:512])
                C.inc(ins, "ST", 16)
    C.x_in_hbm = True



def emit_l2f(C, H, li):
    b = C.block
    xa, xn = H["xa"], H["xn"]
    PS = H["PS"]
    scr = H["scr16"]
    scrq = scr.rearrange("p (c j) -> p c j", c=NCH)
    xsp = C.dram["xspill"]
    t_s = H["t_s"]
    fa, fb = H["f512_a"], H["f512_b"]
    ps4b = PS[4].bitcast(BF16)
    QTf = H["AR2"][:, 0:16384].rearrange("p (c t) -> p c t", c=NCH)
    KTf = H["AR2"][:, 16384:32768].rearrange("p (c t) -> p c t", c=NCH)
    xa_bf = H["xa_t"][:].bitcast(BF16)
    Vtm = xa_bf[:, 0:16384].rearrange("p (tt c) -> p tt c", tt=16)
    xsb = xn
    ix = xa
    xsp2 = C.dram["xspill2"]
    hAR = H["AR2"][:, 0:22528].rearrange("p (f t) -> p f t", f=22)
    qt_s, qsq, rr = H["qt_s"], H["qsq_s"], H["rr"]
    probs = H["qt_s"]

    @b.gpsimd
    def _(g):
        g.wait_ge(C.sem["LDG"], C.cnt["LDG"])
        C.w(g, "MM", "EV", "VV")
        ins = g.dma_start(H["mask_f"][:], C.dram["c_causal_add"][:, :])
        C.inc(ins, "LDG", 16)
        ins = g.dma_start(H["mlow"][:], C.dram["c_causal_low"][:, :])
        C.inc(ins, "LDG", 16)
        ins = g.dma_start(H["ident_bf"][:], C.dram["c_ident"][:, :])
        C.inc(ins, "LDG", 16)
        ins = g.dma_start(H["l2perm_sb"][:], C.dram["c_l2perm"][:, :])
        C.inc(ins, "LDG", 16)

    @b.sync
    def _(s):
        C.wall(s)
        for d_ in range(NCH):
            ins = s.dma_start(xsp[d_ * 128:(d_ + 1) * 128, :], xa[:, d_, :])
            C.inc(ins, "ST", 16)
    @b.vector
    def _(v):
        C.w(v, "ST", "MM", "EV", "GG", "LDG")
        for d_ in range(NCH):
            ins = v.tensor_copy(ix[:, d_, :], xn[:, d_, :])
        C.inc(ins, "VV")

    for il in range(L2F_DEPTH):
        @b.vector
        def _(v):
            C.w(v, "MM", "EV")
            for d_ in range(NCH):
                if d_ < 4:
                    ins = v.tensor_copy(xsb[:, d_, :], ix[:, d_, :])
                else:
                    v.memset(xsb[:, d_, 0:1], 0.0)
                    ins = v.tensor_copy(xsb[:, d_, 1:2048], ix[:, d_, 0:2047])
            C.inc(ins, "VV")
        @b.sync
        def _(s):
            s.wait_ge(C.sem["LDS"], C.cnt["LDS"])
            C.w(s, "VV", "EV", "MM")
            for ti in range(4):
                ins = s.dma_start(H["rr"][:, 44 + ti:45 + ti],
                                  C.dram["l2f_scales"][il, ti, :, :])
                C.inc(ins, "LDS", 16)
        @b.gpsimd
        def _(g):
            g.wait_ge(C.sem["LDG"], C.cnt["LDG"])
            C.w(g, "EV", "VV", "MM")
            for ti, nm in enumerate(("c_cs8fm", "c_ss8fm", "c_cifm", "c_sifm")):
                ins = g.dma_start(H["tabF"][:, ti * 2048:(ti + 1) * 2048], C.dram[nm][:, :])
                C.inc(ins, "LDG", 16)
        @b.vector
        def _(v):
            C.w(v, "LDS", "LDG")
            for ti in range(4):
                ins = v.tensor_scalar(H["tabF"][:, ti * 2048:(ti + 1) * 2048],
                                      H["tabF"][:, ti * 2048:(ti + 1) * 2048],
                                      H["rr"][:, 44 + ti:45 + ti], None, op0=ALU.mult)
            C.inc(ins, "VV")
        @b.sync
        def _(s):
            C.w(s, "VV", "EV", "MM", "ST")
            for d_ in range(NCH):
                ins = s.dma_start(xsp2[d_ * 128:(d_ + 1) * 128, :], ix[:, d_, :])
                C.inc(ins, "ST", 16)
        for sec in range(3):
            @b.gpsimd
            def _(g):
                g.wait_ge(C.sem["LDG"], C.cnt["LDG"])
                C.w(g, "MM", "EV", "VV")
                for ccx in range(NCH):
                    ins = g.dma_start(scrq[:, ccx, :],
                                      C.dram["l2f_wqkv"][il, sec, :, ccx * 1024:(ccx + 1) * 1024])
                    C.inc(ins, "LDG", 16)
            ldg_t = C.cnt["LDG"]
            stq = {}

            def qk_mm(tt):
                p = tt % 2
                @b.tensor
                def _(t):
                    t.wait_ge(C.sem["LDG"], ldg_t)
                    if tt >= 2:
                        esem, ecnt = stq[("evsem", tt - 2)]
                        t.wait_ge(C.sem[esem], ecnt)
                    elif tt == 0:
                        C.w(t, "EV", "VV")
                    for cb in range(2):
                        for ccx in range(NCH):
                            ins = t.matmul(PS[2 * p + cb][:, 0:512],
                                           xsb[:, ccx, tt * 128:(tt + 1) * 128],
                                           scrq[:, ccx, cb * 512:(cb + 1) * 512],
                                           start=(ccx == 0), stop=(ccx == NCH - 1))
                    C.inc(ins, "MM")
                stq[("mm", tt)] = C.cnt["MM"]

            def v_rest(tt):
                p = tt % 2
                @b.vector
                def _(v):
                    v.wait_ge(C.sem["MM"], stq[("mm", tt)])
                    v.tensor_copy(Vtm[:, tt, 0:512], PS[2 * p][:, 0:512])
                    ins = v.tensor_copy(Vtm[:, tt, 512:1024], PS[2 * p + 1][:, 0:512])
                    C.inc(ins, "VV")
                stq[("evsem", tt)] = ("VV", C.cnt["VV"])

            def qk_rest(tt):
                p = tt % 2
                @b.vector
                def _(v):
                    v.wait_ge(C.sem["MM"], stq[("mm", tt)])
                    if tt >= 1:
                        v.wait_ge(C.sem["MM"], stq[("tp", tt - 1)])
                    v.tensor_copy(qt_s[:, 0:512], PS[2 * p][:, 0:512])
                    ins = v.tensor_copy(qt_s[:, 512:1024], PS[2 * p + 1][:, 0:512])
                    C.inc(ins, "VV")
                stq[("evsem", tt)] = ("VV", C.cnt["VV"])
                @b.gpsimd
                def _(g):
                    g.wait_ge(C.sem["VV"], stq[("evsem", tt)][1])
                    ins = g.tensor_tensor(qsq[:], qt_s[:], qt_s[:], op=ALU.mult)
                    C.inc(ins, "GG")
                gg_t = C.cnt["GG"]
                @b.vector
                def _(v):
                    v.wait_ge(C.sem["GG"], gg_t)
                    ins = v.tensor_reduce(rr[:, 0:16], qsq[:].rearrange("p (h k) -> p h k", k=64),
                                          axis=AX.X, op=ALU.add)
                    C.inc(ins, "VV")
                vv_t = C.cnt["VV"]
                @b.scalar
                def _(s):
                    s.wait_ge(C.sem["VV"], vv_t)
                    ins = s.activation(rr[:, 16:32], rr[:, 0:16], AF.Sqrt)
                    C.inc(ins, "EV")
                ev2_t = C.cnt["EV"]
                @b.vector
                def _(v):
                    v.wait_ge(C.sem["EV"], ev2_t)
                    ins = v.tensor_scalar(rr[:, 16:32], rr[:, 16:32], 1e-12, None, op0=ALU.max)
                    C.fence(v, ins, "VV")
                    ins = v.reciprocal(rr[:, 32:48], rr[:, 16:32])
                    C.fence(v, ins, "VV")
                    if sec == 1:
                        for h in range(16):
                            ins = v.tensor_scalar(qt_s[:, h * 64:(h + 1) * 64],
                                                  qt_s[:, h * 64:(h + 1) * 64],
                                                  rr[:, 32 + h:33 + h], None, op0=ALU.mult)
                    else:
                        ins = v.tensor_copy(H["rqall"][:, tt * 16:(tt + 1) * 16], rr[:, 32:48])
                    C.inc(ins, "VV")
                stq[("nm", tt)] = C.cnt["VV"]
                dstT = QTf if sec == 0 else KTf
                for hg in range(2):
                    @b.tensor
                    def _(t):
                        t.wait_ge(C.sem["VV"], stq[("nm", tt)])
                        if tt >= 1 or hg == 1:
                            t.wait_ge(C.sem["VV"], C.cnt["VV"])
                        for j in range(8):
                            h = hg * 8 + j
                            ins = t.transpose(ps4b[0:64, j * 128:(j + 1) * 128],
                                              qt_s[:, h * 64:(h + 1) * 64],
                                              H["ident_bf"][:])
                        C.inc(ins, "MM")
                    mmtp = C.cnt["MM"]
                    @b.vector
                    def _(v):
                        v.wait_ge(C.sem["MM"], mmtp)
                        for j in range(8):
                            h = hg * 8 + j
                            po, chh = (h % 2) * 64, h // 2
                            ins = v.tensor_copy(dstT[po:po + 64, chh, tt * 128:(tt + 1) * 128],
                                                ps4b[0:64, j * 128:(j + 1) * 128])
                        C.inc(ins, "VV")
                stq[("tp", tt)] = C.cnt["MM"]

            for tt in range(17):
                if tt < 16:
                    qk_mm(tt)
                if tt >= 1:
                    (v_rest if sec == 2 else qk_rest)(tt - 1)
        for si, srcT in ((0, QTf), (1, KTf)):
            toff = 0 if si == 0 else 4096
            rst = {}
            for u in range(33):
                d_, tc = u // 4, u % 4
                if u < 32:
                    mm_pre = C.cnt["MM"]
                    @b.tensor
                    def _(t):
                        if u >= 2:
                            t.wait_ge(C.sem["VV"], rst[("tt", u - 2)])
                        else:
                            C.w(t, "EV", "VV")
                        sl = slice(tc * 512, (tc + 1) * 512)
                        ins = t.matmul(PS[6 + u % 2][:, 0:512], H["l2perm_sb"][:],
                                       srcT[:, d_, sl], start=True, stop=True)
                        C.inc(ins, "MM")
                    rst[("mm", u)] = C.cnt["MM"]
                if u >= 1:
                    up = u - 1
                    dp, tcp = up // 4, up % 4
                    slp = slice(tcp * 512, (tcp + 1) * 512)
                    @b.vector
                    def _(v):
                        v.wait_ge(C.sem["MM"], rst[("mm", up)])
                        v.tensor_tensor(fa[:, 0:512], srcT[:, dp, slp],
                                        H["tabF"][:, toff + tcp * 512:toff + (tcp + 1) * 512],
                                        op=ALU.mult)
                        ins = v.tensor_tensor(fb[:, 0:512], PS[6 + up % 2][:, 0:512],
                                              H["tabF"][:, 2048 + toff + tcp * 512:2048 + toff + (tcp + 1) * 512],
                                              op=ALU.mult)
                        C.fence(v, ins, "VV")
                        ins = v.tensor_tensor(srcT[:, dp, slp], fa[:, 0:512], fb[:, 0:512],
                                              op=ALU.add)
                        C.inc(ins, "VV")
                    rst[("tt", up)] = C.cnt["VV"]
        for hp in range(8 if "attn" in L2F_PARTS else 0):
            chh = hp
            st = {}

            def _sc(qt):
                kts = [kt for kt in (qt - 2, qt - 1, qt) if kt >= 0]
                return kts, kts[0] * 128, len(kts) * 128

            def stage_scores(qt):
                p = qt % 2
                kts, k0, NK = _sc(qt)
                ev_guard = C.cnt["EV"]
                vv_guard = C.cnt["VV"]
                @b.tensor
                def _(t):
                    if qt >= 2:
                        t.wait_ge(C.sem["EV"], st[("exp", qt - 2)])
                    elif hp == 0:
                        t.wait_ge(C.sem["EV"], ev_guard)
                        t.wait_ge(C.sem["VV"], vv_guard)
                    for pо in (0, 64):
                        ins = t.matmul(PS[2 * p + pо // 64][:, 0:NK],
                                       QTf[pо:pо + 64, chh, qt * 128:(qt + 1) * 128],
                                       KTf[pо:pо + 64, chh, k0:k0 + NK], start=True, stop=True)
                    C.inc(ins, "MM")
                st[("sc", qt)] = C.cnt["MM"]

            def stage_rest(qt):
                p = qt % 2
                kts, k0, NK = _sc(qt)
                @b.vector
                def _(v):
                    v.wait_ge(C.sem["MM"], st[("sc", qt)])
                    for j2 in range(2):
                        psb = PS[2 * p + j2]
                        v.reduce_max(fa[:, j2 * 4:j2 * 4 + 1], psb[:, 0:NK],
                                     axis=AX.X, negate=True)
                        v.tensor_tensor(psb[:, NK - 128:NK], psb[:, NK - 128:NK],
                                        H["mask_f"][:], op=ALU.add)
                        if len(kts) == 3:
                            v.tensor_tensor(psb[:, 0:128], psb[:, 0:128],
                                            H["mlow"][:], op=ALU.add)
                    ins = v.tensor_copy(fa[:, 8:9], fa[:, 8:9])
                    C.fence(v, ins, "VV")
                    for j2 in range(2):
                        h = chh * 2 + j2
                        ins = v.tensor_tensor(fa[:, j2 * 4 + 1:j2 * 4 + 2], fa[:, j2 * 4:j2 * 4 + 1],
                                              H["rqall"][:, qt * 16 + h:qt * 16 + h + 1], op=ALU.mult)
                    C.inc(ins, "VV")
                st[("mx", qt)] = C.cnt["VV"]
                @b.scalar
                def _(s):
                    s.wait_ge(C.sem["VV"], st[("mx", qt)])
                    for j2 in range(2):
                        h = chh * 2 + j2
                        buf = probs if j2 == 0 else H["qsq_s"]
                        ins = s.activation(buf[:, p * 512:p * 512 + NK],
                                           PS[2 * p + j2][:, 0:NK], AF.Exp,
                                           bias=fa[:, j2 * 4 + 1:j2 * 4 + 2],
                                           scale=H["rqall"][:, qt * 16 + h:qt * 16 + h + 1],
                                           accum_out=fa[:, j2 * 4 + 2:j2 * 4 + 3])
                    C.inc(ins, "EV")
                st[("exp", qt)] = C.cnt["EV"]
                @b.vector
                def _(v):
                    v.wait_ge(C.sem["EV"], st[("exp", qt)])
                    v.reciprocal(fa[:, 3:4], fa[:, 2:3])
                    ins = v.reciprocal(fa[:, 7:8], fa[:, 6:7])
                    C.fence(v, ins, "VV")
                    for j2 in range(2):
                        buf = probs if j2 == 0 else H["qsq_s"]
                        ins = v.tensor_scalar(buf[:, p * 512:p * 512 + NK],
                                              buf[:, p * 512:p * 512 + NK],
                                              fa[:, j2 * 4 + 3:j2 * 4 + 4], None, op0=ALU.mult)
                    C.inc(ins, "VV")
                st[("nm", qt)] = C.cnt["VV"]
                @b.tensor
                def _(t):
                    t.wait_ge(C.sem["VV"], st[("nm", qt)])
                    if qt >= 2:
                        t.wait_ge(C.sem["VV"], st[("slab", qt - 2)])
                    for j2 in range(2):
                        buf = probs if j2 == 0 else H["qsq_s"]
                        for j in range(len(kts)):
                            ins = t.transpose(ps4b[:, (j2 * 3 + j) * 128:(j2 * 3 + j + 1) * 128],
                                              buf[:, p * 512 + j * 128:p * 512 + (j + 1) * 128],
                                              H["ident_bf"][:])
                    C.inc(ins, "MM")
                st[("tp", qt)] = C.cnt["MM"]
                @b.vector
                def _(v):
                    v.wait_ge(C.sem["MM"], st[("tp", qt)])
                    ins = v.tensor_copy(H["slab2"][:, 0:768], ps4b[:, 0:768])
                    C.inc(ins, "VV")
                st[("slab", qt)] = C.cnt["VV"]
                @b.tensor
                def _(t):
                    t.wait_ge(C.sem["VV"], st[("slab", qt)])
                    for j2 in range(2):
                        h = chh * 2 + j2
                        for j, kt in enumerate(kts):
                            ins = t.matmul(PS[5 + j2][0:64, 0:128],
                                           Vtm[:, kt, h * 64:(h + 1) * 64],
                                           H["slab2"][:, (j2 * 3 + j) * 128:(j2 * 3 + j + 1) * 128],
                                           start=(j == 0), stop=(j == len(kts) - 1))
                    C.inc(ins, "MM")
                st[("yt", qt)] = C.cnt["MM"]
                @b.vector
                def _(v):
                    v.wait_ge(C.sem["MM"], st[("yt", qt)])
                    for j2 in range(2):
                        pо = j2 * 64
                        ins = v.tensor_copy(QTf[pо:pо + 64, chh, qt * 128:(qt + 1) * 128],
                                            PS[5 + j2][0:64, 0:128])
                    C.inc(ins, "VV")
                st[("yev", qt)] = C.cnt["VV"]

            for qt in range(17):
                if qt < 16:
                    stage_scores(qt)
                if qt >= 1:
                    stage_rest(qt - 1)
        for oc in range(NCH):
            @b.gpsimd
            def _(g):
                g.wait_ge(C.sem["LDG"], C.cnt["LDG"])
                C.w(g, "MM", "EV", "VV")
                ins = g.dma_start(scr[:, 4096:5120], C.dram["l2f_wo"][il, oc, :, :])
                C.inc(ins, "LDG", 16)
            @b.sync
            def _(s):
                s.wait_ge(C.sem["LDS"], C.cnt["LDS"])
                C.w(s, "VV", "EV")
                ins = s.dma_start(t_s[:, 0:2048], xsp2[oc * 128:(oc + 1) * 128, :])
                C.inc(ins, "LDS", 16)
            scw = scr[:, 4096:5120].rearrange("p (c j) -> p c j", c=NCH)
            for tc in range(NTC):
                sl = slice(tc * 512, (tc + 1) * 512)
                @b.tensor
                def _(t):
                    C.w(t, "LDG", "VV", "EV")
                    for ccx in range(NCH):
                        ins = t.matmul(PS[0][:, 0:512], scw[:, ccx, :], QTf[:, ccx, sl],
                                       start=(ccx == 0), stop=(ccx == NCH - 1))
                    C.inc(ins, "MM")
                @b.vector
                def _(v):
                    C.w(v, "MM", "LDS")
                    ins = v.scalar_tensor_tensor(ix[:, oc, sl], PS[0][:, 0:512], 1.0,
                                                 t_s[:, sl], op0=ALU.mult, op1=ALU.add)
                    C.inc(ins, "VV")
        C.dbg("ycat", H["AR2"][:, 0:16384])
        C.dbg("ixa", H["xa_t"][:])
        if "ffn" not in L2F_PARTS:
            continue
        @b.sync
        def _(s):
            s.wait_ge(C.sem["LDS"], C.cnt["LDS"])
            C.w(s, "VV", "EV", "MM")
            ins = s.dma_start(H["b8"][:, 32:40], C.dram["l2f_lng"][il, :, :])
            C.inc(ins, "LDS", 16)
            ins = s.dma_start(H["b8"][:, 40:48], C.dram["l2f_lnb"][il, :, :])
            C.inc(ins, "LDS", 16)
            ins = s.dma_start(H["rr"][:, 0:44], C.dram["l2f_b1"][il, :, :])
            C.inc(ins, "LDS", 16)
            ins = s.dma_start(H["b8"][:, 0:8], C.dram["l2f_b2"][il, :, :])
            C.inc(ins, "LDS", 16)
        @b.vector
        def _(v):
            C.w(v, "MM", "EV", "LDS")
            for d_ in range(NCH):
                if d_ < 4:
                    ins = v.tensor_copy(xsb[:, d_, :], ix[:, d_, :])
                else:
                    v.memset(xsb[:, d_, 0:1], 0.0)
                    ins = v.tensor_copy(xsb[:, d_, 1:2048], ix[:, d_, 0:2047])
            C.inc(ins, "VV")
        emit_ln(C, H, xsb, xsb, src_f32=False,
                gamma=H["b8"][:, 32:40], beta=H["b8"][:, 40:48])
        for half in range(2):
            tok0 = half * 1024
            for fc in range(22):
                slot = 4096 + (fc % 2) * 2048
                lsem = "LDG" if fc % 2 == 0 else "LDH"
                @b.gpsimd
                def _(g):
                    need = C.cnt["MM"] - 2 if fc >= 1 else C.cnt["MM"]
                    g.wait_ge(C.sem["MM"], max(0, need))
                    if fc == 0:
                        C.w(g, "EV", "VV")
                    g.wait_ge(C.sem[lsem], C.cnt[lsem])
                    ins = g.dma_start(scr[:, slot:slot + 1024], C.dram["l2f_w1"][il, fc, :, :])
                    C.inc(ins, lsem, 16)
                    ins = g.dma_start(scr[:, slot + 1024:slot + 2048],
                                      C.dram["l2f_w1"][il, 22 + fc, :, :])
                    C.inc(ins, lsem, 16)
                ldg_t = C.cnt[lsem]
                mm0, ev0, vv0 = C.cnt["MM"], C.cnt["EV"], C.cnt["VV"]
                scw1 = scr[:, slot:slot + 1024].rearrange("p (c j) -> p c j", c=NCH)
                scw2 = scr[:, slot + 1024:slot + 2048].rearrange("p (c j) -> p c j", c=NCH)
                @b.tensor
                def _(t):
                    t.wait_ge(C.sem[lsem], ldg_t)
                    for tc2 in range(2):
                        gi = fc * 2 + tc2
                        p = gi % 2
                        sl = slice(tok0 + tc2 * 512, tok0 + (tc2 + 1) * 512)
                        t.wait_ge(C.sem["VV"], max(vv0 - 1, vv0 + tc2 - 1))
                        for ccx in range(NCH):
                            t.matmul(PS[p][:, 0:512], scw1[:, ccx, :], xsb[:, ccx, sl],
                                     start=(ccx == 0), stop=(ccx == NCH - 1))
                        for ccx in range(NCH):
                            ins = t.matmul(PS[2 + p][:, 0:512], scw2[:, ccx, :],
                                           xsb[:, ccx, sl], start=(ccx == 0), stop=(ccx == NCH - 1))
                        C.inc(ins, "MM")
                @b.scalar
                def _(s):
                    for tc2 in range(2):
                        gi = fc * 2 + tc2
                        p = gi % 2
                        s.wait_ge(C.sem["MM"], mm0 + tc2 + 1)
                        s.wait_ge(C.sem["VV"], max(vv0 - 1, vv0 + tc2 - 1))
                        ins = s.activation(fa[:, 0:512] if p == 0 else fb[:, 0:512],
                                           PS[2 + p][:, 0:512], AF.Gelu,
                                           bias=H["rr"][:, 22 + fc:23 + fc])
                        C.inc(ins, "EV")
                @b.vector
                def _(v):
                    for tc2 in range(2):
                        gi = fc * 2 + tc2
                        p = gi % 2
                        hsl = slice(tc2 * 512, (tc2 + 1) * 512)
                        v.wait_ge(C.sem["EV"], ev0 + tc2 + 1)
                        ins = v.scalar_tensor_tensor(hAR[:, fc, hsl], PS[p][:, 0:512],
                                                     H["rr"][:, fc:fc + 1],
                                                     fa[:, 0:512] if p == 0 else fb[:, 0:512],
                                                     op0=ALU.add, op1=ALU.mult)
                        C.inc(ins, "VV")
            for dc in range(NCH):
                slot = (dc % 2) * 4096
                lsem = "LDG" if dc % 2 == 0 else "LDH"
                @b.gpsimd
                def _(g):
                    need = C.cnt["MM"] - 2 if dc >= 1 else C.cnt["MM"]
                    g.wait_ge(C.sem["MM"], max(0, need))
                    if dc == 0:
                        C.w(g, "EV", "VV")
                    g.wait_ge(C.sem[lsem], C.cnt[lsem])
                    ins = g.dma_start(scr[:, slot:slot + 2816], C.dram["l2f_w2"][il, dc, :, :])
                    C.inc(ins, lsem, 16)
                ldg_t = C.cnt[lsem]
                mm0, vv0 = C.cnt["MM"], C.cnt["VV"]
                @b.tensor
                def _(t):
                    t.wait_ge(C.sem[lsem], ldg_t)
                    for tc2 in range(2):
                        t.wait_ge(C.sem["VV"], max(vv0, vv0 + tc2 - 1))
                        for fcc in range(22):
                            ins = t.matmul(PS[4 + tc2 % 2][:, 0:512],
                                           scr[:, slot + fcc * 128:slot + (fcc + 1) * 128],
                                           hAR[:, fcc, slice(tc2 * 512, (tc2 + 1) * 512)],
                                           start=(fcc == 0), stop=(fcc == 21))
                        C.inc(ins, "MM")
                @b.vector
                def _(v):
                    for tc2 in range(2):
                        sl = slice(tok0 + tc2 * 512, tok0 + (tc2 + 1) * 512)
                        v.wait_ge(C.sem["MM"], mm0 + tc2 + 1)
                        ins = v.scalar_tensor_tensor(ix[:, dc, sl], PS[4 + tc2 % 2][:, 0:512],
                                                     H["b8"][:, dc:dc + 1], ix[:, dc, sl],
                                                     op0=ALU.add, op1=ALU.add)
                        C.inc(ins, "VV")

    C.dbg("ixf", H["xa_t"][:])
    for d_ in range(NCH):
        @b.sync
        def _(s):
            s.wait_ge(C.sem["LDS"], C.cnt["LDS"])
            C.w(s, "VV", "EV", "ST")
            ins = s.dma_start(t_s[:, 0:2048], xsp[d_ * 128:(d_ + 1) * 128, :])
            C.inc(ins, "LDS", 16)
        for tc in range(NTC):
            sl = slice(tc * 512, (tc + 1) * 512)
            @b.vector
            def _(v):
                C.w(v, "LDS", "ST")
                ins = v.scalar_tensor_tensor(fb[:, 0:512], ix[:, d_, sl], 0.5,
                                             t_s[:, sl], op0=ALU.mult, op1=ALU.add)
                C.inc(ins, "VV")
            @b.sync
            def _(s):
                C.w(s, "VV")
                ins = s.dma_start(xsp[d_ * 128:(d_ + 1) * 128, sl], fb[:, 0:512])
                C.inc(ins, "ST", 16)
    C.x_in_hbm = True



def emit_l3a(C, H, li):
    b = C.block
    xa, xn = H["xa"], H["xn"]
    PS = H["PS"]
    scr = H["scr16"]
    scrv = scr.rearrange("p (c k) -> p c k", c=NCH)
    xsp = C.dram["xspill"]
    t_s = H["t_s"]
    fa, fb = H["f512_a"], H["f512_b"]
    xa_bf = H["xa_t"][:].bitcast(BF16)
    QT = H["AR2"][:, 0:16384].rearrange("p (c t) -> p c t", c=NCH)
    KT = H["AR2"][:, 16384:32768].rearrange("p (c t) -> p c t", c=NCH)
    V = xa_bf[:, 0:16384].rearrange("p (tt c) -> p tt c", tt=16)
    KRT = xa_bf[:, 16384:32768].rearrange("p (c t) -> p c t", c=NCH)
    QRT = H["xn_t"][:].rearrange("p (c t) -> p c t", c=NCH)
    ycat = QRT

    @b.gpsimd
    def _(g):
        g.wait_ge(C.sem["LDG"], C.cnt["LDG"])
        C.w(g, "MM", "EV", "VV")
        for (dst, nm) in ((H["l3cos"][:], "c_l3cos"), (H["l3sin"][:], "c_l3sin"),
                          (H["l3perm"][:], "c_l3perm"), (H["l3mask"][:], "c_l3mask")):
            ins = g.dma_start(dst, C.dram[nm][:, :])
            C.inc(ins, "LDG", 16)
    @b.sync
    def _(s):
        s.wait_ge(C.sem["LDS"], C.cnt["LDS"])
        C.w(s, "VV", "EV")
        ins = s.dma_start(H["b8"][:, 16:24], C.dram["l3a_bo"][:, :])
        C.inc(ins, "LDS", 16)

    @b.sync
    def _(s):
        C.wall(s)
        for d_ in range(NCH):
            ins = s.dma_start(xsp[d_ * 128:(d_ + 1) * 128, :], xa[:, d_, :])
            C.inc(ins, "ST", 16)

    for sec in range(2):
        @b.gpsimd
        def _(g):
            g.wait_ge(C.sem["LDG"], C.cnt["LDG"])
            C.w(g, "MM", "EV", "VV")
            for oc in range(NCH):
                ins = g.dma_start(scrv[:, oc, :], C.dram["l3a_wqk"][sec, oc, :, :])
                C.inc(ins, "LDG", 16)
        dst = QT if sec == 0 else KT
        for oc in range(NCH):
            for tc in range(NTC):
                sl = slice(tc * 512, (tc + 1) * 512)
                @b.tensor
                def _(t):
                    C.w(t, "LDG", "EV", "VV")
                    for c_ in range(NCH):
                        ins = t.matmul(PS[tc % 2][:, 0:512],
                                       scr[:, c_ * 1024 + oc * 128:c_ * 1024 + oc * 128 + 128]
                                       if False else scrv[:, oc, c_ * 128:(c_ + 1) * 128],
                                       xn[:, c_, sl], start=(c_ == 0), stop=(c_ == NCH - 1))
                    C.inc(ins, "MM")
                @b.scalar
                def _(s):
                    C.w(s, "MM")
                    ins = s.activation(dst[:, oc, sl], PS[tc % 2][:, 0:512],
                                       AF.Identity, scale=0.125)
                    C.inc(ins, "EV")
    @b.gpsimd
    def _(g):
        g.wait_ge(C.sem["LDG"], C.cnt["LDG"])
        C.w(g, "MM", "EV", "VV")
        ins = g.dma_start(scr[:, 0:8192], C.dram["l3a_wv"][:, :])
        C.inc(ins, "LDG", 16)
    for tt in range(16):
        for cb in range(2):
            @b.tensor
            def _(t):
                C.w(t, "LDG", "VV", "EV", "ST")
                for c_ in range(NCH):
                    ins = t.matmul(PS[cb % 2][:, 0:512],
                                   xn[:, c_, tt * 128:(tt + 1) * 128],
                                   scrv[:, c_, cb * 512:(cb + 1) * 512],
                                   start=(c_ == 0), stop=(c_ == NCH - 1))
                C.inc(ins, "MM")
            @b.scalar
            def _(s):
                C.w(s, "MM", "ST")
                ins = s.copy(V[:, tt, cb * 512:(cb + 1) * 512], PS[cb % 2][:, 0:512])
                C.inc(ins, "EV")
    for (srcT, dstT) in ((QT, QRT), (KT, KRT)):
        for d_ in range(NCH):
            for tc in range(NTC):
                sl = slice(tc * 512, (tc + 1) * 512)
                @b.tensor
                def _(t):
                    C.w(t, "EV", "VV")
                    ins = t.matmul(PS[2][:, 0:512], H["l3perm"][:], srcT[:, d_, sl],
                                   start=True, stop=True)
                    C.inc(ins, "MM")
                @b.vector
                def _(v):
                    C.w(v, "MM")
                    v.tensor_tensor(fa[:, 0:512], srcT[:, d_, sl],
                                    H["l3cos"][:, sl], op=ALU.mult)
                    ins = v.tensor_tensor(fb[:, 0:512], PS[2][:, 0:512],
                                          H["l3sin"][:, sl], op=ALU.mult)
                    C.fence(v, ins, "VV")
                    ins = v.tensor_tensor(dstT[:, d_, sl], fa[:, 0:512], fb[:, 0:512],
                                          op=ALU.add)
                    C.inc(ins, "VV")
    for h in range(16):
        po, chh = (h % 2) * 64, h // 2
        for qb in range(NTC):
            qsl = slice(qb * 512, (qb + 1) * 512)
            last_kt = min(15, (qb * 512 + 511) // 128)
            st = {}
            ev_tail = C.cnt["EV"]
            vv_tail = C.cnt["VV"]

            def sc_stage(kt):
                p = kt % 2
                @b.tensor
                def _(t):
                    if kt >= 2:
                        t.wait_ge(C.sem["EV"], st[("exp", kt - 2)])
                    else:
                        t.wait_ge(C.sem["EV"], ev_tail)
                        t.wait_ge(C.sem["VV"], vv_tail)
                    ins = t.matmul(PS[2 * p][:, 0:512],
                                   KRT[po:po + 64, chh, kt * 128:(kt + 1) * 128],
                                   QRT[po:po + 64, chh, qsl], start=True, stop=True)
                    ins = t.matmul(PS[2 * p + 1][:, 0:512],
                                   KT[po:po + 64, chh, kt * 128:(kt + 1) * 128],
                                   QT[po:po + 64, chh, qsl], start=True, stop=True)
                    C.inc(ins, "MM")
                st[("sc", kt)] = C.cnt["MM"]

            def exp_stage(kt):
                p = kt % 2
                @b.scalar
                def _(s):
                    s.wait_ge(C.sem["MM"], st[("sc", kt)])
                    if kt >= 2:
                        s.wait_ge(C.sem["MM"], st[("acc", kt - 2)])
                    s.activation(scr[:, p * 512:(p + 1) * 512], PS[2 * p][:, 0:512], AF.Exp)
                    ins = s.activation(scr[:, 1024 + p * 512:1024 + (p + 1) * 512],
                                       PS[2 * p + 1][:, 0:512], AF.Exp)
                    C.inc(ins, "EV")
                st[("exp", kt)] = C.cnt["EV"]
                dv = kt * 128 - qb * 512
                @b.vector
                def _(v):
                    v.wait_ge(C.sem["EV"], st[("exp", kt)])
                    if dv >= 0:
                        ins = v.tensor_tensor(scr[:, p * 512:(p + 1) * 512],
                                              scr[:, p * 512:(p + 1) * 512],
                                              H["l3mask"][:, (dv // 128) * 512:(dv // 128 + 1) * 512],
                                              op=ALU.mult)
                    else:
                        ins = v.tensor_copy(fa[:, 8:9], fa[:, 8:9])
                    C.inc(ins, "VV")
                st[("mk", kt)] = C.cnt["VV"]

            def acc_stage(kt):
                p = kt % 2
                @b.tensor
                def _(t):
                    t.wait_ge(C.sem["VV"], st[("mk", kt)])
                    first = (kt == 0)
                    t.matmul(PS[5][0:64, 0:512], V[:, kt, h * 64:(h + 1) * 64],
                             scr[:, p * 512:(p + 1) * 512],
                             start=first, stop=(kt == last_kt))
                    ins = t.matmul(PS[6][0:1, 0:512], H["ones_bf"][:, 0:1],
                                   scr[:, 1024 + p * 512:1024 + (p + 1) * 512],
                                   start=first, stop=(kt == last_kt))
                    C.inc(ins, "MM")
                st[("acc", kt)] = C.cnt["MM"]

            for kt in range(last_kt + 2):
                if kt <= last_kt:
                    sc_stage(kt)
                if kt >= 1:
                    exp_stage(kt - 1)
                    acc_stage(kt - 1)
            ysb = scr[po:po + 64, 512:1024] if False else scr[po:po + 64, 2048:2560]
            @b.scalar
            def _(s):
                s.wait_ge(C.sem["MM"], st[("acc", last_kt)])
                s.copy(ysb, PS[5][0:64, 0:512])
                ins = s.activation(fa[0:1, 0:512], PS[6][0:1, 0:512], AF.Identity)
                C.inc(ins, "EV")
            @b.vector
            def _(v):
                C.w(v, "EV")
                ins = v.reciprocal(fb[0:1, 0:512], fa[0:1, 0:512])
                C.inc(ins, "VV")
            @b.tensor
            def _(t):
                C.w(t, "VV")
                ins = t.matmul(PS[7][:, 0:512], H["ones_f"][0:1, :], fb[0:1, 0:512],
                               start=True, stop=True)
                C.inc(ins, "MM")
            @b.vector
            def _(v):
                C.w(v, "MM")
                ins = v.tensor_tensor(ycat[po:po + 64, chh, qsl], ysb,
                                      PS[7][po:po + 64, 0:512], op=ALU.mult)
                C.inc(ins, "VV")
    for oc in range(NCH):
        @b.gpsimd
        def _(g):
            g.wait_ge(C.sem["LDG"], C.cnt["LDG"])
            C.w(g, "MM", "EV", "VV")
            ins = g.dma_start(scr[:, 4096:5120], C.dram["l3a_wo"][oc, :, :])
            C.inc(ins, "LDG", 16)
        @b.sync
        def _(s):
            s.wait_ge(C.sem["LDS"], C.cnt["LDS"])
            C.w(s, "VV", "ST")
            ins = s.dma_start(t_s[:, 0:2048], xsp[oc * 128:(oc + 1) * 128, :])
            C.inc(ins, "LDS", 16)
        scw = scr[:, 4096:5120].rearrange("p (c j) -> p c j", c=NCH)
        for tc in range(NTC):
            sl = slice(tc * 512, (tc + 1) * 512)
            @b.tensor
            def _(t):
                C.w(t, "LDG", "VV", "EV")
                for c_ in range(NCH):
                    ins = t.matmul(PS[0][:, 0:512], scw[:, c_, :], ycat[:, c_, sl],
                                   start=(c_ == 0), stop=(c_ == NCH - 1))
                C.inc(ins, "MM")
            @b.vector
            def _(v):
                C.w(v, "MM", "LDS", "ST")
                ins = v.scalar_tensor_tensor(fb[:, 0:512], PS[0][:, 0:512],
                                             H["b8"][:, 16 + oc:17 + oc],
                                             t_s[:, sl], op0=ALU.add, op1=ALU.add)
                C.inc(ins, "VV")
            @b.sync
            def _(s):
                C.w(s, "VV")
                ins = s.dma_start(xsp[oc * 128:(oc + 1) * 128, sl], fb[:, 0:512])
                C.inc(ins, "ST", 16)
    C.x_in_hbm = True

_EMITTERS = {"l1f": emit_l1f, "l0a": emit_l0a, "l2a": emit_l2a,
             "l0f": emit_l0f, "l3f": emit_l3f, "l1a": emit_l1a, "l2f": emit_l2f,
             "l3a": emit_l3a}



def build_program(sub_idxs):
    nc = bass.Bass("TRN2", target_bir_lowering=False)
    es = ExitStack()
    x_in = nc.dram_tensor("x", [D, T], F32, kind="ExternalInput").ap()
    y_out = nc.dram_tensor("y", [D, T], F32, kind="ExternalOutput").ap()
    xspill = nc.dram_tensor("xspill", [D, T], F32, kind="Internal").ap()
    xspill2 = nc.dram_tensor("xspill2", [D, T], F32, kind="Internal").ap()

    block = es.enter_context(nc.Block())
    C = Ctx(nc, es, block)

    need = []
    for i in sub_idxs:
        for wname in _SUB_WEIGHTS.get(SUBS[i], []):
            if wname not in need:
                need.append(wname)
    for wname in need:
        shape = list(_WEIGHT_SPECS[wname])
        dt = BF16 if wname in ("c_cs8fm", "c_ss8fm", "c_cifm", "c_sifm", "c_l2perm", "c_l3cos",
                               "c_l3sin", "c_l3perm", "c_l3mask") else F32
        C.dram[wname] = nc.dram_tensor(wname, shape, dt, kind="ExternalInput").ap()


    H = {}
    kinds = {SUBS[i] for i in sub_idxs}
    has_l1a = "l1a" in kinds
    has_moe = bool(kinds & {"l0f", "l3f"})
    sb = lambda name, shape, dt: es.enter_context(nc.sbuf_tensor(name, shape, dt))
    H["xa_t"] = sb("xa", [128, NCH * T], F32)
    H["xa"] = H["xa_t"][:].rearrange("p (d t) -> p d t", d=NCH)
    H["xn_t"] = sb("xn", [128, NCH * T], BF16)
    H["xn"] = H["xn_t"][:].rearrange("p (d t) -> p d t", d=NCH)
    H["AR2"] = sb("AR2", [128, 32768], BF16)[:]
    H["t_s"] = H["AR2"][:, 16384:24576].bitcast(F32)
    H["tsq_s"] = H["AR2"][:, 24576:28672]
    H["f512_a"] = sb("f512_a", [128, 512], F32)[:]
    H["f512_b"] = sb("f512_b", [128, 512], F32)[:]
    H["row_m"] = H["f512_a"][0:1, :]
    H["row_r"] = H["f512_b"][0:1, :]
    H["scr16"] = sb("scr16", [128, 8192], BF16)[:]
    H["wstage_f"] = H["scr16"][:, 0:4096].bitcast(F32)
    H["ident_bf"] = sb("ident_bf", [128, 128], BF16)[:]
    H["mask_f"] = sb("mask_f", [128, 128], F32)[:]
    H["m1_s"] = sb("m1_s", [128, 8], F32)[:]
    H["b8"] = sb("b8", [128, 48], F32)[:]
    H["ones_bf"] = sb("ones_bf", [128, 1], BF16)[:]
    H["cst"] = sb("cst", [128, 8], F32)[:]
    H["ones_f"] = sb("ones_f", [1, 128], F32)[:]
    H["ones_cf"] = sb("ones_cf", [128, 1], F32)[:]
    H["SM"] = sb("SM", [128, 12288], BF16)[:]

    def carve(off_b, part, n, dt):
        e0 = off_b // 2
        if dt == BF16:
            return H["SM"][0:part, e0:e0 + n]
        return H["SM"][0:part, e0:e0 + 2 * n].bitcast(F32)

    if has_moe:
        H["wb"] = carve(0, 128, 2048, BF16)
        H["w8b"] = carve(4096, 128, 8, BF16)
        H["w8T"] = carve(4160, 8, 2048, BF16)
        H["wrtf"] = carve(8256, 128, 64, F32)
        H["ltm"] = carve(8512, 128, 128, F32)
        H["xn32"] = carve(9024, 128, 512, F32)
        H["esel"] = carve(11072, 8, 1024, BF16)
        H["b2sb"] = carve(13120, 8, 1024, BF16)
        H["b256"] = carve(15168, 128, 256, F32)
    if has_l1a:
        H["loraA"] = carve(0, 64, 2048, BF16)
        H["loraB"] = carve(4096, 64, 2048, BF16)
        H["loraC"] = carve(8192, 32, 2048, BF16)
        H["twT"] = carve(12288, 64, 2048, BF16)
        H["tdb"] = carve(16384, 64, 1024, BF16)
        H["blf"] = carve(18432, 64, 1024, BF16)
        H["Ssb"] = carve(20480, 128, 512, F32)
        H["Sbf"] = carve(22528, 128, 64, BF16)
        H["pp"] = carve(22656, 128, 64, F32)
        H["u2"] = carve(22912, 128, 8, F32)
        H["tdmiu2"] = carve(22944, 128, 8, F32)
        H["msk2"] = carve(22976, 128, 256, F32)
    if "l3a" in kinds:
        H["l3cos"] = carve(0, 128, 2048, BF16)
        H["l3sin"] = carve(4096, 128, 2048, BF16)
        H["l3perm"] = carve(8192, 128, 128, BF16)
        H["l3mask"] = carve(8448, 128, 2048, BF16)
    if "l2f" in kinds:
        H["qt_s"] = carve(0, 128, 1024, BF16)
        H["qsq_s"] = carve(2048, 128, 1024, BF16)
        H["rr"] = carve(4096, 128, 48, F32)
        H["tabF"] = carve(4288, 128, 8192, BF16)
        H["mlow"] = carve(20672, 128, 128, F32)
        H["rqall"] = carve(21184, 128, 256, F32)
        H["slab2"] = carve(22208, 128, 768, BF16)
        H["l2perm_sb"] = carve(23744, 128, 128, BF16)
    H["PS"] = [es.enter_context(nc.psum_tensor(f"ps{i}", [128, 512], F32))[:]
               for i in range(8)]

    with es:
        @block.gpsimd
        def _(g):
            g.memset(H["ones_bf"], 1.0)
            g.memset(H["ones_f"], 1.0)
            g.memset(H["ones_cf"], 1.0)
            g.memset(H["cst"][:, 0:1], 1e-5)
            g.memset(H["cst"][:, 1:2], float(np.pi / 2))
            ins = g.memset(H["cst"][:, 2:3], 1.6e-4)
            C.inc(ins, "GG")

        @block.sync
        def _(s):
            s.wait_ge(C.sem["LDS"], C.cnt["LDS"])
            for d_ in range(NCH):
                ins = s.dma_start(H["xa"][:, d_, :], x_in[d_ * 128:(d_ + 1) * 128, :])
                C.inc(ins, "LDS", 16)

        C.dram["xspill"] = xspill
        C.dram["xspill2"] = xspill2
        for i in sub_idxs:
            kind = SUBS[i]
            ismoe = kind in ("l0f", "l3f")
            if ismoe:
                wrt_nm = "l0f_wrt" if kind == "l0f" else "l3f_wrt"
                @block.sync
                def _(s):
                    s.wait_ge(C.sem["LDS"], C.cnt["LDS"])
                    C.w(s, "MM", "EV", "VV")
                    ins = s.dma_start(H["wrtf"][:], C.dram[wrt_nm][:, :])
                    C.inc(ins, "LDS", 16)
            emit_ln(C, H, H["xa"], H["xn"], router=ismoe)
            _EMITTERS[kind](C, H, i)
            if C.x_in_hbm:
                @block.sync
                def _(s):
                    s.wait_ge(C.sem["LDS"], C.cnt["LDS"])
                    C.wall(s)
                    for d_ in range(NCH):
                        ins = s.dma_start(H["xa"][:, d_, :],
                                          xspill[d_ * 128:(d_ + 1) * 128, :])
                        C.inc(ins, "LDS", 16)
                C.x_in_hbm = False

        @block.sync
        def _(s):
            C.w(s, "VV", "EV", "MM", "GG")
            for d_ in range(NCH):
                ins = s.dma_start(y_out[d_ * 128:(d_ + 1) * 128, :], H["xa"][:, d_, :])
                C.inc(ins, "ST", 16)
            s.wait_ge(C.sem["ST"], C.cnt["ST"])
    return nc



_PROG_CACHE = {}


def _get_program(key):
    if key not in _PROG_CACHE:
        _PROG_CACHE[key] = build_program(list(key))
    return _PROG_CACHE[key]


def run_groups(consts, xs, groups):
    for grp in groups:
        need = []
        for i in grp:
            for wname in _SUB_WEIGHTS.get(SUBS[i], []):
                if wname not in need:
                    need.append(wname)
        nc = _get_program(tuple(grp))
        in_maps = []
        for c in range(8):
            m = {"x": xs[c]}
            for wname in need:
                m[wname] = consts[wname]
            in_maps.append(m)
        res = run_bass_kernel_spmd(nc, in_maps, core_ids=list(range(8)))
        xs = [np.asarray(res.results[c]["y"]) for c in range(8)]
    return xs


def kernel(**inputs):
    consts = _prep(inputs)
    for k, v in consts.items():
        _WEIGHT_SPECS[k] = v.shape
    x = inputs["L0_attn_embeddings"]
    xs = [np.ascontiguousarray(np.asarray(x[c], np.float32).T) for c in range(8)]
    xs = run_groups(consts, xs, LAUNCH_GROUPS)
    out = np.stack([xs[c].T for c in range(8)], axis=0)
    return np.ascontiguousarray(out, dtype=np.float32)
